# version 03
# baseline (speedup 1.0000x reference)
import contextlib
import numpy as np
import ml_dtypes
import concourse.bass as bass
import concourse.mybir as mybir
from concourse.bass_utils import run_bass_kernel_spmd

F32 = mybir.dt.float32; BF16 = mybir.dt.bfloat16
AF = mybir.ActivationFunctionType; ALU = mybir.AluOpType; AX = mybir.AxisListType
NB, NPOS, D = 32, 256, 1024
NTOK = NB * NPOS
GB = 4
NG = NB // GB
BSH = 20.0
LN_EPS = 1e-5

class Chain:
    def __init__(self, nc, name, stack):
        self.sem = stack.enter_context(nc.semaphore(name)); self.n = 0
    def bump(self, inst, k=1):
        inst.then_inc(self.sem, k); self.n += k; return inst
    def dma(self, inst):
        inst.then_inc(self.sem, 16); self.n += 16; return inst
    def wait(self, eng):
        eng.wait_ge(self.sem, self.n)


def build_l0attn(nc, stack):
    x = nc.dram_tensor("x", [NB, NPOS, D], F32, kind="ExternalInput").ap()
    wqkv = nc.dram_tensor("wqkv", [D, 3*D], F32, kind="ExternalInput").ap()
    bqkv = nc.dram_tensor("bqkv", [3*D], F32, kind="ExternalInput").ap()
    idn_in = nc.dram_tensor("idn", [128, 128], F32, kind="ExternalInput").ap()
    mask_in = nc.dram_tensor("mask", [4, 128, 512], F32, kind="ExternalInput").ap()
    y_out = nc.dram_tensor("y", [2, NB, 64, 2048], BF16, kind="ExternalOutput").ap()
    d_out = nc.dram_tensor("den", [2, NB, 2048], F32, kind="ExternalOutput").ap()
    E = stack.enter_context
    wq   = E(nc.sbuf_tensor("wq", [128, 8, 3072], BF16))
    bq   = E(nc.sbuf_tensor("bq", [1, 3072], BF16))
    idn  = E(nc.sbuf_tensor("idn_s", [128, 128], BF16))
    mask = E(nc.sbuf_tensor("mask_s", [128, 4, 512], BF16))
    onesr= E(nc.sbuf_tensor("onesr", [1, 512], BF16))
    xt   = E(nc.sbuf_tensor("xt", [128, GB, 1024], F32))
    stat = E(nc.sbuf_tensor("stat", [128, 20], F32))
    xbf  = E(nc.sbuf_tensor("xbf", [128, GB, 1024], BF16))
    xfm  = E(nc.sbuf_tensor("xfm", [128, 8, GB*128], BF16))
    qs_t = E(nc.sbuf_tensor("qs", [64, GB, 2048], BF16))
    ks_t = E(nc.sbuf_tensor("ks", [64, GB, 2048], BF16))
    vtm  = E(nc.sbuf_tensor("vtm", [128, GB, 1024], BF16))
    vsp  = E(nc.sbuf_tensor("vsp", [128, GB, 16, 65], BF16))
    atile= E(nc.sbuf_tensor("atile", [128, 16, 512], BF16))
    ybuf = E(nc.sbuf_tensor("ybuf", [64, 2048], BF16))
    dbuf = E(nc.sbuf_tensor("dbuf", [1, 2048], F32))
    cst  = E(nc.sbuf_tensor("cst", [128, 2], F32))
    ps_tp0 = E(nc.psum_tensor("ps_tp0", [128, 128], BF16))
    ps_tp1 = E(nc.psum_tensor("ps_tp1", [128, 128], BF16))
    ps_tp = [ps_tp0, ps_tp1]
    ps_mm = E(nc.psum_tensor("ps_mm", [128, 2, 512], F32))
    ps_s  = E(nc.psum_tensor("ps_s", [128, 2, 512], F32))
    ps_av = E(nc.psum_tensor("ps_av", [65, 512], F32))

    ch_w   = Chain(nc, "ch_w", stack)
    ch_x   = Chain(nc, "ch_x", stack)
    ch_v1  = Chain(nc, "ch_v1", stack)
    ch_s1  = Chain(nc, "ch_s1", stack)
    ch_v2  = Chain(nc, "ch_v2", stack)
    ch_s2  = Chain(nc, "ch_s2", stack)
    ch_v3  = Chain(nc, "ch_v3", stack)
    tp_p = [Chain(nc, f"tp_p{i}", stack) for i in range(2)]
    tp_c = [Chain(nc, f"tp_c{i}", stack) for i in range(2)]
    mm_p = [Chain(nc, f"mm_p{i}", stack) for i in range(2)]
    mm_c = [Chain(nc, f"mm_c{i}", stack) for i in range(2)]
    s_p  = [Chain(nc, f"s_p{i}", stack) for i in range(2)]
    s_c  = [Chain(nc, f"s_c{i}", stack) for i in range(2)]
    ch_fold = Chain(nc, "ch_fold", stack)
    ch_ae  = Chain(nc, "ch_ae", stack)
    ch_am  = Chain(nc, "ch_am", stack)
    ch_avd = Chain(nc, "ch_avd", stack)
    ch_ye  = Chain(nc, "ch_ye", stack)
    ch_yd  = Chain(nc, "ch_yd", stack)
    slot_need = [0, 0]

    for kc in range(8):
        ch_w.dma(nc.gpsimd.dma_start(out=wq[:, kc, :], in_=wqkv[kc*128:(kc+1)*128, :]))
    ch_w.dma(nc.gpsimd.dma_start(out=bq[:], in_=bqkv[None, :]))
    ch_w.dma(nc.gpsimd.dma_start(out=idn[:], in_=idn_in[:]))
    ch_w.dma(nc.gpsimd.dma_start(out=mask[:], in_=mask_in.transpose([1, 0, 2])))
    nc.vector.memset(onesr[:], 1.0)
    nc.vector.memset(cst[:, 0:1], 1e-5)
    nc.vector.memset(cst[:, 1:2], -20.0)
    ch_w.bump(nc.vector.memset(vsp[:, :, :, 64:65], 1.0))
    nc.all_engine_barrier()

    for hloc in range(2):
      for g in range(NG):
        for b in range(GB):
            ch_x.dma(nc.sync.dma_start(out=xt[:, b, :], in_=x[g*GB+b, hloc*128:(hloc+1)*128, :]))
        ch_x.wait(nc.vector)
        for b in range(GB):
            inst = nc.vector.reduce_sum(stat[:, 0+b:1+b], xt[:, b, :], axis=AX.X)
        ch_v1.bump(inst)
        ch_v1.wait(nc.vector)
        ch_v1.bump(nc.vector.tensor_scalar_mul(stat[:, 8:12], stat[:, 0:4], 1.0/1024.0))
        ch_x.wait(nc.scalar)
        for b in range(GB):
            inst = nc.scalar.activation(xbf[:, b, :], xt[:, b, :], AF.Square, accum_out=stat[:, 4+b:5+b])
        ch_s1.bump(inst)
        ch_v1.wait(nc.vector); ch_s1.wait(nc.vector)
        nc.vector.tensor_tensor(stat[:, 16:20], stat[:, 8:12], stat[:, 8:12], op=ALU.mult)
        ch_v2.bump(nc.vector.engine_nop())
        ch_v2.wait(nc.vector)
        ch_v2.bump(nc.vector.tensor_scalar(stat[:, 12:16], stat[:, 4:8], 1.0/1024.0, None, op0=ALU.mult))
        ch_v2.wait(nc.vector)
        ch_v2.bump(nc.vector.tensor_tensor(stat[:, 12:16], stat[:, 12:16], stat[:, 16:20], op=ALU.subtract))
        ch_v2.wait(nc.scalar)
        ch_s2.bump(nc.scalar.activation(stat[:, 16:20], stat[:, 12:16], AF.Ln, bias=cst[:, 0:1]))
        ch_s2.wait(nc.scalar)
        ch_s2.bump(nc.scalar.activation(stat[:, 16:20], stat[:, 16:20], AF.Exp, scale=-0.5))
        ch_s2.wait(nc.vector)
        for b in range(GB):
            inst = nc.vector.tensor_scalar(xbf[:, b, :], xt[:, b, :], stat[:, 8+b:9+b], stat[:, 16+b:17+b],
                                           op0=ALU.subtract, op1=ALU.mult)
        ch_v3.bump(inst)
        ch_v3.wait(nc.tensor)
        ti = 0
        for kc in range(8):
            for b in range(GB):
                sl = ti % 2
                tp_c[sl].wait(nc.tensor)
                tp_p[sl].bump(nc.tensor.transpose(ps_tp[sl][:], xbf[:, b, kc*128:(kc+1)*128], idn[:]))
                tp_p[sl].wait(nc.vector)
                tp_c[sl].bump(nc.vector.tensor_copy(xfm[:, kc, b*128:(b+1)*128], ps_tp[sl][:]))
                ti += 1
        ch_w.wait(nc.tensor)
        for mc in range(16):
            sl = mc % 2
            mm_c[sl].wait(nc.tensor)
            for kc in range(8):
                nc.tensor.matmul(ps_mm[:, sl, :], wq[:, kc, mc*128:(mc+1)*128], xfm[:, kc, :],
                                 start=(kc == 0), stop=False)
            mm_p[sl].bump(nc.tensor.matmul(ps_mm[:, sl, :], bq[:, mc*128:(mc+1)*128], onesr[:],
                                           start=False, stop=True))
            mm_p[sl].wait(nc.vector)
            third, mloc = (0, mc) if mc < 8 else (1, mc-8)
            dst = qs_t if third == 0 else ks_t
            scale = 0.125 if third == 0 else 1.0
            for jsub in range(2):
                j = 2*mloc + jsub
                for b in range(GB):
                    inst = nc.vector.tensor_scalar(dst[:, b, j:2048:16], ps_mm[jsub*64:(jsub+1)*64, sl, b*128:(b+1)*128],
                                                   scale, None, op0=ALU.mult)
            mm_c[sl].bump(inst)
        for b in range(GB):
            for nn in range(2):
                sl = nn % 2
                mm_c[sl].wait(nc.tensor)
                for kc in range(8):
                    nc.tensor.matmul(ps_mm[:, sl, :], xfm[:, kc, b*128:(b+1)*128], wq[:, kc, 2048+nn*512:2048+(nn+1)*512],
                                     start=(kc == 0), stop=False)
                mm_p[sl].bump(nc.tensor.matmul(ps_mm[:, sl, :], onesr[:, 0:128], bq[:, 2048+nn*512:2048+(nn+1)*512],
                                               start=False, stop=True))
                mm_p[sl].wait(nc.vector)
                mm_c[sl].bump(nc.vector.tensor_copy(vtm[:, b, nn*512:(nn+1)*512], ps_mm[:, sl, :]))
        mm_c[0].wait(nc.gpsimd); mm_c[1].wait(nc.gpsimd)
        for b in range(GB):
            for kt in range(16):
                ch_fold.dma(nc.gpsimd.dma_start(out=vsp[:, b, kt, 0:64],
                            in_=vtm[kt*8:(kt+1)*8, b, :].rearrange("p (j dd) -> p j dd", dd=64)))
        for b in range(GB):
            ch_fold.wait(nc.tensor)
            for qsup in range(4):
                nkt = 4*qsup + 4
                for kt in range(nkt):
                    sl = kt % 2
                    nc.tensor.wait_ge(ch_ae.sem, slot_need[sl])
                    inst = nc.tensor.matmul(ps_s[:, sl, :], ks_t[:, b, kt*128:(kt+1)*128],
                                            qs_t[:, b, qsup*512:(qsup+1)*512], start=True, stop=True)
                    s_p[sl].bump(inst)
                    s_p[sl].wait(nc.scalar)
                    inst = nc.scalar.activation(atile[:, kt, :], ps_s[:, sl, :], AF.Exp, bias=cst[0:128, 1:2])
                    ch_ae.bump(inst)
                    slot_need[sl] = ch_ae.n
                ch_ae.wait(nc.vector)
                for mi in range(4):
                    kt = 4*qsup + mi
                    inst = nc.vector.tensor_tensor(atile[:, kt, :], atile[:, kt, :], mask[:, mi, :], op=ALU.mult)
                ch_am.bump(inst)
                ch_ae.wait(nc.tensor); ch_am.wait(nc.tensor); ch_ye.wait(nc.tensor)
                for kt in range(nkt):
                    inst = nc.tensor.matmul(ps_av[:, :], vsp[:, b, kt, :], atile[:, kt, :],
                                            start=(kt == 0), stop=(kt == nkt-1))
                ch_avd.bump(inst)
                ch_avd.wait(nc.vector)
                nc.vector.tensor_copy(ybuf[:, qsup*512:(qsup+1)*512], ps_av[0:64, :])
                inst = nc.vector.tensor_copy(dbuf[:, qsup*512:(qsup+1)*512], ps_av[64:65, :])
                ch_ye.bump(inst)
            ch_ye.wait(nc.sync)
            ch_yd.dma(nc.sync.dma_start(out=y_out[hloc, g*GB+b, :, :], in_=ybuf[:]))
            ch_yd.dma(nc.sync.dma_start(out=d_out[hloc, g*GB+b, :], in_=dbuf[:]))
            ch_yd.wait(nc.sync)
        nc.all_engine_barrier()
    ch_yd.wait(nc.sync)
    return nc


def host_constants():
    idn = np.eye(128, dtype=np.float32)
    masks = np.zeros((4, 128, 512), np.float32)
    for mi, o in enumerate((0, 128, 256, 384)):
        for k in range(128):
            masks[mi, k, :] = (np.arange(512) >= k + o)
    return idn, masks


def emit_ln(nc, xt, xbf, stat, cst_eps, ch):
    ch.bump(nc.vector.reduce_sum(stat[:, 0:1], xt[:], axis=AX.X))
    ch.bump(nc.scalar.activation(xbf[:], xt[:], AF.Square, accum_out=stat[:, 1:2]))
    ch.wait(nc.vector)
    ch.bump(nc.vector.tensor_scalar_mul(stat[:, 2:3], stat[:, 0:1], 1.0/1024.0))
    ch.wait(nc.vector)
    nc.vector.tensor_tensor(stat[:, 3:4], stat[:, 2:3], stat[:, 2:3], op=ALU.mult)
    ch.bump(nc.vector.tensor_scalar(stat[:, 4:5], stat[:, 1:2], 1.0/1024.0, None, op0=ALU.mult))
    ch.wait(nc.vector)
    ch.bump(nc.vector.tensor_tensor(stat[:, 4:5], stat[:, 4:5], stat[:, 3:4], op=ALU.subtract))
    ch.wait(nc.scalar)
    ch.bump(nc.scalar.activation(stat[:, 5:6], stat[:, 4:5], AF.Ln, bias=cst_eps))
    ch.wait(nc.scalar)
    ch.bump(nc.scalar.activation(stat[:, 5:6], stat[:, 5:6], AF.Exp, scale=-0.5))
    ch.wait(nc.vector)
    ch.bump(nc.vector.tensor_scalar(xbf[:], xt[:], stat[:, 2:3], stat[:, 5:6],
                                    op0=ALU.subtract, op1=ALU.mult))


def emit_transposes(nc, srcs, dsts, idn, ps_tp, tp_p, tp_c, state):
    for src, dst in zip(srcs, dsts):
        sl = state[0] % 2
        tp_c[sl].wait(nc.tensor)
        tp_p[sl].bump(nc.tensor.transpose(ps_tp[sl][:], src, idn[:]))
        tp_p[sl].wait(nc.vector)
        tp_c[sl].bump(nc.vector.tensor_copy(dst, ps_tp[sl][:]))
        state[0] += 1


def build_l2(nc, stack):
    yg  = nc.dram_tensor("yg", [16, NB, 64, NPOS], BF16, kind="ExternalInput").ap()
    den = nc.dram_tensor("den", [16, NB, NPOS], F32, kind="ExternalInput").ap()
    x   = nc.dram_tensor("x", [NB, NPOS, D], F32, kind="ExternalInput").ap()
    wp  = nc.dram_tensor("wp", [D, D], F32, kind="ExternalInput").ap()
    bp  = nc.dram_tensor("bp", [D], F32, kind="ExternalInput").ap()
    sel = nc.dram_tensor("sel", [8, 16, 128], F32, kind="ExternalInput").ap()
    x1  = nc.dram_tensor("x1", [NB, NPOS, D], F32, kind="ExternalOutput").ap()
    E = stack.enter_context
    wps  = E(nc.sbuf_tensor("wps", [128, 8, 1024], BF16))
    bps  = E(nc.sbuf_tensor("bps", [1, 1024], BF16))
    sels = E(nc.sbuf_tensor("sels", [16, 8, 128], F32))
    onesr= E(nc.sbuf_tensor("onesr", [1, 128], BF16))
    yfm  = E(nc.sbuf_tensor("yfm", [128, 8, NPOS], BF16))
    dent = E(nc.sbuf_tensor("dent", [16, NPOS], F32))
    recip= E(nc.sbuf_tensor("recip", [16, NPOS], F32))
    ynrm = E(nc.sbuf_tensor("ynrm", [128, 8, NPOS], BF16))
    xres = E(nc.sbuf_tensor("xres", [128, 2, 1024], F32))
    x1t  = E(nc.sbuf_tensor("x1t", [128, 2, 1024], F32))
    ps_bc = E(nc.psum_tensor("ps_bc", [128, 512], F32))
    ps_o0 = E(nc.psum_tensor("ps_o0", [128, 512], F32))
    ps_o1 = E(nc.psum_tensor("ps_o1", [128, 512], F32))
    ch_w = Chain(nc, "l2w", stack); ch_y = Chain(nc, "l2y", stack)
    ch_b = Chain(nc, "l2b", stack); ch_n = Chain(nc, "l2n", stack)
    ch_mm = Chain(nc, "l2mm", stack); ch_ev = Chain(nc, "l2ev", stack)
    ch_od = Chain(nc, "l2od", stack)
    for kc in range(8):
        ch_w.dma(nc.gpsimd.dma_start(out=wps[:, kc, :], in_=wp[kc*128:(kc+1)*128, :]))
    ch_w.dma(nc.gpsimd.dma_start(out=bps[:], in_=bp[None, :]))
    ch_w.dma(nc.sync.dma_start(out=sels[:], in_=sel.transpose([1, 0, 2])))
    ch_w.bump(nc.vector.memset(onesr[:], 1.0))
    ch_w.wait(nc.sync); ch_w.wait(nc.vector); ch_w.wait(nc.tensor); ch_w.wait(nc.scalar); ch_w.wait(nc.gpsimd)
    nc.all_engine_barrier()
    for b in range(NB):
        for j in range(8):
            ch_y.dma(nc.sync.dma_start(out=yfm[:, j, :], in_=yg[2*j:2*j+2, b, :, :]))
        ch_y.dma(nc.sync.dma_start(out=dent[:], in_=den[:, b, :]))
        for pt in range(2):
            ch_y.dma(nc.sync.dma_start(out=xres[:, pt, :], in_=x[b, pt*128:(pt+1)*128, :]))
        ch_y.wait(nc.vector)
        ch_n.bump(nc.vector.reciprocal(recip[:], dent[:]))
        ch_n.wait(nc.tensor)
        for j in range(8):
            ch_b.bump(nc.tensor.matmul(ps_bc[:, 0:NPOS], sels[:, j, :], recip[:], start=True, stop=True))
            ch_b.wait(nc.vector)
            ch_n.bump(nc.vector.tensor_tensor(ynrm[:, j, :], yfm[:, j, :], ps_bc[:, 0:NPOS], op=ALU.mult))
            ch_n.wait(nc.tensor)
        for pt in range(2):
            ch_ev.wait(nc.tensor)
            for nn, ps in ((0, ps_o0), (1, ps_o1)):
                for kc in range(8):
                    nc.tensor.matmul(ps[:], ynrm[:, kc, pt*128:(pt+1)*128], wps[:, kc, nn*512:(nn+1)*512],
                                     start=(kc == 0), stop=False)
                ch_mm.bump(nc.tensor.matmul(ps[:], onesr[:], bps[:, nn*512:(nn+1)*512], start=False, stop=True))
            ch_mm.wait(nc.vector)
            nc.vector.tensor_add(x1t[:, pt, 0:512], ps_o0[:], xres[:, pt, 0:512])
            ch_ev.bump(nc.vector.tensor_add(x1t[:, pt, 512:1024], ps_o1[:], xres[:, pt, 512:1024]))
            ch_ev.wait(nc.sync)
            ch_od.dma(nc.sync.dma_start(out=x1[b, pt*128:(pt+1)*128, :], in_=x1t[:, pt, :]))
        nc.all_engine_barrier()
        ch_od.wait(nc.sync)
    nc.all_engine_barrier()
    return nc


def build_l3(nc, stack):
    x1  = nc.dram_tensor("x1", [NB, NPOS, D], F32, kind="ExternalInput").ap()
    w3  = nc.dram_tensor("w3", [D, 3*D], F32, kind="ExternalInput").ap()
    u3  = nc.dram_tensor("u3", [D, 3*D], F32, kind="ExternalInput").ap()
    b3  = nc.dram_tensor("b3", [3*D], F32, kind="ExternalInput").ap()
    wo  = nc.dram_tensor("wo", [D, D], F32, kind="ExternalInput").ap()
    uo  = nc.dram_tensor("uo", [D, D], F32, kind="ExternalInput").ap()
    bo  = nc.dram_tensor("bo", [D], F32, kind="ExternalInput").ap()
    inc = nc.dram_tensor("inc", [D], F32, kind="ExternalInput").ap()
    idn_in = nc.dram_tensor("idn", [128, 128], F32, kind="ExternalInput").ap()
    x2  = nc.dram_tensor("x2", [NB, NPOS, D], F32, kind="ExternalOutput").ap()
    E = stack.enter_context
    w3s = E(nc.sbuf_tensor("w3s", [128, 8, 3072], BF16))
    u3s = E(nc.sbuf_tensor("u3s", [128, 8, 3072], BF16))
    b3s = E(nc.sbuf_tensor("b3s", [1, 3072], BF16))
    wos = E(nc.sbuf_tensor("wos", [128, 8, 1024], BF16))
    uos = E(nc.sbuf_tensor("uos", [128, 8, 1024], BF16))
    bos = E(nc.sbuf_tensor("bos", [1, 1024], BF16))
    incr= E(nc.sbuf_tensor("incr", [1, 1024], F32))
    ones32 = E(nc.sbuf_tensor("ones32", [1, 128], F32))
    idn = E(nc.sbuf_tensor("idn3", [128, 128], BF16))
    onesr=E(nc.sbuf_tensor("onesr3", [1, 128], BF16))
    cst = E(nc.sbuf_tensor("cst3", [128, 1], F32))
    xt  = E(nc.sbuf_tensor("xt3", [128, 2, 1024], F32))
    stat= E(nc.sbuf_tensor("stat3", [128, 12], F32))
    xbf = E(nc.sbuf_tensor("xbf3", [128, 2, 1024], BF16))
    xsf = E(nc.sbuf_tensor("xsf", [128, 8, 256], BF16))
    c_tm= E(nc.sbuf_tensor("c_tm", [128, 2, 1024], F32))
    cfm = E(nc.sbuf_tensor("cfm", [128, 8, 256], BF16))
    gat = E(nc.sbuf_tensor("gat", [128, 2, 3072], F32))
    cuo = E(nc.sbuf_tensor("cuo", [128, 2, 1024], F32))
    ht  = E(nc.sbuf_tensor("ht", [128, 1024], F32))
    ps_g  = E(nc.psum_tensor("ps_g", [128, 3072], F32))
    ps_t0 = E(nc.psum_tensor("ps_t0", [128, 128], BF16))
    ps_t1 = E(nc.psum_tensor("ps_t1", [128, 128], BF16))
    ps_tp = [ps_t0, ps_t1]
    ch_w = Chain(nc, "l3w", stack); ch_x = Chain(nc, "l3x", stack)
    ch_ln = Chain(nc, "l3ln", stack)
    tp_p = [Chain(nc, f"l3tpp{i}", stack) for i in range(2)]
    tp_c = [Chain(nc, f"l3tpc{i}", stack) for i in range(2)]
    ch_cc = Chain(nc, "l3cc", stack)
    ch_mm = Chain(nc, "l3mm", stack)
    ch_ge = Chain(nc, "l3ge", stack)
    ch_cu = Chain(nc, "l3cu", stack)
    ch_h  = Chain(nc, "l3h", stack)
    ch_ta = Chain(nc, "l3ta", stack)
    ch_od = Chain(nc, "l3od", stack)
    tstate = [0]

    for kc in range(8):
        ch_w.dma(nc.gpsimd.dma_start(out=w3s[:, kc, :], in_=w3[kc*128:(kc+1)*128, :]))
        ch_w.dma(nc.gpsimd.dma_start(out=u3s[:, kc, :], in_=u3[kc*128:(kc+1)*128, :]))
        ch_w.dma(nc.gpsimd.dma_start(out=wos[:, kc, :], in_=wo[kc*128:(kc+1)*128, :]))
        ch_w.dma(nc.gpsimd.dma_start(out=uos[:, kc, :], in_=uo[kc*128:(kc+1)*128, :]))
    ch_w.dma(nc.gpsimd.dma_start(out=b3s[:], in_=b3[None, :]))
    ch_w.dma(nc.gpsimd.dma_start(out=bos[:], in_=bo[None, :]))
    ch_w.dma(nc.gpsimd.dma_start(out=idn[:], in_=idn_in[:]))
    ch_w.dma(nc.sync.dma_start(out=incr[:], in_=inc[None, :]))
    nc.vector.memset(onesr[:], 1.0)
    nc.vector.memset(ones32[:], 1.0)
    ch_w.bump(nc.vector.memset(cst[:], LN_EPS))
    ch_w.wait(nc.sync); ch_w.wait(nc.vector); ch_w.wait(nc.tensor); ch_w.wait(nc.scalar); ch_w.wait(nc.gpsimd)
    nc.all_engine_barrier()

    for nn in range(2):
        ch_cc.bump(nc.tensor.matmul(ps_g[:, nn*512:(nn+1)*512], ones32[:], incr[:, nn*512:(nn+1)*512],
                                    start=True, stop=True))
    ch_cc.wait(nc.vector)
    nc.vector.tensor_copy(c_tm[:, 0, :], ps_g[:, 0:1024])
    nc.vector.tensor_copy(c_tm[:, 1, :], ps_g[:, 0:1024])
    nc.all_engine_barrier()

    def ln_tr(t, dest):
        for pt in range(2):
            ch_x.dma(nc.sync.dma_start(out=xt[:, pt, :], in_=x1[t, pt*128:(pt+1)*128, :]))
        ch_x.wait(nc.vector); ch_x.wait(nc.scalar)
        for pt in range(2):
            emit_ln(nc, xt[:, pt, :], xbf[:, pt, :], stat[:, 6*pt:6*pt+6], cst[:], ch_ln)
        ch_ln.wait(nc.tensor)
        emit_transposes(nc,
            [xbf[:, pt, kc*128:(kc+1)*128] for pt in range(2) for kc in range(8)],
            [dest[:, kc, pt*128:(pt+1)*128] for pt in range(2) for kc in range(8)],
            idn, ps_tp, tp_p, tp_c, tstate)

    for t in range(NB):
        ln_tr(t, xsf)
        ch_cc.bump(nc.vector.tensor_copy(xbf[:], c_tm[:]))
        ch_cc.wait(nc.tensor)
        emit_transposes(nc,
            [xbf[:, pt, kc*128:(kc+1)*128] for pt in range(2) for kc in range(8)],
            [cfm[:, kc, pt*128:(pt+1)*128] for pt in range(2) for kc in range(8)],
            idn, ps_tp, tp_p, tp_c, tstate)
        tp_c[0].wait(nc.tensor); tp_c[1].wait(nc.tensor)
        for pt in range(2):
            ch_ge.wait(nc.tensor)
            for n6 in range(6):
                for kc in range(8):
                    nc.tensor.matmul(ps_g[:, n6*512:(n6+1)*512], cfm[:, kc, pt*128:(pt+1)*128],
                                     u3s[:, kc, n6*512:(n6+1)*512], start=(kc == 0), stop=False)
                for kc in range(8):
                    nc.tensor.matmul(ps_g[:, n6*512:(n6+1)*512], xsf[:, kc, pt*128:(pt+1)*128],
                                     w3s[:, kc, n6*512:(n6+1)*512], start=False, stop=False)
                ch_mm.bump(nc.tensor.matmul(ps_g[:, n6*512:(n6+1)*512], onesr[:],
                                            b3s[:, n6*512:(n6+1)*512], start=False, stop=True))
            ch_mm.wait(nc.scalar)
            nc.scalar.activation(gat[:, pt, 0:2048], ps_g[:, 0:2048], AF.Sigmoid)
            ch_ge.bump(nc.scalar.activation(gat[:, pt, 2048:3072], ps_g[:, 2048:3072], AF.Tanh))
        ch_ge.wait(nc.vector)
        ch_cu.bump(nc.vector.tensor_tensor(gat[:, :, 1024:2048], gat[:, :, 1024:2048], gat[:, :, 2048:3072], op=ALU.mult))
        ch_cu.bump(nc.vector.tensor_tensor(c_tm[:], c_tm[:], gat[:, :, 0:1024], op=ALU.mult))
        ch_cu.wait(nc.vector)
        ch_cu.bump(nc.vector.tensor_tensor(c_tm[:], c_tm[:], gat[:, :, 1024:2048], op=ALU.add))
        nc.all_engine_barrier()
    ch_cc.bump(nc.vector.tensor_copy(xbf[:], c_tm[:]))
    ch_cc.wait(nc.tensor)
    emit_transposes(nc,
        [xbf[:, pt, kc*128:(kc+1)*128] for pt in range(2) for kc in range(8)],
        [cfm[:, kc, pt*128:(pt+1)*128] for pt in range(2) for kc in range(8)],
        idn, ps_tp, tp_p, tp_c, tstate)
    tp_c[0].wait(nc.tensor); tp_c[1].wait(nc.tensor)
    for pt in range(2):
        for nn in range(2):
            for kc in range(8):
                nc.tensor.matmul(ps_g[:, pt*1024+nn*512:pt*1024+(nn+1)*512], cfm[:, kc, pt*128:(pt+1)*128],
                                 uos[:, kc, nn*512:(nn+1)*512], start=(kc == 0), stop=False)
            ch_mm.bump(nc.tensor.matmul(ps_g[:, pt*1024+nn*512:pt*1024+(nn+1)*512], onesr[:],
                                        bos[:, nn*512:(nn+1)*512], start=False, stop=True))
    ch_mm.wait(nc.vector)
    nc.vector.tensor_copy(cuo[:, 0, :], ps_g[:, 0:1024])
    nc.vector.tensor_copy(cuo[:, 1, :], ps_g[:, 1024:2048])
    nc.all_engine_barrier()
    for t in range(NB):
        ln_tr(t, xsf)
        tp_c[0].wait(nc.tensor); tp_c[1].wait(nc.tensor)
        for pt in range(2):
            for nn in range(2):
                for kc in range(8):
                    inst = nc.tensor.matmul(ps_g[:, nn*512:(nn+1)*512], xsf[:, kc, pt*128:(pt+1)*128],
                                     wos[:, kc, nn*512:(nn+1)*512], start=(kc == 0), stop=(kc == 7))
                ch_mm.bump(inst)
            ch_mm.wait(nc.vector)
            ch_h.bump(nc.vector.tensor_add(ht[:], ps_g[:, 0:1024], cuo[:, pt, :]))
            ch_h.wait(nc.scalar)
            x2t = gat[:, 0, 0:1024]
            ch_ta.bump(nc.scalar.activation(x2t, ht[:], AF.Tanh))
            ch_ta.wait(nc.vector)
            ch_od.bump(nc.vector.tensor_add(x2t, x2t, xt[:, pt, :]))
            ch_od.wait(nc.sync)
            ch_od.dma(nc.sync.dma_start(out=x2[t, pt*128:(pt+1)*128, :], in_=x2t))
            nc.all_engine_barrier()
        ch_od.wait(nc.sync)
    nc.all_engine_barrier()
    return nc


def host_sel():
    sel = np.zeros((8, 16, 128), np.float32)
    for j in range(8):
        for h2 in range(2):
            sel[j, 2*j+h2, h2*64:(h2+1)*64] = 1.0
    return sel


def build_l4(nc, stack):
    x2  = nc.dram_tensor("x2", [NB, NPOS, D], F32, kind="ExternalInput").ap()
    wq  = nc.dram_tensor("wq", [D, 3*D], F32, kind="ExternalInput").ap()
    bq  = nc.dram_tensor("bq", [3*D], F32, kind="ExternalInput").ap()
    wo  = nc.dram_tensor("wo", [D, D], F32, kind="ExternalInput").ap()
    bo  = nc.dram_tensor("bo", [D], F32, kind="ExternalInput").ap()
    idn_in = nc.dram_tensor("idn", [128, 128], F32, kind="ExternalInput").ap()
    x3  = nc.dram_tensor("x3", [NB, NPOS, D], F32, kind="ExternalOutput").ap()
    E = stack.enter_context
    wqs = E(nc.sbuf_tensor("wqs4", [128, 8, 3072], BF16))
    bqs = E(nc.sbuf_tensor("bqs4", [1, 3072], BF16))
    wos = E(nc.sbuf_tensor("wos4", [128, 8, 1024], BF16))
    bos = E(nc.sbuf_tensor("bos4", [1, 1024], BF16))
    idn = E(nc.sbuf_tensor("idn4", [128, 128], BF16))
    onesr=E(nc.sbuf_tensor("onesr4", [1, 128], BF16))
    cst = E(nc.sbuf_tensor("cst4", [128, 1], F32))
    xt  = E(nc.sbuf_tensor("xt4", [128, 1024], F32))
    stat= E(nc.sbuf_tensor("stat4", [128, 6], F32))
    xbf = E(nc.sbuf_tensor("xbf4", [128, 1024], BF16))
    xfm = E(nc.sbuf_tensor("xfm4", [128, 8, 128], BF16))
    qkv = E(nc.sbuf_tensor("qkv4", [128, 3072], BF16))
    stm = E(nc.sbuf_tensor("stm4", [128, 256], F32))
    etm = E(nc.sbuf_tensor("etm4", [128, 256], F32))
    mx  = E(nc.sbuf_tensor("mx4", [128, 16], F32))
    sm  = E(nc.sbuf_tensor("sm4", [128, 16], F32))
    atm = E(nc.sbuf_tensor("atm4", [128, 256], BF16))
    prd = [E(nc.sbuf_tensor(f"prd4_{i}", [128, 1024], BF16)) for i in range(2)]
    ctxb= E(nc.sbuf_tensor("ctxb4", [128, 1024], BF16))
    ctxf= E(nc.sbuf_tensor("ctxf4", [128, 8, 128], BF16))
    x3t = E(nc.sbuf_tensor("x3t4", [128, 1024], F32))
    ps_q0 = E(nc.psum_tensor("ps_q0", [128, 512], F32))
    ps_q1 = E(nc.psum_tensor("ps_q1", [128, 512], F32))
    ps_c  = E(nc.psum_tensor("ps_c4", [128, 1024], F32))
    ps_t0 = E(nc.psum_tensor("ps_t04", [128, 128], BF16))
    ps_t1 = E(nc.psum_tensor("ps_t14", [128, 128], BF16))
    ps_tp = [ps_t0, ps_t1]
    ps_qs = [ps_q0, ps_q1]
    ch_w = Chain(nc, "l4w", stack); ch_x = Chain(nc, "l4x", stack); ch_ln = Chain(nc, "l4ln", stack)
    tp_p = [Chain(nc, f"l4tpp{i}", stack) for i in range(2)]
    tp_c = [Chain(nc, f"l4tpc{i}", stack) for i in range(2)]
    ch_mm = Chain(nc, "l4mm", stack); ch_qe = Chain(nc, "l4qe", stack)
    ch_p  = Chain(nc, "l4p", stack)
    ch_r  = Chain(nc, "l4r", stack)
    ch_sx = Chain(nc, "l4sx", stack)
    ch_cm = Chain(nc, "l4cm", stack)
    ch_ce = Chain(nc, "l4ce", stack)
    ch_od = Chain(nc, "l4od", stack)
    tstate = [0]
    for kc in range(8):
        ch_w.dma(nc.gpsimd.dma_start(out=wqs[:, kc, :], in_=wq[kc*128:(kc+1)*128, :]))
        ch_w.dma(nc.gpsimd.dma_start(out=wos[:, kc, :], in_=wo[kc*128:(kc+1)*128, :]))
    ch_w.dma(nc.gpsimd.dma_start(out=bqs[:], in_=bq[None, :]))
    ch_w.dma(nc.gpsimd.dma_start(out=bos[:], in_=bo[None, :]))
    ch_w.dma(nc.gpsimd.dma_start(out=idn[:], in_=idn_in[:]))
    nc.vector.memset(onesr[:], 1.0)
    ch_w.bump(nc.vector.memset(cst[:], LN_EPS))
    ch_w.wait(nc.sync); ch_w.wait(nc.vector); ch_w.wait(nc.tensor); ch_w.wait(nc.scalar); ch_w.wait(nc.gpsimd)
    nc.all_engine_barrier()

    def emit_tp(srcs, dsts):
        for src, dst in zip(srcs, dsts):
            sl = tstate[0] % 2
            tp_c[sl].wait(nc.tensor)
            tp_p[sl].bump(nc.tensor.transpose(ps_tp[sl][:], src, idn[:]))
            tp_p[sl].wait(nc.vector)
            tp_c[sl].bump(nc.vector.tensor_copy(dst, ps_tp[sl][:]))
            tstate[0] += 1

    for ti in range(NTOK // 128):
        b, pt = ti // 2, ti % 2
        ch_x.dma(nc.sync.dma_start(out=xt[:], in_=x2[b, pt*128:(pt+1)*128, :]))
        ch_x.wait(nc.vector); ch_x.wait(nc.scalar)
        emit_ln(nc, xt[:], xbf[:], stat[:], cst[:], ch_ln)
        ch_ln.wait(nc.tensor)
        emit_tp([xbf[:, kc*128:(kc+1)*128] for kc in range(8)],
                [xfm[:, kc, :] for kc in range(8)])
        tp_c[0].wait(nc.tensor); tp_c[1].wait(nc.tensor)
        for n6 in range(6):
            sl = n6 % 2
            ch_qe.wait(nc.tensor) if n6 >= 2 else None
            for kc in range(8):
                nc.tensor.matmul(ps_qs[sl][:], xfm[:, kc, :], wqs[:, kc, n6*512:(n6+1)*512],
                                 start=(kc == 0), stop=False)
            ch_mm.bump(nc.tensor.matmul(ps_qs[sl][:], onesr[:], bqs[:, n6*512:(n6+1)*512],
                                        start=False, stop=True))
            ch_mm.wait(nc.vector)
            if n6 < 2:
                ch_qe.bump(nc.vector.tensor_scalar(qkv[:, n6*512:(n6+1)*512], ps_qs[sl][:], 0.125, None, op0=ALU.mult))
            else:
                ch_qe.bump(nc.vector.tensor_copy(qkv[:, n6*512:(n6+1)*512], ps_qs[sl][:]))
        q = qkv[:, 0:1024]; k = qkv[:, 1024:2048]; v = qkv[:, 2048:3072]
        stm3 = stm.rearrange("p (h d) -> p h d", d=16)
        ch_qe.wait(nc.vector)
        for dl in range(16):
            sl = dl % 2
            ch_r.wait(nc.vector)
            w1 = (16 - dl) * 64
            nc.vector.tensor_tensor(prd[sl][:, 0:w1], q[:, 0:w1], k[:, dl*64:1024], op=ALU.mult)
            if dl > 0:
                inst = nc.vector.tensor_tensor(prd[sl][:, w1:1024], q[:, w1:1024], k[:, 0:dl*64], op=ALU.mult)
            else:
                inst = nc.vector.engine_nop()
            ch_p.bump(inst)
            ch_p.wait(nc.vector)
            ch_r.bump(nc.vector.tensor_reduce(stm3[:, :, dl:dl+1],
                        prd[sl][:].rearrange("p (h d) -> p h d", d=64), axis=AX.X, op=ALU.add))
        ch_r.wait(nc.vector)
        ch_sx.bump(nc.vector.reduce_max(mx[:, :, None], stm3[:, :, :], axis=AX.X))
        ch_sx.wait(nc.vector)
        ch_sx.bump(nc.vector.tensor_tensor(etm.rearrange("p (h d) -> p h d", d=16), stm3[:, :, :],
                                           mx[:, :, None].broadcast_to([128, 16, 16]), op=ALU.subtract))
        ch_sx.wait(nc.scalar)
        ch_sx.bump(nc.scalar.activation(etm[:], etm[:], AF.Exp))
        ch_sx.wait(nc.vector)
        ch_sx.bump(nc.vector.reduce_sum(sm[:, :, None], etm.rearrange("p (h d) -> p h d", d=16), axis=AX.X))
        ch_sx.wait(nc.vector)
        ch_sx.bump(nc.vector.reciprocal(sm[:], sm[:]))
        ch_sx.wait(nc.vector)
        ch_sx.bump(nc.vector.tensor_tensor(atm.rearrange("p (h d) -> p h d", d=16),
                                           etm.rearrange("p (h d) -> p h d", d=16),
                                           sm[:, :, None].broadcast_to([128, 16, 16]), op=ALU.mult))
        atm3 = atm.rearrange("p (h d) -> p h d", d=16)
        ch_sx.wait(nc.vector)
        for dl in range(16):
            sl = dl % 2
            ch_cm.wait(nc.vector)
            w1 = (16 - dl) * 64
            p3a = prd[sl][:, 0:w1].rearrange("p (h d) -> p h d", d=64)
            nc.vector.tensor_tensor(p3a, v[:, dl*64:1024].rearrange("p (h d) -> p h d", d=64),
                                    atm3[:, 0:(16-dl), dl:dl+1].broadcast_to([128, 16-dl, 64]), op=ALU.mult)
            if dl > 0:
                p3b = prd[sl][:, w1:1024].rearrange("p (h d) -> p h d", d=64)
                inst = nc.vector.tensor_tensor(p3b, v[:, 0:dl*64].rearrange("p (h d) -> p h d", d=64),
                                        atm3[:, (16-dl):16, dl:dl+1].broadcast_to([128, dl, 64]), op=ALU.mult)
            else:
                inst = nc.vector.engine_nop()
            ch_p.bump(inst)
            ch_p.wait(nc.tensor)
            for nn in range(2):
                inst = nc.tensor.matmul(ps_c[:, nn*512:(nn+1)*512], idn[:], prd[sl][:, nn*512:(nn+1)*512],
                                        start=(dl == 0), stop=(dl == 15))
            ch_cm.bump(inst)
        ch_cm.wait(nc.vector)
        ch_ce.bump(nc.vector.tensor_copy(ctxb[:], ps_c[:]))
        ch_ce.wait(nc.tensor)
        emit_tp([ctxb[:, kc*128:(kc+1)*128] for kc in range(8)],
                [ctxf[:, kc, :] for kc in range(8)])
        tp_c[0].wait(nc.tensor); tp_c[1].wait(nc.tensor)
        for nn in range(2):
            sl = nn
            for kc in range(8):
                nc.tensor.matmul(ps_qs[sl][:], ctxf[:, kc, :], wos[:, kc, nn*512:(nn+1)*512],
                                 start=(kc == 0), stop=False)
            ch_mm.bump(nc.tensor.matmul(ps_qs[sl][:], onesr[:], bos[:, nn*512:(nn+1)*512],
                                        start=False, stop=True))
        ch_mm.wait(nc.vector)
        nc.vector.tensor_add(x3t[:, 0:512], ps_q0[:], xt[:, 0:512])
        ch_od.bump(nc.vector.tensor_add(x3t[:, 512:1024], ps_q1[:], xt[:, 512:1024]))
        ch_od.wait(nc.sync)
        ch_od.dma(nc.sync.dma_start(out=x3[b, pt*128:(pt+1)*128, :], in_=x3t[:]))
        nc.all_engine_barrier()
    ch_od.wait(nc.sync)
    nc.all_engine_barrier()
    return nc


def host_l4_weights(Wqkv, bqkv):
    idx = []
    for third in range(3):
        for h in range(16):
            idx.extend(range(h*192 + third*64, h*192 + third*64 + 64))
    idx = np.array(idx)
    return np.ascontiguousarray(Wqkv[:, idx]), np.ascontiguousarray(bqkv[idx])


def l5_band_vectors(weights):
    wband = np.zeros((12, 3, 4096), np.float32)
    for slot in range(12):
        s = slot
        is_g = slot in (6, 11)
        for fp in range(4096):
            if slot < 7:
                f = fp
            else:
                j = fp & 31; u = fp >> 5
                f = j*128 + u
            S_ref = 1 << s
            q = f >> (s+1); r = (f >> s) & 1; u_ref = f & (S_ref - 1)
            widx = q * S_ref + u_ref
            w = weights[s, widx]
            wband[slot, 0, fp] = w[0, 0] if r == 0 else w[1, 1]
            if is_g:
                wband[slot, 1, fp] = w[1, 0] if r == 0 else 0.0
                wband[slot, 2, fp] = w[0, 1] if r == 1 else 0.0
            else:
                wband[slot, 1, fp] = w[0, 1] if r == 0 else 0.0
                wband[slot, 2, fp] = w[1, 0] if r == 1 else 0.0
    return wband

def l5_band_stride(slot):
    return 1 << slot if slot < 7 else 1 << (slot - 7)

def l5_make_mats(wband, slot):
    S = l5_band_stride(slot)
    mats = np.zeros((32, 128, 128), np.float32)
    for c in range(32):
        for p in range(128):
            fp = c*128 + p
            mats[c, p, p] = wband[slot, 0, fp]
            if p + S < 128:
                mats[c, p, p+S] = wband[slot, 1, fp]
            if p - S >= 0:
                mats[c, p, p-S] = wband[slot, 2, fp]
    return mats

def l5_compose_host(wband, lo):
    slots = range(0, 7) if lo else range(7, 12)
    slots = list(slots)
    P = l5_make_mats(wband, slots[-1])
    for s in reversed(slots[:-1]):
        H = l5_make_mats(wband, s)
        P = np.einsum('cpm,cpn->cmn', H, P)
    return P

def l5_perm_idx():
    fp = np.arange(4096)
    j = fp & 31; u = fp >> 5
    return j*128 + u


def build_l5(nc, stack):
    x3  = nc.dram_tensor("x3", [NB, NPOS, D], F32, kind="ExternalInput").ap()
    wexp= nc.dram_tensor("wexp", [D, 4096], F32, kind="ExternalInput").ap()
    bexp= nc.dram_tensor("bexp", [4096], F32, kind="ExternalInput").ap()
    wband=nc.dram_tensor("wband", [2, 12, 3, 128, 32], F32, kind="ExternalInput").ap()
    upb = nc.dram_tensor("upb", [128, 32], F32, kind="ExternalInput").ap()
    wcon= nc.dram_tensor("wcon", [4096, D], F32, kind="ExternalInput").ap()
    bcon= nc.dram_tensor("bcon", [D], F32, kind="ExternalInput").ap()
    dbias=nc.dram_tensor("dbias", [D], F32, kind="ExternalInput").ap()
    idn_in = nc.dram_tensor("idn", [128, 128], F32, kind="ExternalInput").ap()
    x4  = nc.dram_tensor("x4", [NB, NPOS, D], F32, kind="ExternalOutput").ap()
    hfin = nc.dram_tensor("hfin", [128, 32, NTOK], BF16, kind="Internal").ap()
    E = stack.enter_context
    idn = E(nc.sbuf_tensor("idn5", [128, 128], BF16))
    ones5=E(nc.sbuf_tensor("ones5", [1, 512], BF16))
    cst = E(nc.sbuf_tensor("cst5", [128, 1], F32))
    upbs= E(nc.sbuf_tensor("upbs", [128, 32], F32))
    Lmat= E(nc.sbuf_tensor("Lmat", [128, 4, 32, 128], BF16))
    h1  = E(nc.sbuf_tensor("h1", [128, 32, 512], BF16))
    hp  = E(nc.sbuf_tensor("hp", [128, 32, 512], BF16))
    xt  = E(nc.sbuf_tensor("xt5", [128, 1024], F32))
    stat= E(nc.sbuf_tensor("stat5", [128, 6], F32))
    xbf = E(nc.sbuf_tensor("xbf5", [128, 1024], BF16))
    xfm = E(nc.sbuf_tensor("xfm5", [128, 8, 512], BF16))
    h0c = E(nc.sbuf_tensor("h0c", [128, 2, 512], BF16))
    wv  = E(nc.sbuf_tensor("wv5", [128, 3, 32], BF16))
    bexps=E(nc.sbuf_tensor("bexps", [1, 4096], BF16))
    ps_a = E(nc.psum_tensor("ps_a5", [128, 512], F32))
    ps_b = E(nc.psum_tensor("ps_b5", [128, 512], F32))
    ps_t0= E(nc.psum_tensor("ps_t05", [128, 128], BF16))
    ps_t1= E(nc.psum_tensor("ps_t15", [128, 128], BF16))
    ps_c0= E(nc.psum_tensor("ps_c05", [128, 128], F32))
    ps_c1= E(nc.psum_tensor("ps_c15", [128, 128], F32))
    ps_tp = [ps_t0, ps_t1]
    ps_cc = [ps_c0, ps_c1]
    ch_w = Chain(nc, "l5w", stack); ch_x = Chain(nc, "l5x", stack); ch_ln = Chain(nc, "l5ln", stack)
    tp_p = [Chain(nc, f"l5tpp{i}", stack) for i in range(2)]
    tp_c = [Chain(nc, f"l5tpc{i}", stack) for i in range(2)]
    ch_bv = Chain(nc, "l5bv", stack)
    ch_bs = Chain(nc, "l5bs", stack)
    ch_ba = Chain(nc, "l5ba", stack)
    cc_p = [Chain(nc, f"l5ccp{i}", stack) for i in range(2)]
    cc_c = [Chain(nc, f"l5ccc{i}", stack) for i in range(2)]
    ch_mm = Chain(nc, "l5mm", stack)
    ch_ev = Chain(nc, "l5ev", stack)
    ch_pm = Chain(nc, "l5pm", stack)
    ch_hf = Chain(nc, "l5hf", stack)
    ch_od = Chain(nc, "l5od", stack)
    tstate = [0]

    ch_w.dma(nc.gpsimd.dma_start(out=idn[:], in_=idn_in[:]))
    ch_w.dma(nc.sync.dma_start(out=upbs[:], in_=upb[:]))
    ch_w.dma(nc.gpsimd.dma_start(out=bexps[:], in_=bexp[None, :]))
    nc.vector.memset(ones5[:], 1.0)
    ch_w.bump(nc.vector.memset(cst[:], LN_EPS))
    ch_w.wait(nc.sync); ch_w.wait(nc.vector); ch_w.wait(nc.tensor); ch_w.wait(nc.scalar); ch_w.wait(nc.gpsimd)
    nc.all_engine_barrier()

    bufA = h1[:, :, 0:128]; bufB = h1[:, :, 128:256]
    Hbuf = h1[:, :, 256:384]; t0 = h1[:, :, 384:512]; t1 = hp[:, :, 0:128]
    def construct(bf, slot, dest):
        S = l5_band_stride(slot)
        for band in range(3):
            ch_bv.dma(nc.gpsimd.dma_start(out=wv[:, band, :], in_=wband[bf, slot, band, :, :]))
        ch_bv.wait(nc.gpsimd); ch_ba.wait(nc.gpsimd)
        for band, (dst, off) in enumerate(((t0, 0), (t1, S), (dest, -S))):
            ch_bs.bump(nc.gpsimd.affine_select(out=dst, in_=wv[:, band, :, None].broadcast_to([128, 32, 128]),
                        compare_op=ALU.is_equal, fill=0.0, base=-off,
                        pattern=[[0, 32], [1, 128]], channel_multiplier=-1))
        ch_bs.wait(nc.vector)
        nc.vector.tensor_tensor(dest, dest, t0, op=ALU.add)
        ch_ba.bump(nc.vector.tensor_tensor(dest, dest, t1, op=ALU.add))
    for bf in range(2):
        for lo in range(2):
            slots = list(range(0, 7)) if lo == 0 else list(range(7, 12))
            construct(bf, slots[-1], bufA)
            cur, nxt = bufA, bufB
            for s in reversed(slots[:-1]):
                construct(bf, s, Hbuf)
                ch_ba.wait(nc.tensor)
                for c in range(32):
                    sl = c % 2
                    cc_c[sl].wait(nc.tensor)
                    cc_p[sl].bump(nc.tensor.matmul(ps_cc[sl][:], Hbuf[:, c, :], cur[:, c, :],
                                                   start=True, stop=True))
                    cc_p[sl].wait(nc.vector)
                    cc_c[sl].bump(nc.vector.tensor_copy(nxt[:, c, :], ps_cc[sl][:]))
                cur, nxt = nxt, cur
                cc_c[0].wait(nc.gpsimd); cc_c[1].wait(nc.gpsimd)
            cc_c[0].wait(nc.vector); cc_c[1].wait(nc.vector)
            ch_ba.wait(nc.vector)
            seti = bf*2 + lo
            ch_ba.bump(nc.vector.tensor_copy(Lmat[:, seti, :, :], cur))
            ch_ba.wait(nc.tensor); ch_ba.wait(nc.gpsimd)
    nc.all_engine_barrier()

    with contextlib.ExitStack() as s1:
        wexps = s1.enter_context(nc.sbuf_tensor("wexps", [128, 8, 4096], BF16))
        for kc in range(8):
            ch_w.dma(nc.gpsimd.dma_start(out=wexps[:, kc, :], in_=wexp[kc*128:(kc+1)*128, :]))
        ch_w.wait(nc.tensor); ch_w.wait(nc.vector)
        nc.all_engine_barrier()
        def emit_tp(srcs, dsts):
            for src, dst in zip(srcs, dsts):
                sl = tstate[0] % 2
                tp_c[sl].wait(nc.tensor)
                tp_p[sl].bump(nc.tensor.transpose(ps_tp[sl][:], src, idn[:]))
                tp_p[sl].wait(nc.vector)
                tp_c[sl].bump(nc.vector.tensor_copy(dst, ps_tp[sl][:]))
                tstate[0] += 1
        NT = NTOK // 512
        for t in range(NT):
            for st4 in range(4):
                gtok = t*512 + st4*128
                b, po = gtok // NPOS, gtok % NPOS
                ch_ln.wait(nc.sync)
                ch_x.dma(nc.sync.dma_start(out=xt[:], in_=x3[b, po:po+128, :]))
                ch_x.wait(nc.vector); ch_x.wait(nc.scalar)
                emit_ln(nc, xt[:], xbf[:], stat[:], cst[:], ch_ln)
                ch_ln.wait(nc.tensor)
                emit_tp([xbf[:, kc*128:(kc+1)*128] for kc in range(8)],
                        [xfm[:, kc, st4*128:(st4+1)*128] for kc in range(8)])
            tp_c[0].wait(nc.tensor); tp_c[1].wait(nc.tensor)
            ch_pm.wait(nc.tensor)
            for c in range(32):
                sl = c % 2
                ch_ev.wait(nc.tensor) if c >= 2 else None
                for kc in range(8):
                    nc.tensor.matmul(ps_a[:], wexps[:, kc, c*128:(c+1)*128], xfm[:, kc, :],
                                     start=(kc == 0), stop=False)
                ch_mm.bump(nc.tensor.matmul(ps_a[:], bexps[:, c*128:(c+1)*128], ones5[:],
                                            start=False, stop=True))
                ch_mm.wait(nc.vector)
                ch_ev.bump(nc.vector.tensor_copy(h0c[:, sl, :], ps_a[:]))
                ch_ev.wait(nc.tensor)
                ch_mm.bump(nc.tensor.matmul(ps_b[:], Lmat[:, 0, c, :], h0c[:, sl, :],
                                            start=True, stop=True))
                ch_mm.wait(nc.vector)
                ch_ev.bump(nc.vector.tensor_copy(h1[:, c, :], ps_b[:]))
            ch_ev.wait(nc.gpsimd)
            for g in range(32):
                ch_pm.dma(nc.gpsimd.dma_start(out=hp[:, g, :], in_=h1[g*4:(g+1)*4, :, :]))
            ch_pm.wait(nc.tensor)
            for g in range(32):
                ch_mm.bump(nc.tensor.matmul(ps_b[:] if g % 2 == 0 else ps_a[:], Lmat[:, 1, g, :], hp[:, g, :],
                                            start=True, stop=True))
                ch_mm.wait(nc.scalar)
                ch_ev.bump(nc.scalar.activation(h1[:, g, :], ps_b[:] if g % 2 == 0 else ps_a[:],
                                                AF.Gelu, bias=upbs[:, g:g+1]))
                ch_ev.wait(nc.tensor)
            ch_ev.wait(nc.gpsimd)
            for g in range(32):
                ch_pm.dma(nc.gpsimd.dma_start(out=hp[g*4:(g+1)*4, :, :], in_=h1[:, g, :]))
            ch_pm.wait(nc.tensor)
            for c in range(32):
                ch_mm.bump(nc.tensor.matmul(ps_b[:] if c % 2 == 0 else ps_a[:], Lmat[:, 2, c, :], hp[:, c, :],
                                            start=True, stop=True))
                ch_mm.wait(nc.vector)
                ch_ev.bump(nc.vector.tensor_copy(h1[:, c, :], ps_b[:] if c % 2 == 0 else ps_a[:]))
                ch_ev.wait(nc.tensor)
            ch_ev.wait(nc.gpsimd)
            for g in range(32):
                ch_pm.dma(nc.gpsimd.dma_start(out=hp[:, g, :], in_=h1[g*4:(g+1)*4, :, :]))
            ch_pm.wait(nc.tensor)
            for g in range(32):
                ch_mm.bump(nc.tensor.matmul(ps_b[:] if g % 2 == 0 else ps_a[:], Lmat[:, 3, g, :], hp[:, g, :],
                                            start=True, stop=True))
                ch_mm.wait(nc.vector)
                ch_ev.bump(nc.vector.tensor_copy(h1[:, g, :], ps_b[:] if g % 2 == 0 else ps_a[:]))
                ch_ev.wait(nc.tensor)
            ch_ev.wait(nc.sync)
            ch_hf.dma(nc.sync.dma_start(out=hfin[:, :, t*512:(t+1)*512], in_=h1[:]))
            ch_hf.wait(nc.sync)
        nc.all_engine_barrier()
    with contextlib.ExitStack() as s2:
        wcons = s2.enter_context(nc.sbuf_tensor("wcons", [128, 32, 1024], BF16))
        bcons = s2.enter_context(nc.sbuf_tensor("bcons", [1, 1024], BF16))
        dbs   = s2.enter_context(nc.sbuf_tensor("dbs", [1, 1024], BF16))
        x4t   = s2.enter_context(nc.sbuf_tensor("x4t", [128, 1024], F32))
        for g in range(32):
            ch_w.dma(nc.gpsimd.dma_start(out=wcons[:, g, :], in_=wcon[g*128:(g+1)*128, :]))
        ch_w.dma(nc.gpsimd.dma_start(out=bcons[:], in_=bcon[None, :]))
        ch_w.dma(nc.gpsimd.dma_start(out=dbs[:], in_=dbias[None, :]))
        ch_w.wait(nc.tensor); ch_w.wait(nc.vector); ch_w.wait(nc.sync)
        nc.all_engine_barrier()
        NT = NTOK // 512
        for t in range(NT):
            ch_mm.wait(nc.sync)
            ch_x.dma(nc.sync.dma_start(out=h1[:], in_=hfin[:, :, t*512:(t+1)*512]))
            for st4 in range(4):
                gtok = t*512 + st4*128
                b, po = gtok // NPOS, gtok % NPOS
                ch_od.wait(nc.sync)
                ch_x.dma(nc.sync.dma_start(out=xt[:], in_=x3[b, po:po+128, :]))
                ch_x.wait(nc.tensor)
                ch_od.wait(nc.tensor)
                for nn, ps in ((0, ps_a), (1, ps_b)):
                    for g in range(32):
                        nc.tensor.matmul(ps[:], h1[:, g, st4*128:(st4+1)*128], wcons[:, g, nn*512:(nn+1)*512],
                                         start=(g == 0), stop=False)
                    nc.tensor.matmul(ps[:], ones5[:, 0:128], bcons[:, nn*512:(nn+1)*512], start=False, stop=False)
                    ch_mm.bump(nc.tensor.matmul(ps[:], ones5[:, 0:128], dbs[:, nn*512:(nn+1)*512], start=False, stop=True))
                ch_mm.wait(nc.vector)
                nc.vector.tensor_add(x4t[:, 0:512], ps_a[:], xt[:, 0:512])
                ch_od.bump(nc.vector.tensor_add(x4t[:, 512:1024], ps_b[:], xt[:, 512:1024]))
                ch_od.wait(nc.sync)
                ch_od.dma(nc.sync.dma_start(out=x4[b, po:po+128, :], in_=x4t[:]))
        ch_od.wait(nc.sync)
        nc.all_engine_barrier()
    return nc



def _run(build, in_maps):
    nc = bass.Bass("TRN2", target_bir_lowering=False)
    with contextlib.ExitStack() as stack:
        build(nc, stack)
    return run_bass_kernel_spmd(nc, in_maps, core_ids=list(range(8))).results

def kernel(**inp):
    f32 = np.float32
    asf = lambda a: np.ascontiguousarray(np.asarray(a, dtype=f32))
    x0 = asf(inp["L0_attn_x"])
    idn, masks = host_constants()
    in_maps = [{"x": np.ascontiguousarray(x0[:, c*256:(c+1)*256, :]),
                "wqkv": asf(inp["L0_attn_Wqkv"]), "bqkv": asf(inp["L0_attn_bqkv"]),
                "idn": idn, "mask": masks} for c in range(8)]
    r1 = _run(build_l0attn, in_maps)
    Y = [r["y"] for r in r1]; DEN = [r["den"] for r in r1]
    sel = host_sel()
    in_maps = []
    for c in range(8):
        yg = np.ascontiguousarray(np.stack(
            [Y[src][hl, :, :, c*256:(c+1)*256] for src in range(8) for hl in range(2)]))
        dg = np.ascontiguousarray(np.stack(
            [DEN[src][hl, :, c*256:(c+1)*256] for src in range(8) for hl in range(2)]))
        in_maps.append({"yg": yg, "den": dg, "x": np.ascontiguousarray(x0[:, c*256:(c+1)*256, :]),
                        "wp": asf(inp["L0_attn_Wproj"]), "bp": asf(inp["L0_attn_bproj"]), "sel": sel})
    r2 = _run(build_l2, in_maps)
    X1 = [r["x1"] for r in r2]
    w3 = np.ascontiguousarray(np.concatenate(
        [asf(inp["L0_ffn_wf"]).T, asf(inp["L0_ffn_wi"]).T, asf(inp["L0_ffn_wc"]).T], axis=1))
    u3 = np.ascontiguousarray(np.concatenate(
        [asf(inp["L0_ffn_uf"]).T, asf(inp["L0_ffn_ui"]).T, asf(inp["L0_ffn_uc"]).T], axis=1))
    b3 = np.concatenate([asf(inp["L0_ffn_bf"]), asf(inp["L0_ffn_bi"]), asf(inp["L0_ffn_bc"])])
    in_maps = [{"x1": X1[c], "w3": w3, "u3": u3, "b3": b3,
                "wo": np.ascontiguousarray(asf(inp["L0_ffn_wo"]).T),
                "uo": np.ascontiguousarray(asf(inp["L0_ffn_uo"]).T),
                "bo": asf(inp["L0_ffn_bo"]), "inc": asf(inp["L0_ffn_in_c"]), "idn": idn}
               for c in range(8)]
    r3 = _run(build_l3, in_maps)
    X2 = [r["x2"] for r in r3]
    wqr, bqr = host_l4_weights(asf(inp["L1_attn_W_qkv"]), asf(inp["L1_attn_b_qkv"]))
    in_maps = [{"x2": X2[c], "wq": wqr, "bq": bqr,
                "wo": asf(inp["L1_attn_W_out"]), "bo": asf(inp["L1_attn_b_out"]), "idn": idn}
               for c in range(8)]
    r4 = _run(build_l4, in_maps)
    X3 = [r["x3"] for r in r4]
    perm = l5_perm_idx()
    wband_all = np.zeros((2, 12, 3, 128, 32), np.float32)
    for bf, wts in ((0, asf(inp["L1_ffn_up_weights"])), (1, asf(inp["L1_ffn_down_weights"]))):
        wb = l5_band_vectors(wts)
        wband_all[bf] = wb.reshape(12, 3, 32, 128).transpose(0, 1, 3, 2)
    upb_in = asf(inp["L1_ffn_up_bias"])[perm].reshape(32, 128).T.copy()
    wcon_perm = np.ascontiguousarray(asf(inp["L1_ffn_W_con"])[perm, :])
    in_maps = [{"x3": X3[c], "wexp": asf(inp["L1_ffn_W_exp"]), "bexp": asf(inp["L1_ffn_b_exp"]),
                "wband": wband_all, "upb": upb_in, "wcon": wcon_perm,
                "bcon": asf(inp["L1_ffn_b_con"]), "dbias": asf(inp["L1_ffn_down_bias"]), "idn": idn}
               for c in range(8)]
    r5 = _run(build_l5, in_maps)
    out = np.concatenate([r5[c]["x4"] for c in range(8)], axis=1)
    return out.astype(np.float32)


# version 04
# speedup vs baseline: 1.0815x; 1.0815x over previous
import contextlib
import numpy as np
import ml_dtypes
import concourse.bass as bass
import concourse.mybir as mybir
from concourse.bass_utils import run_bass_kernel_spmd

F32 = mybir.dt.float32; BF16 = mybir.dt.bfloat16
AF = mybir.ActivationFunctionType; ALU = mybir.AluOpType; AX = mybir.AxisListType
NB, NPOS, D = 32, 256, 1024
NTOK = NB * NPOS
GB = 4
NG = NB // GB
BSH = 20.0
LN_EPS = 1e-5

class Chain:
    def __init__(self, nc, name, stack):
        self.sem = stack.enter_context(nc.semaphore(name)); self.n = 0
    def bump(self, inst, k=1):
        inst.then_inc(self.sem, k); self.n += k; return inst
    def dma(self, inst):
        inst.then_inc(self.sem, 16); self.n += 16; return inst
    def wait(self, eng):
        eng.wait_ge(self.sem, self.n)


def build_l0attn(nc, stack):
    x = nc.dram_tensor("x", [NB, NPOS, D], F32, kind="ExternalInput").ap()
    wqkv = nc.dram_tensor("wqkv", [D, 3*D], F32, kind="ExternalInput").ap()
    bqkv = nc.dram_tensor("bqkv", [3*D], F32, kind="ExternalInput").ap()
    idn_in = nc.dram_tensor("idn", [128, 128], F32, kind="ExternalInput").ap()
    mask_in = nc.dram_tensor("mask", [4, 128, 512], F32, kind="ExternalInput").ap()
    y_out = nc.dram_tensor("y", [2, NB, 64, 2048], BF16, kind="ExternalOutput").ap()
    d_out = nc.dram_tensor("den", [2, NB, 2048], F32, kind="ExternalOutput").ap()
    E = stack.enter_context
    wq   = E(nc.sbuf_tensor("wq", [128, 8, 3072], BF16))
    bq   = E(nc.sbuf_tensor("bq", [1, 3072], BF16))
    idn  = E(nc.sbuf_tensor("idn_s", [128, 128], BF16))
    mask = E(nc.sbuf_tensor("mask_s", [128, 4, 512], BF16))
    onesr= E(nc.sbuf_tensor("onesr", [1, 512], BF16))
    xt   = E(nc.sbuf_tensor("xt", [128, GB, 1024], F32))
    stat = E(nc.sbuf_tensor("stat", [128, 20], F32))
    xbf  = E(nc.sbuf_tensor("xbf", [128, GB, 1024], BF16))
    xfm  = E(nc.sbuf_tensor("xfm", [128, 8, GB*128], BF16))
    qs_t = E(nc.sbuf_tensor("qs", [64, GB, 2048], BF16))
    ks_t = E(nc.sbuf_tensor("ks", [64, GB, 2048], BF16))
    vtm  = E(nc.sbuf_tensor("vtm", [128, GB, 1024], BF16))
    vsp  = E(nc.sbuf_tensor("vsp", [128, GB, 16, 65], BF16))
    atile= E(nc.sbuf_tensor("atile", [128, 16, 512], BF16))
    ybuf = E(nc.sbuf_tensor("ybuf", [64, 2048], BF16))
    dbuf = E(nc.sbuf_tensor("dbuf", [1, 2048], F32))
    cst  = E(nc.sbuf_tensor("cst", [128, 2], F32))
    ps_tp0 = E(nc.psum_tensor("ps_tp0", [128, 128], BF16))
    ps_tp1 = E(nc.psum_tensor("ps_tp1", [128, 128], BF16))
    ps_tp = [ps_tp0, ps_tp1]
    ps_mm = E(nc.psum_tensor("ps_mm", [128, 2, 512], F32))
    ps_s  = E(nc.psum_tensor("ps_s", [128, 2, 512], F32))
    ps_av = E(nc.psum_tensor("ps_av", [65, 512], F32))

    ch_w   = Chain(nc, "ch_w", stack)
    ch_x   = Chain(nc, "ch_x", stack)
    ch_v1  = Chain(nc, "ch_v1", stack)
    ch_s1  = Chain(nc, "ch_s1", stack)
    ch_v2  = Chain(nc, "ch_v2", stack)
    ch_s2  = Chain(nc, "ch_s2", stack)
    ch_v3  = Chain(nc, "ch_v3", stack)
    tp_p = [Chain(nc, f"tp_p{i}", stack) for i in range(2)]
    tp_c = [Chain(nc, f"tp_c{i}", stack) for i in range(2)]
    mm_p = [Chain(nc, f"mm_p{i}", stack) for i in range(2)]
    mm_c = [Chain(nc, f"mm_c{i}", stack) for i in range(2)]
    s_p  = [Chain(nc, f"s_p{i}", stack) for i in range(2)]
    s_c  = [Chain(nc, f"s_c{i}", stack) for i in range(2)]
    ch_fold = Chain(nc, "ch_fold", stack)
    ch_ae  = Chain(nc, "ch_ae", stack)
    ch_am  = Chain(nc, "ch_am", stack)
    ch_avd = Chain(nc, "ch_avd", stack)
    ch_ye  = Chain(nc, "ch_ye", stack)
    ch_yd  = Chain(nc, "ch_yd", stack)
    slot_need = [0, 0]

    for kc in range(8):
        ch_w.dma(nc.gpsimd.dma_start(out=wq[:, kc, :], in_=wqkv[kc*128:(kc+1)*128, :]))
    ch_w.dma(nc.gpsimd.dma_start(out=bq[:], in_=bqkv[None, :]))
    ch_w.dma(nc.gpsimd.dma_start(out=idn[:], in_=idn_in[:]))
    ch_w.dma(nc.gpsimd.dma_start(out=mask[:], in_=mask_in.transpose([1, 0, 2])))
    nc.vector.memset(onesr[:], 1.0)
    nc.vector.memset(cst[:, 0:1], 1e-5)
    nc.vector.memset(cst[:, 1:2], -20.0)
    ch_w.bump(nc.vector.memset(vsp[:, :, :, 64:65], 1.0))
    nc.all_engine_barrier()

    for hloc in range(2):
      for g in range(NG):
        for b in range(GB):
            ch_x.dma(nc.sync.dma_start(out=xt[:, b, :], in_=x[g*GB+b, hloc*128:(hloc+1)*128, :]))
        ch_x.wait(nc.vector)
        for b in range(GB):
            inst = nc.vector.reduce_sum(stat[:, 0+b:1+b], xt[:, b, :], axis=AX.X)
        ch_v1.bump(inst)
        ch_v1.wait(nc.vector)
        ch_v1.bump(nc.vector.tensor_scalar_mul(stat[:, 8:12], stat[:, 0:4], 1.0/1024.0))
        ch_x.wait(nc.scalar)
        for b in range(GB):
            inst = nc.scalar.activation(xbf[:, b, :], xt[:, b, :], AF.Square, accum_out=stat[:, 4+b:5+b])
        ch_s1.bump(inst)
        ch_v1.wait(nc.vector); ch_s1.wait(nc.vector)
        nc.vector.tensor_tensor(stat[:, 16:20], stat[:, 8:12], stat[:, 8:12], op=ALU.mult)
        ch_v2.bump(nc.vector.engine_nop())
        ch_v2.wait(nc.vector)
        ch_v2.bump(nc.vector.tensor_scalar(stat[:, 12:16], stat[:, 4:8], 1.0/1024.0, None, op0=ALU.mult))
        ch_v2.wait(nc.vector)
        ch_v2.bump(nc.vector.tensor_tensor(stat[:, 12:16], stat[:, 12:16], stat[:, 16:20], op=ALU.subtract))
        ch_v2.wait(nc.scalar)
        ch_s2.bump(nc.scalar.activation(stat[:, 16:20], stat[:, 12:16], AF.Ln, bias=cst[:, 0:1]))
        ch_s2.wait(nc.scalar)
        ch_s2.bump(nc.scalar.activation(stat[:, 16:20], stat[:, 16:20], AF.Exp, scale=-0.5))
        ch_s2.wait(nc.vector)
        for b in range(GB):
            inst = nc.vector.tensor_scalar(xbf[:, b, :], xt[:, b, :], stat[:, 8+b:9+b], stat[:, 16+b:17+b],
                                           op0=ALU.subtract, op1=ALU.mult)
        ch_v3.bump(inst)
        ch_v3.wait(nc.tensor)
        ti = 0
        for kc in range(8):
            for b in range(GB):
                sl = ti % 2
                tp_c[sl].wait(nc.tensor)
                tp_p[sl].bump(nc.tensor.transpose(ps_tp[sl][:], xbf[:, b, kc*128:(kc+1)*128], idn[:]))
                tp_p[sl].wait(nc.vector)
                tp_c[sl].bump(nc.vector.tensor_copy(xfm[:, kc, b*128:(b+1)*128], ps_tp[sl][:]))
                ti += 1
        ch_w.wait(nc.tensor)
        for mc in range(16):
            sl = mc % 2
            mm_c[sl].wait(nc.tensor)
            for kc in range(8):
                nc.tensor.matmul(ps_mm[:, sl, :], wq[:, kc, mc*128:(mc+1)*128], xfm[:, kc, :],
                                 start=(kc == 0), stop=False)
            mm_p[sl].bump(nc.tensor.matmul(ps_mm[:, sl, :], bq[:, mc*128:(mc+1)*128], onesr[:],
                                           start=False, stop=True))
            mm_p[sl].wait(nc.vector)
            third, mloc = (0, mc) if mc < 8 else (1, mc-8)
            dst = qs_t if third == 0 else ks_t
            scale = 0.125 if third == 0 else 1.0
            for jsub in range(2):
                j = 2*mloc + jsub
                for b in range(GB):
                    inst = nc.vector.tensor_scalar(dst[:, b, j:2048:16], ps_mm[jsub*64:(jsub+1)*64, sl, b*128:(b+1)*128],
                                                   scale, None, op0=ALU.mult)
            mm_c[sl].bump(inst)
        for b in range(GB):
            for nn in range(2):
                sl = nn % 2
                mm_c[sl].wait(nc.tensor)
                for kc in range(8):
                    nc.tensor.matmul(ps_mm[:, sl, :], xfm[:, kc, b*128:(b+1)*128], wq[:, kc, 2048+nn*512:2048+(nn+1)*512],
                                     start=(kc == 0), stop=False)
                mm_p[sl].bump(nc.tensor.matmul(ps_mm[:, sl, :], onesr[:, 0:128], bq[:, 2048+nn*512:2048+(nn+1)*512],
                                               start=False, stop=True))
                mm_p[sl].wait(nc.vector)
                mm_c[sl].bump(nc.vector.tensor_copy(vtm[:, b, nn*512:(nn+1)*512], ps_mm[:, sl, :]))
        mm_c[0].wait(nc.gpsimd); mm_c[1].wait(nc.gpsimd)
        for b in range(GB):
            for kt in range(16):
                ch_fold.dma(nc.gpsimd.dma_start(out=vsp[:, b, kt, 0:64],
                            in_=vtm[kt*8:(kt+1)*8, b, :].rearrange("p (j dd) -> p j dd", dd=64)))
        for b in range(GB):
            ch_fold.wait(nc.tensor)
            for qsup in range(4):
                nkt = 4*qsup + 4
                for kt in range(nkt):
                    sl = kt % 2
                    nc.tensor.wait_ge(ch_ae.sem, slot_need[sl])
                    inst = nc.tensor.matmul(ps_s[:, sl, :], ks_t[:, b, kt*128:(kt+1)*128],
                                            qs_t[:, b, qsup*512:(qsup+1)*512], start=True, stop=True)
                    s_p[sl].bump(inst)
                    s_p[sl].wait(nc.scalar)
                    inst = nc.scalar.activation(atile[:, kt, :], ps_s[:, sl, :], AF.Exp, bias=cst[0:128, 1:2])
                    ch_ae.bump(inst)
                    slot_need[sl] = ch_ae.n
                ch_ae.wait(nc.vector)
                for mi in range(4):
                    kt = 4*qsup + mi
                    inst = nc.vector.tensor_tensor(atile[:, kt, :], atile[:, kt, :], mask[:, mi, :], op=ALU.mult)
                ch_am.bump(inst)
                ch_ae.wait(nc.tensor); ch_am.wait(nc.tensor); ch_ye.wait(nc.tensor)
                for kt in range(nkt):
                    inst = nc.tensor.matmul(ps_av[:, :], vsp[:, b, kt, :], atile[:, kt, :],
                                            start=(kt == 0), stop=(kt == nkt-1))
                ch_avd.bump(inst)
                ch_avd.wait(nc.vector)
                nc.vector.tensor_copy(ybuf[:, qsup*512:(qsup+1)*512], ps_av[0:64, :])
                inst = nc.vector.tensor_copy(dbuf[:, qsup*512:(qsup+1)*512], ps_av[64:65, :])
                ch_ye.bump(inst)
            ch_ye.wait(nc.sync)
            ch_yd.dma(nc.sync.dma_start(out=y_out[hloc, g*GB+b, :, :], in_=ybuf[:]))
            ch_yd.dma(nc.sync.dma_start(out=d_out[hloc, g*GB+b, :], in_=dbuf[:]))
            ch_yd.wait(nc.sync)
        nc.all_engine_barrier()
    ch_yd.wait(nc.sync)
    return nc


def host_constants():
    idn = np.eye(128, dtype=np.float32)
    masks = np.zeros((4, 128, 512), np.float32)
    for mi, o in enumerate((0, 128, 256, 384)):
        for k in range(128):
            masks[mi, k, :] = (np.arange(512) >= k + o)
    return idn, masks


def emit_ln(nc, xt, xbf, stat, cst_eps, ch):
    ch.bump(nc.vector.reduce_sum(stat[:, 0:1], xt[:], axis=AX.X))
    ch.bump(nc.scalar.activation(xbf[:], xt[:], AF.Square, accum_out=stat[:, 1:2]))
    ch.wait(nc.vector)
    ch.bump(nc.vector.tensor_scalar_mul(stat[:, 2:3], stat[:, 0:1], 1.0/1024.0))
    ch.wait(nc.vector)
    nc.vector.tensor_tensor(stat[:, 3:4], stat[:, 2:3], stat[:, 2:3], op=ALU.mult)
    ch.bump(nc.vector.tensor_scalar(stat[:, 4:5], stat[:, 1:2], 1.0/1024.0, None, op0=ALU.mult))
    ch.wait(nc.vector)
    ch.bump(nc.vector.tensor_tensor(stat[:, 4:5], stat[:, 4:5], stat[:, 3:4], op=ALU.subtract))
    ch.wait(nc.scalar)
    ch.bump(nc.scalar.activation(stat[:, 5:6], stat[:, 4:5], AF.Ln, bias=cst_eps))
    ch.wait(nc.scalar)
    ch.bump(nc.scalar.activation(stat[:, 5:6], stat[:, 5:6], AF.Exp, scale=-0.5))
    ch.wait(nc.vector)
    ch.bump(nc.vector.tensor_scalar(xbf[:], xt[:], stat[:, 2:3], stat[:, 5:6],
                                    op0=ALU.subtract, op1=ALU.mult))


def emit_transposes(nc, srcs, dsts, idn, ps_tp, tp_p, tp_c, state):
    for src, dst in zip(srcs, dsts):
        sl = state[0] % 2
        tp_c[sl].wait(nc.tensor)
        tp_p[sl].bump(nc.tensor.transpose(ps_tp[sl][:], src, idn[:]))
        tp_p[sl].wait(nc.vector)
        tp_c[sl].bump(nc.vector.tensor_copy(dst, ps_tp[sl][:]))
        state[0] += 1


def build_l2(nc, stack):
    yg  = nc.dram_tensor("yg", [16, NB, 64, NPOS], BF16, kind="ExternalInput").ap()
    den = nc.dram_tensor("den", [16, NB, NPOS], F32, kind="ExternalInput").ap()
    x   = nc.dram_tensor("x", [NB, NPOS, D], F32, kind="ExternalInput").ap()
    wp  = nc.dram_tensor("wp", [D, D], F32, kind="ExternalInput").ap()
    bp  = nc.dram_tensor("bp", [D], F32, kind="ExternalInput").ap()
    sel = nc.dram_tensor("sel", [8, 16, 128], F32, kind="ExternalInput").ap()
    x1  = nc.dram_tensor("x1", [NB, NPOS, D], F32, kind="ExternalOutput").ap()
    E = stack.enter_context
    wps  = E(nc.sbuf_tensor("wps", [128, 8, 1024], BF16))
    bps  = E(nc.sbuf_tensor("bps", [1, 1024], BF16))
    sels = E(nc.sbuf_tensor("sels", [16, 8, 128], F32))
    onesr= E(nc.sbuf_tensor("onesr", [1, 128], BF16))
    yfm  = E(nc.sbuf_tensor("yfm", [128, 8, NPOS], BF16))
    dent = E(nc.sbuf_tensor("dent", [16, NPOS], F32))
    recip= E(nc.sbuf_tensor("recip", [16, NPOS], F32))
    ynrm = E(nc.sbuf_tensor("ynrm", [128, 8, NPOS], BF16))
    xres = E(nc.sbuf_tensor("xres", [128, 2, 1024], F32))
    x1t  = E(nc.sbuf_tensor("x1t", [128, 2, 1024], F32))
    ps_bc = E(nc.psum_tensor("ps_bc", [128, 512], F32))
    ps_o0 = E(nc.psum_tensor("ps_o0", [128, 512], F32))
    ps_o1 = E(nc.psum_tensor("ps_o1", [128, 512], F32))
    ch_w = Chain(nc, "l2w", stack); ch_y = Chain(nc, "l2y", stack)
    ch_b = Chain(nc, "l2b", stack); ch_n = Chain(nc, "l2n", stack)
    ch_mm = Chain(nc, "l2mm", stack); ch_ev = Chain(nc, "l2ev", stack)
    ch_od = Chain(nc, "l2od", stack)
    for kc in range(8):
        ch_w.dma(nc.gpsimd.dma_start(out=wps[:, kc, :], in_=wp[kc*128:(kc+1)*128, :]))
    ch_w.dma(nc.gpsimd.dma_start(out=bps[:], in_=bp[None, :]))
    ch_w.dma(nc.sync.dma_start(out=sels[:], in_=sel.transpose([1, 0, 2])))
    ch_w.bump(nc.vector.memset(onesr[:], 1.0))
    ch_w.wait(nc.sync); ch_w.wait(nc.vector); ch_w.wait(nc.tensor); ch_w.wait(nc.scalar); ch_w.wait(nc.gpsimd)
    nc.all_engine_barrier()
    for b in range(NB):
        for j in range(8):
            ch_y.dma(nc.sync.dma_start(out=yfm[:, j, :], in_=yg[2*j:2*j+2, b, :, :]))
        ch_y.dma(nc.sync.dma_start(out=dent[:], in_=den[:, b, :]))
        for pt in range(2):
            ch_y.dma(nc.sync.dma_start(out=xres[:, pt, :], in_=x[b, pt*128:(pt+1)*128, :]))
        ch_y.wait(nc.vector)
        ch_n.bump(nc.vector.reciprocal(recip[:], dent[:]))
        ch_n.wait(nc.tensor)
        for j in range(8):
            ch_b.bump(nc.tensor.matmul(ps_bc[:, 0:NPOS], sels[:, j, :], recip[:], start=True, stop=True))
            ch_b.wait(nc.vector)
            ch_n.bump(nc.vector.tensor_tensor(ynrm[:, j, :], yfm[:, j, :], ps_bc[:, 0:NPOS], op=ALU.mult))
            ch_n.wait(nc.tensor)
        for pt in range(2):
            ch_ev.wait(nc.tensor)
            for nn, ps in ((0, ps_o0), (1, ps_o1)):
                for kc in range(8):
                    nc.tensor.matmul(ps[:], ynrm[:, kc, pt*128:(pt+1)*128], wps[:, kc, nn*512:(nn+1)*512],
                                     start=(kc == 0), stop=False)
                ch_mm.bump(nc.tensor.matmul(ps[:], onesr[:], bps[:, nn*512:(nn+1)*512], start=False, stop=True))
            ch_mm.wait(nc.vector)
            nc.vector.tensor_add(x1t[:, pt, 0:512], ps_o0[:], xres[:, pt, 0:512])
            ch_ev.bump(nc.vector.tensor_add(x1t[:, pt, 512:1024], ps_o1[:], xres[:, pt, 512:1024]))
            ch_ev.wait(nc.sync)
            ch_od.dma(nc.sync.dma_start(out=x1[b, pt*128:(pt+1)*128, :], in_=x1t[:, pt, :]))
        nc.all_engine_barrier()
        ch_od.wait(nc.sync)
    nc.all_engine_barrier()
    return nc


def build_l3(nc, stack):
    x1  = nc.dram_tensor("x1", [NB, NPOS, D], F32, kind="ExternalInput").ap()
    w3  = nc.dram_tensor("w3", [D, 3*D], F32, kind="ExternalInput").ap()
    u3  = nc.dram_tensor("u3", [D, 3*D], F32, kind="ExternalInput").ap()
    b3  = nc.dram_tensor("b3", [3*D], F32, kind="ExternalInput").ap()
    wo  = nc.dram_tensor("wo", [D, D], F32, kind="ExternalInput").ap()
    uo  = nc.dram_tensor("uo", [D, D], F32, kind="ExternalInput").ap()
    bo  = nc.dram_tensor("bo", [D], F32, kind="ExternalInput").ap()
    inc = nc.dram_tensor("inc", [D], F32, kind="ExternalInput").ap()
    idn_in = nc.dram_tensor("idn", [128, 128], F32, kind="ExternalInput").ap()
    x2  = nc.dram_tensor("x2", [NB, NPOS, D], F32, kind="ExternalOutput").ap()
    E = stack.enter_context
    w3s = E(nc.sbuf_tensor("w3s", [128, 8, 3072], BF16))
    u3s = E(nc.sbuf_tensor("u3s", [128, 8, 3072], BF16))
    b3s = E(nc.sbuf_tensor("b3s", [1, 3072], BF16))
    wos = E(nc.sbuf_tensor("wos", [128, 8, 1024], BF16))
    uos = E(nc.sbuf_tensor("uos", [128, 8, 1024], BF16))
    bos = E(nc.sbuf_tensor("bos", [1, 1024], BF16))
    incr= E(nc.sbuf_tensor("incr", [1, 1024], F32))
    ones32 = E(nc.sbuf_tensor("ones32", [1, 128], F32))
    idn = E(nc.sbuf_tensor("idn3", [128, 128], BF16))
    onesr=E(nc.sbuf_tensor("onesr3", [1, 128], BF16))
    cst = E(nc.sbuf_tensor("cst3", [128, 1], F32))
    xt  = E(nc.sbuf_tensor("xt3", [128, 2, 1024], F32))
    stat= E(nc.sbuf_tensor("stat3", [128, 12], F32))
    xbf = E(nc.sbuf_tensor("xbf3", [128, 2, 1024], BF16))
    xsf = E(nc.sbuf_tensor("xsf", [128, 8, 256], BF16))
    c_tm= E(nc.sbuf_tensor("c_tm", [128, 2, 1024], F32))
    cfm = E(nc.sbuf_tensor("cfm", [128, 8, 256], BF16))
    gat = E(nc.sbuf_tensor("gat", [128, 2, 3072], F32))
    cuo = E(nc.sbuf_tensor("cuo", [128, 2, 1024], F32))
    ht  = E(nc.sbuf_tensor("ht", [128, 1024], F32))
    ps_g  = E(nc.psum_tensor("ps_g", [128, 3072], F32))
    ps_t0 = E(nc.psum_tensor("ps_t0", [128, 128], BF16))
    ps_t1 = E(nc.psum_tensor("ps_t1", [128, 128], BF16))
    ps_tp = [ps_t0, ps_t1]
    ch_w = Chain(nc, "l3w", stack); ch_x = Chain(nc, "l3x", stack)
    ch_ln = Chain(nc, "l3ln", stack)
    tp_p = [Chain(nc, f"l3tpp{i}", stack) for i in range(2)]
    tp_c = [Chain(nc, f"l3tpc{i}", stack) for i in range(2)]
    ch_cc = Chain(nc, "l3cc", stack)
    ch_mm = Chain(nc, "l3mm", stack)
    ch_ge = Chain(nc, "l3ge", stack)
    ch_cu = Chain(nc, "l3cu", stack)
    ch_h  = Chain(nc, "l3h", stack)
    ch_ta = Chain(nc, "l3ta", stack)
    ch_od = Chain(nc, "l3od", stack)
    tstate = [0]

    for kc in range(8):
        ch_w.dma(nc.gpsimd.dma_start(out=w3s[:, kc, :], in_=w3[kc*128:(kc+1)*128, :]))
        ch_w.dma(nc.gpsimd.dma_start(out=u3s[:, kc, :], in_=u3[kc*128:(kc+1)*128, :]))
        ch_w.dma(nc.gpsimd.dma_start(out=wos[:, kc, :], in_=wo[kc*128:(kc+1)*128, :]))
        ch_w.dma(nc.gpsimd.dma_start(out=uos[:, kc, :], in_=uo[kc*128:(kc+1)*128, :]))
    ch_w.dma(nc.gpsimd.dma_start(out=b3s[:], in_=b3[None, :]))
    ch_w.dma(nc.gpsimd.dma_start(out=bos[:], in_=bo[None, :]))
    ch_w.dma(nc.gpsimd.dma_start(out=idn[:], in_=idn_in[:]))
    ch_w.dma(nc.sync.dma_start(out=incr[:], in_=inc[None, :]))
    nc.vector.memset(onesr[:], 1.0)
    nc.vector.memset(ones32[:], 1.0)
    ch_w.bump(nc.vector.memset(cst[:], LN_EPS))
    ch_w.wait(nc.sync); ch_w.wait(nc.vector); ch_w.wait(nc.tensor); ch_w.wait(nc.scalar); ch_w.wait(nc.gpsimd)
    nc.all_engine_barrier()

    for nn in range(2):
        ch_cc.bump(nc.tensor.matmul(ps_g[:, nn*512:(nn+1)*512], ones32[:], incr[:, nn*512:(nn+1)*512],
                                    start=True, stop=True))
    ch_cc.wait(nc.vector)
    nc.vector.tensor_copy(c_tm[:, 0, :], ps_g[:, 0:1024])
    nc.vector.tensor_copy(c_tm[:, 1, :], ps_g[:, 0:1024])
    nc.all_engine_barrier()

    def ln_tr(t, dest):
        for pt in range(2):
            ch_x.dma(nc.sync.dma_start(out=xt[:, pt, :], in_=x1[t, pt*128:(pt+1)*128, :]))
        ch_x.wait(nc.vector); ch_x.wait(nc.scalar)
        for pt in range(2):
            emit_ln(nc, xt[:, pt, :], xbf[:, pt, :], stat[:, 6*pt:6*pt+6], cst[:], ch_ln)
        ch_ln.wait(nc.tensor)
        emit_transposes(nc,
            [xbf[:, pt, kc*128:(kc+1)*128] for pt in range(2) for kc in range(8)],
            [dest[:, kc, pt*128:(pt+1)*128] for pt in range(2) for kc in range(8)],
            idn, ps_tp, tp_p, tp_c, tstate)

    for t in range(NB):
        ln_tr(t, xsf)
        ch_cc.bump(nc.vector.tensor_copy(xbf[:], c_tm[:]))
        ch_cc.wait(nc.tensor)
        emit_transposes(nc,
            [xbf[:, pt, kc*128:(kc+1)*128] for pt in range(2) for kc in range(8)],
            [cfm[:, kc, pt*128:(pt+1)*128] for pt in range(2) for kc in range(8)],
            idn, ps_tp, tp_p, tp_c, tstate)
        tp_c[0].wait(nc.tensor); tp_c[1].wait(nc.tensor)
        for pt in range(2):
            ch_ge.wait(nc.tensor)
            for n6 in range(6):
                for kc in range(8):
                    nc.tensor.matmul(ps_g[:, n6*512:(n6+1)*512], cfm[:, kc, pt*128:(pt+1)*128],
                                     u3s[:, kc, n6*512:(n6+1)*512], start=(kc == 0), stop=False)
                for kc in range(8):
                    nc.tensor.matmul(ps_g[:, n6*512:(n6+1)*512], xsf[:, kc, pt*128:(pt+1)*128],
                                     w3s[:, kc, n6*512:(n6+1)*512], start=False, stop=False)
                ch_mm.bump(nc.tensor.matmul(ps_g[:, n6*512:(n6+1)*512], onesr[:],
                                            b3s[:, n6*512:(n6+1)*512], start=False, stop=True))
            ch_mm.wait(nc.scalar)
            nc.scalar.activation(gat[:, pt, 0:2048], ps_g[:, 0:2048], AF.Sigmoid)
            ch_ge.bump(nc.scalar.activation(gat[:, pt, 2048:3072], ps_g[:, 2048:3072], AF.Tanh))
        ch_ge.wait(nc.vector)
        ch_cu.bump(nc.vector.tensor_tensor(gat[:, :, 1024:2048], gat[:, :, 1024:2048], gat[:, :, 2048:3072], op=ALU.mult))
        ch_cu.bump(nc.vector.tensor_tensor(c_tm[:], c_tm[:], gat[:, :, 0:1024], op=ALU.mult))
        ch_cu.wait(nc.vector)
        ch_cu.bump(nc.vector.tensor_tensor(c_tm[:], c_tm[:], gat[:, :, 1024:2048], op=ALU.add))
        nc.all_engine_barrier()
    ch_cc.bump(nc.vector.tensor_copy(xbf[:], c_tm[:]))
    ch_cc.wait(nc.tensor)
    emit_transposes(nc,
        [xbf[:, pt, kc*128:(kc+1)*128] for pt in range(2) for kc in range(8)],
        [cfm[:, kc, pt*128:(pt+1)*128] for pt in range(2) for kc in range(8)],
        idn, ps_tp, tp_p, tp_c, tstate)
    tp_c[0].wait(nc.tensor); tp_c[1].wait(nc.tensor)
    for pt in range(2):
        for nn in range(2):
            for kc in range(8):
                nc.tensor.matmul(ps_g[:, pt*1024+nn*512:pt*1024+(nn+1)*512], cfm[:, kc, pt*128:(pt+1)*128],
                                 uos[:, kc, nn*512:(nn+1)*512], start=(kc == 0), stop=False)
            ch_mm.bump(nc.tensor.matmul(ps_g[:, pt*1024+nn*512:pt*1024+(nn+1)*512], onesr[:],
                                        bos[:, nn*512:(nn+1)*512], start=False, stop=True))
    ch_mm.wait(nc.vector)
    nc.vector.tensor_copy(cuo[:, 0, :], ps_g[:, 0:1024])
    nc.vector.tensor_copy(cuo[:, 1, :], ps_g[:, 1024:2048])
    nc.all_engine_barrier()
    for t in range(NB):
        ln_tr(t, xsf)
        tp_c[0].wait(nc.tensor); tp_c[1].wait(nc.tensor)
        for pt in range(2):
            for nn in range(2):
                for kc in range(8):
                    inst = nc.tensor.matmul(ps_g[:, nn*512:(nn+1)*512], xsf[:, kc, pt*128:(pt+1)*128],
                                     wos[:, kc, nn*512:(nn+1)*512], start=(kc == 0), stop=(kc == 7))
                ch_mm.bump(inst)
            ch_mm.wait(nc.vector)
            ch_h.bump(nc.vector.tensor_add(ht[:], ps_g[:, 0:1024], cuo[:, pt, :]))
            ch_h.wait(nc.scalar)
            x2t = gat[:, 0, 0:1024]
            ch_ta.bump(nc.scalar.activation(x2t, ht[:], AF.Tanh))
            ch_ta.wait(nc.vector)
            ch_od.bump(nc.vector.tensor_add(x2t, x2t, xt[:, pt, :]))
            ch_od.wait(nc.sync)
            ch_od.dma(nc.sync.dma_start(out=x2[t, pt*128:(pt+1)*128, :], in_=x2t))
            nc.all_engine_barrier()
        ch_od.wait(nc.sync)
    nc.all_engine_barrier()
    return nc


def host_sel():
    sel = np.zeros((8, 16, 128), np.float32)
    for j in range(8):
        for h2 in range(2):
            sel[j, 2*j+h2, h2*64:(h2+1)*64] = 1.0
    return sel


def build_l4(nc, stack):
    x2  = nc.dram_tensor("x2", [NB, NPOS, D], F32, kind="ExternalInput").ap()
    wq  = nc.dram_tensor("wq", [D, 3*D], F32, kind="ExternalInput").ap()
    bq  = nc.dram_tensor("bq", [3*D], F32, kind="ExternalInput").ap()
    wo  = nc.dram_tensor("wo", [D, D], F32, kind="ExternalInput").ap()
    bo  = nc.dram_tensor("bo", [D], F32, kind="ExternalInput").ap()
    idn_in = nc.dram_tensor("idn", [128, 128], F32, kind="ExternalInput").ap()
    x3  = nc.dram_tensor("x3", [NB, NPOS, D], F32, kind="ExternalOutput").ap()
    E = stack.enter_context
    wqs = E(nc.sbuf_tensor("wqs4", [128, 8, 3072], BF16))
    bqs = E(nc.sbuf_tensor("bqs4", [1, 3072], BF16))
    wos = E(nc.sbuf_tensor("wos4", [128, 8, 1024], BF16))
    bos = E(nc.sbuf_tensor("bos4", [1, 1024], BF16))
    idn = E(nc.sbuf_tensor("idn4", [128, 128], BF16))
    onesr=E(nc.sbuf_tensor("onesr4", [1, 128], BF16))
    cst = E(nc.sbuf_tensor("cst4", [128, 1], F32))
    xt  = E(nc.sbuf_tensor("xt4", [128, 1024], F32))
    stat= E(nc.sbuf_tensor("stat4", [128, 6], F32))
    xbf = E(nc.sbuf_tensor("xbf4", [128, 1024], BF16))
    xfm = E(nc.sbuf_tensor("xfm4", [128, 8, 128], BF16))
    qkv = E(nc.sbuf_tensor("qkv4", [128, 3072], BF16))
    stm = E(nc.sbuf_tensor("stm4", [128, 256], F32))
    etm = E(nc.sbuf_tensor("etm4", [128, 256], F32))
    mx  = E(nc.sbuf_tensor("mx4", [128, 16], F32))
    sm  = E(nc.sbuf_tensor("sm4", [128, 16], F32))
    atm = E(nc.sbuf_tensor("atm4", [128, 256], BF16))
    prd = [E(nc.sbuf_tensor(f"prd4_{i}", [128, 1024], BF16)) for i in range(2)]
    ctxb= E(nc.sbuf_tensor("ctxb4", [128, 1024], BF16))
    ctxf= E(nc.sbuf_tensor("ctxf4", [128, 8, 128], BF16))
    x3t = E(nc.sbuf_tensor("x3t4", [128, 1024], F32))
    ps_q0 = E(nc.psum_tensor("ps_q0", [128, 512], F32))
    ps_q1 = E(nc.psum_tensor("ps_q1", [128, 512], F32))
    ps_c  = E(nc.psum_tensor("ps_c4", [128, 1024], F32))
    ps_t0 = E(nc.psum_tensor("ps_t04", [128, 128], BF16))
    ps_t1 = E(nc.psum_tensor("ps_t14", [128, 128], BF16))
    ps_tp = [ps_t0, ps_t1]
    ps_qs = [ps_q0, ps_q1]
    ch_w = Chain(nc, "l4w", stack); ch_x = Chain(nc, "l4x", stack); ch_ln = Chain(nc, "l4ln", stack)
    tp_p = [Chain(nc, f"l4tpp{i}", stack) for i in range(2)]
    tp_c = [Chain(nc, f"l4tpc{i}", stack) for i in range(2)]
    ch_mm = Chain(nc, "l4mm", stack); ch_qe = Chain(nc, "l4qe", stack)
    ch_p  = Chain(nc, "l4p", stack)
    ch_r  = Chain(nc, "l4r", stack)
    ch_sx = Chain(nc, "l4sx", stack)
    ch_cm = Chain(nc, "l4cm", stack)
    ch_ce = Chain(nc, "l4ce", stack)
    ch_od = Chain(nc, "l4od", stack)
    tstate = [0]
    for kc in range(8):
        ch_w.dma(nc.gpsimd.dma_start(out=wqs[:, kc, :], in_=wq[kc*128:(kc+1)*128, :]))
        ch_w.dma(nc.gpsimd.dma_start(out=wos[:, kc, :], in_=wo[kc*128:(kc+1)*128, :]))
    ch_w.dma(nc.gpsimd.dma_start(out=bqs[:], in_=bq[None, :]))
    ch_w.dma(nc.gpsimd.dma_start(out=bos[:], in_=bo[None, :]))
    ch_w.dma(nc.gpsimd.dma_start(out=idn[:], in_=idn_in[:]))
    nc.vector.memset(onesr[:], 1.0)
    ch_w.bump(nc.vector.memset(cst[:], LN_EPS))
    ch_w.wait(nc.sync); ch_w.wait(nc.vector); ch_w.wait(nc.tensor); ch_w.wait(nc.scalar); ch_w.wait(nc.gpsimd)
    nc.all_engine_barrier()

    def emit_tp(srcs, dsts):
        for src, dst in zip(srcs, dsts):
            sl = tstate[0] % 2
            tp_c[sl].wait(nc.tensor)
            tp_p[sl].bump(nc.tensor.transpose(ps_tp[sl][:], src, idn[:]))
            tp_p[sl].wait(nc.vector)
            tp_c[sl].bump(nc.vector.tensor_copy(dst, ps_tp[sl][:]))
            tstate[0] += 1

    for ti in range(NTOK // 128):
        b, pt = ti // 2, ti % 2
        ch_x.dma(nc.sync.dma_start(out=xt[:], in_=x2[b, pt*128:(pt+1)*128, :]))
        ch_x.wait(nc.vector); ch_x.wait(nc.scalar)
        emit_ln(nc, xt[:], xbf[:], stat[:], cst[:], ch_ln)
        ch_ln.wait(nc.tensor)
        emit_tp([xbf[:, kc*128:(kc+1)*128] for kc in range(8)],
                [xfm[:, kc, :] for kc in range(8)])
        tp_c[0].wait(nc.tensor); tp_c[1].wait(nc.tensor)
        for n6 in range(6):
            sl = n6 % 2
            ch_qe.wait(nc.tensor) if n6 >= 2 else None
            for kc in range(8):
                nc.tensor.matmul(ps_qs[sl][:], xfm[:, kc, :], wqs[:, kc, n6*512:(n6+1)*512],
                                 start=(kc == 0), stop=False)
            ch_mm.bump(nc.tensor.matmul(ps_qs[sl][:], onesr[:], bqs[:, n6*512:(n6+1)*512],
                                        start=False, stop=True))
            ch_mm.wait(nc.vector)
            if n6 < 2:
                ch_qe.bump(nc.vector.tensor_scalar(qkv[:, n6*512:(n6+1)*512], ps_qs[sl][:], 0.125, None, op0=ALU.mult))
            else:
                ch_qe.bump(nc.vector.tensor_copy(qkv[:, n6*512:(n6+1)*512], ps_qs[sl][:]))
        q = qkv[:, 0:1024]; k = qkv[:, 1024:2048]; v = qkv[:, 2048:3072]
        stm3 = stm.rearrange("p (h d) -> p h d", d=16)
        ch_qe.wait(nc.vector)
        for dl in range(16):
            sl = dl % 2
            ch_r.wait(nc.vector)
            w1 = (16 - dl) * 64
            nc.vector.tensor_tensor(prd[sl][:, 0:w1], q[:, 0:w1], k[:, dl*64:1024], op=ALU.mult)
            if dl > 0:
                inst = nc.vector.tensor_tensor(prd[sl][:, w1:1024], q[:, w1:1024], k[:, 0:dl*64], op=ALU.mult)
            else:
                inst = nc.vector.engine_nop()
            ch_p.bump(inst)
            ch_p.wait(nc.vector)
            ch_r.bump(nc.vector.tensor_reduce(stm3[:, :, dl:dl+1],
                        prd[sl][:].rearrange("p (h d) -> p h d", d=64), axis=AX.X, op=ALU.add))
        ch_r.wait(nc.vector)
        ch_sx.bump(nc.vector.reduce_max(mx[:, :, None], stm3[:, :, :], axis=AX.X))
        ch_sx.wait(nc.vector)
        ch_sx.bump(nc.vector.tensor_tensor(etm.rearrange("p (h d) -> p h d", d=16), stm3[:, :, :],
                                           mx[:, :, None].broadcast_to([128, 16, 16]), op=ALU.subtract))
        ch_sx.wait(nc.scalar)
        ch_sx.bump(nc.scalar.activation(etm[:], etm[:], AF.Exp))
        ch_sx.wait(nc.vector)
        ch_sx.bump(nc.vector.reduce_sum(sm[:, :, None], etm.rearrange("p (h d) -> p h d", d=16), axis=AX.X))
        ch_sx.wait(nc.vector)
        ch_sx.bump(nc.vector.reciprocal(sm[:], sm[:]))
        ch_sx.wait(nc.vector)
        ch_sx.bump(nc.vector.tensor_tensor(atm.rearrange("p (h d) -> p h d", d=16),
                                           etm.rearrange("p (h d) -> p h d", d=16),
                                           sm[:, :, None].broadcast_to([128, 16, 16]), op=ALU.mult))
        atm3 = atm.rearrange("p (h d) -> p h d", d=16)
        ch_sx.wait(nc.vector)
        for dl in range(16):
            sl = dl % 2
            ch_cm.wait(nc.vector)
            w1 = (16 - dl) * 64
            p3a = prd[sl][:, 0:w1].rearrange("p (h d) -> p h d", d=64)
            nc.vector.tensor_tensor(p3a, v[:, dl*64:1024].rearrange("p (h d) -> p h d", d=64),
                                    atm3[:, 0:(16-dl), dl:dl+1].broadcast_to([128, 16-dl, 64]), op=ALU.mult)
            if dl > 0:
                p3b = prd[sl][:, w1:1024].rearrange("p (h d) -> p h d", d=64)
                inst = nc.vector.tensor_tensor(p3b, v[:, 0:dl*64].rearrange("p (h d) -> p h d", d=64),
                                        atm3[:, (16-dl):16, dl:dl+1].broadcast_to([128, dl, 64]), op=ALU.mult)
            else:
                inst = nc.vector.engine_nop()
            ch_p.bump(inst)
            ch_p.wait(nc.tensor)
            for nn in range(2):
                inst = nc.tensor.matmul(ps_c[:, nn*512:(nn+1)*512], idn[:], prd[sl][:, nn*512:(nn+1)*512],
                                        start=(dl == 0), stop=(dl == 15))
            ch_cm.bump(inst)
        ch_cm.wait(nc.vector)
        ch_ce.bump(nc.vector.tensor_copy(ctxb[:], ps_c[:]))
        ch_ce.wait(nc.tensor)
        emit_tp([ctxb[:, kc*128:(kc+1)*128] for kc in range(8)],
                [ctxf[:, kc, :] for kc in range(8)])
        tp_c[0].wait(nc.tensor); tp_c[1].wait(nc.tensor)
        for nn in range(2):
            sl = nn
            for kc in range(8):
                nc.tensor.matmul(ps_qs[sl][:], ctxf[:, kc, :], wos[:, kc, nn*512:(nn+1)*512],
                                 start=(kc == 0), stop=False)
            ch_mm.bump(nc.tensor.matmul(ps_qs[sl][:], onesr[:], bos[:, nn*512:(nn+1)*512],
                                        start=False, stop=True))
        ch_mm.wait(nc.vector)
        nc.vector.tensor_add(x3t[:, 0:512], ps_q0[:], xt[:, 0:512])
        ch_od.bump(nc.vector.tensor_add(x3t[:, 512:1024], ps_q1[:], xt[:, 512:1024]))
        ch_od.wait(nc.sync)
        ch_od.dma(nc.sync.dma_start(out=x3[b, pt*128:(pt+1)*128, :], in_=x3t[:]))
        nc.all_engine_barrier()
    ch_od.wait(nc.sync)
    nc.all_engine_barrier()
    return nc


def host_l4_weights(Wqkv, bqkv):
    idx = []
    for third in range(3):
        for h in range(16):
            idx.extend(range(h*192 + third*64, h*192 + third*64 + 64))
    idx = np.array(idx)
    return np.ascontiguousarray(Wqkv[:, idx]), np.ascontiguousarray(bqkv[idx])


def l5_band_vectors(weights):
    wband = np.zeros((12, 3, 4096), np.float32)
    for slot in range(12):
        s = slot
        is_g = slot in (6, 11)
        for fp in range(4096):
            if slot < 7:
                f = fp
            else:
                j = fp & 31; u = fp >> 5
                f = j*128 + u
            S_ref = 1 << s
            q = f >> (s+1); r = (f >> s) & 1; u_ref = f & (S_ref - 1)
            widx = q * S_ref + u_ref
            w = weights[s, widx]
            wband[slot, 0, fp] = w[0, 0] if r == 0 else w[1, 1]
            if is_g:
                wband[slot, 1, fp] = w[1, 0] if r == 0 else 0.0
                wband[slot, 2, fp] = w[0, 1] if r == 1 else 0.0
            else:
                wband[slot, 1, fp] = w[0, 1] if r == 0 else 0.0
                wband[slot, 2, fp] = w[1, 0] if r == 1 else 0.0
    return wband

def l5_band_stride(slot):
    return 1 << slot if slot < 7 else 1 << (slot - 7)

def l5_make_mats(wband, slot):
    S = l5_band_stride(slot)
    mats = np.zeros((32, 128, 128), np.float32)
    for c in range(32):
        for p in range(128):
            fp = c*128 + p
            mats[c, p, p] = wband[slot, 0, fp]
            if p + S < 128:
                mats[c, p, p+S] = wband[slot, 1, fp]
            if p - S >= 0:
                mats[c, p, p-S] = wband[slot, 2, fp]
    return mats

def l5_compose_host(wband, lo):
    slots = range(0, 7) if lo else range(7, 12)
    slots = list(slots)
    P = l5_make_mats(wband, slots[-1])
    for s in reversed(slots[:-1]):
        H = l5_make_mats(wband, s)
        P = np.einsum('cpm,cpn->cmn', H, P)
    return P

def l5_perm_idx():
    fp = np.arange(4096)
    j = fp & 31; u = fp >> 5
    return j*128 + u


def build_l5(nc, stack):
    x3  = nc.dram_tensor("x3", [NB, NPOS, D], F32, kind="ExternalInput").ap()
    wexp= nc.dram_tensor("wexp", [D, 4096], F32, kind="ExternalInput").ap()
    bexp= nc.dram_tensor("bexp", [4096], F32, kind="ExternalInput").ap()
    wband=nc.dram_tensor("wband", [2, 12, 3, 128, 32], F32, kind="ExternalInput").ap()
    upb = nc.dram_tensor("upb", [128, 32], F32, kind="ExternalInput").ap()
    wcon= nc.dram_tensor("wcon", [4096, D], F32, kind="ExternalInput").ap()
    bcon= nc.dram_tensor("bcon", [D], F32, kind="ExternalInput").ap()
    dbias=nc.dram_tensor("dbias", [D], F32, kind="ExternalInput").ap()
    idn_in = nc.dram_tensor("idn", [128, 128], F32, kind="ExternalInput").ap()
    x4  = nc.dram_tensor("x4", [NB, NPOS, D], F32, kind="ExternalOutput").ap()
    hfin = nc.dram_tensor("hfin", [128, 32, NTOK], BF16, kind="Internal").ap()
    E = stack.enter_context
    idn = E(nc.sbuf_tensor("idn5", [128, 128], BF16))
    ones5=E(nc.sbuf_tensor("ones5", [1, 512], BF16))
    cst = E(nc.sbuf_tensor("cst5", [128, 1], F32))
    upbs= E(nc.sbuf_tensor("upbs", [128, 32], F32))
    Lmat= E(nc.sbuf_tensor("Lmat", [128, 4, 32, 128], BF16))
    h1  = E(nc.sbuf_tensor("h1", [128, 32, 512], BF16))
    hp  = E(nc.sbuf_tensor("hp", [128, 32, 512], BF16))
    xt  = E(nc.sbuf_tensor("xt5", [128, 1024], F32))
    stat= E(nc.sbuf_tensor("stat5", [128, 6], F32))
    xbf = E(nc.sbuf_tensor("xbf5", [128, 1024], BF16))
    xfm = E(nc.sbuf_tensor("xfm5", [128, 8, 512], BF16))
    h0c = E(nc.sbuf_tensor("h0c", [128, 2, 512], BF16))
    wv  = E(nc.sbuf_tensor("wv5", [128, 3, 32], BF16))
    bexps=E(nc.sbuf_tensor("bexps", [1, 4096], BF16))
    ps_a = E(nc.psum_tensor("ps_a5", [128, 512], F32))
    ps_b = E(nc.psum_tensor("ps_b5", [128, 512], F32))
    ps_t0= E(nc.psum_tensor("ps_t05", [128, 128], BF16))
    ps_t1= E(nc.psum_tensor("ps_t15", [128, 128], BF16))
    ps_c0= E(nc.psum_tensor("ps_c05", [128, 128], F32))
    ps_c1= E(nc.psum_tensor("ps_c15", [128, 128], F32))
    ps_tp = [ps_t0, ps_t1]
    ps_cc = [ps_c0, ps_c1]
    ch_w = Chain(nc, "l5w", stack); ch_x = Chain(nc, "l5x", stack); ch_ln = Chain(nc, "l5ln", stack)
    tp_p = [Chain(nc, f"l5tpp{i}", stack) for i in range(2)]
    tp_c = [Chain(nc, f"l5tpc{i}", stack) for i in range(2)]
    ch_bv = Chain(nc, "l5bv", stack)
    ch_bs = Chain(nc, "l5bs", stack)
    ch_ba = Chain(nc, "l5ba", stack)
    cc_p = [Chain(nc, f"l5ccp{i}", stack) for i in range(2)]
    cc_c = [Chain(nc, f"l5ccc{i}", stack) for i in range(2)]
    ch_mm = Chain(nc, "l5mm", stack)
    ch_ev = Chain(nc, "l5ev", stack)
    ch_pm = Chain(nc, "l5pm", stack)
    ch_hf = Chain(nc, "l5hf", stack)
    ch_od = Chain(nc, "l5od", stack)
    tstate = [0]

    ch_w.dma(nc.gpsimd.dma_start(out=idn[:], in_=idn_in[:]))
    ch_w.dma(nc.sync.dma_start(out=upbs[:], in_=upb[:]))
    ch_w.dma(nc.gpsimd.dma_start(out=bexps[:], in_=bexp[None, :]))
    nc.vector.memset(ones5[:], 1.0)
    ch_w.bump(nc.vector.memset(cst[:], LN_EPS))
    ch_w.wait(nc.sync); ch_w.wait(nc.vector); ch_w.wait(nc.tensor); ch_w.wait(nc.scalar); ch_w.wait(nc.gpsimd)
    nc.all_engine_barrier()

    bufA = h1[:, :, 0:128]; bufB = h1[:, :, 128:256]
    Hbuf = h1[:, :, 256:384]; t0 = h1[:, :, 384:512]; t1 = hp[:, :, 0:128]
    def construct(bf, slot, dest):
        S = l5_band_stride(slot)
        for band in range(3):
            ch_bv.dma(nc.gpsimd.dma_start(out=wv[:, band, :], in_=wband[bf, slot, band, :, :]))
        ch_bv.wait(nc.gpsimd); ch_ba.wait(nc.gpsimd)
        for band, (dst, off) in enumerate(((t0, 0), (t1, S), (dest, -S))):
            ch_bs.bump(nc.gpsimd.affine_select(out=dst, in_=wv[:, band, :, None].broadcast_to([128, 32, 128]),
                        compare_op=ALU.is_equal, fill=0.0, base=-off,
                        pattern=[[0, 32], [1, 128]], channel_multiplier=-1))
        ch_bs.wait(nc.vector)
        nc.vector.tensor_tensor(dest, dest, t0, op=ALU.add)
        ch_ba.bump(nc.vector.tensor_tensor(dest, dest, t1, op=ALU.add))
    for bf in range(2):
        for lo in range(2):
            slots = list(range(0, 7)) if lo == 0 else list(range(7, 12))
            construct(bf, slots[-1], bufA)
            cur, nxt = bufA, bufB
            for s in reversed(slots[:-1]):
                construct(bf, s, Hbuf)
                ch_ba.wait(nc.tensor)
                for c in range(32):
                    sl = c % 2
                    cc_c[sl].wait(nc.tensor)
                    cc_p[sl].bump(nc.tensor.matmul(ps_cc[sl][:], Hbuf[:, c, :], cur[:, c, :],
                                                   start=True, stop=True))
                    cc_p[sl].wait(nc.vector)
                    cc_c[sl].bump(nc.vector.tensor_copy(nxt[:, c, :], ps_cc[sl][:]))
                cur, nxt = nxt, cur
                cc_c[0].wait(nc.gpsimd); cc_c[1].wait(nc.gpsimd)
            cc_c[0].wait(nc.vector); cc_c[1].wait(nc.vector)
            ch_ba.wait(nc.vector)
            seti = bf*2 + lo
            ch_ba.bump(nc.vector.tensor_copy(Lmat[:, seti, :, :], cur))
            ch_ba.wait(nc.tensor); ch_ba.wait(nc.gpsimd)
    nc.all_engine_barrier()

    with contextlib.ExitStack() as s1:
        wexps = s1.enter_context(nc.sbuf_tensor("wexps", [128, 8, 4096], BF16))
        for kc in range(8):
            ch_w.dma(nc.gpsimd.dma_start(out=wexps[:, kc, :], in_=wexp[kc*128:(kc+1)*128, :]))
        ch_w.wait(nc.tensor); ch_w.wait(nc.vector)
        nc.all_engine_barrier()
        def emit_tp(srcs, dsts):
            for src, dst in zip(srcs, dsts):
                sl = tstate[0] % 2
                tp_c[sl].wait(nc.tensor)
                tp_p[sl].bump(nc.tensor.transpose(ps_tp[sl][:], src, idn[:]))
                tp_p[sl].wait(nc.vector)
                tp_c[sl].bump(nc.vector.tensor_copy(dst, ps_tp[sl][:]))
                tstate[0] += 1
        NT = NTOK // 512
        for t in range(NT):
            for st4 in range(4):
                gtok = t*512 + st4*128
                b, po = gtok // NPOS, gtok % NPOS
                ch_ln.wait(nc.sync)
                ch_x.dma(nc.sync.dma_start(out=xt[:], in_=x3[b, po:po+128, :]))
                ch_x.wait(nc.vector); ch_x.wait(nc.scalar)
                emit_ln(nc, xt[:], xbf[:], stat[:], cst[:], ch_ln)
                ch_ln.wait(nc.tensor)
                emit_tp([xbf[:, kc*128:(kc+1)*128] for kc in range(8)],
                        [xfm[:, kc, st4*128:(st4+1)*128] for kc in range(8)])
            tp_c[0].wait(nc.tensor); tp_c[1].wait(nc.tensor)
            ch_pm.wait(nc.tensor)
            evn = {"a0": 0, "a1": 0, "b0": 0, "b1": 0}
            psl = [ps_a, ps_b, ps_c0, ps_c1]
            for c in range(32):
                sl = c % 2
                psA = ps_a if sl == 0 else ps_b
                nc.tensor.wait_ge(ch_ev.sem, evn["a%d" % sl])
                for kc in range(8):
                    nc.tensor.matmul(psA[:], wexps[:, kc, c*128:(c+1)*128], xfm[:, kc, :],
                                     start=(kc == 0), stop=False)
                ch_mm.bump(nc.tensor.matmul(psA[:], bexps[:, c*128:(c+1)*128], ones5[:],
                                            start=False, stop=True))
                nc.vector.wait_ge(ch_mm.sem, ch_mm.n)
                ch_ev.bump(nc.vector.tensor_copy(h0c[:, sl, :], psA[:]))
                evn["a%d" % sl] = ch_ev.n
                nc.tensor.wait_ge(ch_ev.sem, ch_ev.n)
                ch_mm.bump(nc.tensor.matmul(psA[:], Lmat[:, 0, c, :], h0c[:, sl, :],
                                            start=True, stop=True))
                nc.vector.wait_ge(ch_mm.sem, ch_mm.n)
                ch_ev.bump(nc.vector.tensor_copy(h1[:, c, :], psA[:]))
                evn["a%d" % sl] = ch_ev.n
            ch_ev.wait(nc.gpsimd); ch_ev.wait(nc.sync)
            for g in range(32):
                eng = nc.gpsimd if g % 2 == 0 else nc.sync
                ch_pm.dma(eng.dma_start(out=hp[:, g, :], in_=h1[g*4:(g+1)*4, :, :]))
            ch_pm.wait(nc.tensor)
            evb = [0, 0]
            for g in range(32):
                sl = g % 2
                psA = ps_b if sl == 0 else ps_a
                nc.tensor.wait_ge(ch_ev.sem, evb[sl])
                ch_mm.bump(nc.tensor.matmul(psA[:], Lmat[:, 1, g, :], hp[:, g, :], start=True, stop=True))
                nc.scalar.wait_ge(ch_mm.sem, ch_mm.n)
                ch_ev.bump(nc.scalar.activation(h1[:, g, :], psA[:], AF.Gelu, bias=upbs[:, g:g+1]))
                evb[sl] = ch_ev.n
            ch_ev.wait(nc.gpsimd); ch_ev.wait(nc.sync)
            for g in range(32):
                eng = nc.gpsimd if g % 2 == 0 else nc.sync
                ch_pm.dma(eng.dma_start(out=hp[g*4:(g+1)*4, :, :], in_=h1[:, g, :]))
            ch_pm.wait(nc.tensor)
            evb = [0, 0]
            for c in range(32):
                sl = c % 2
                psA = ps_b if sl == 0 else ps_a
                nc.tensor.wait_ge(ch_ev.sem, evb[sl])
                ch_mm.bump(nc.tensor.matmul(psA[:], Lmat[:, 2, c, :], hp[:, c, :], start=True, stop=True))
                nc.vector.wait_ge(ch_mm.sem, ch_mm.n)
                ch_ev.bump(nc.vector.tensor_copy(h1[:, c, :], psA[:]))
                evb[sl] = ch_ev.n
            ch_ev.wait(nc.gpsimd); ch_ev.wait(nc.sync)
            for g in range(32):
                eng = nc.gpsimd if g % 2 == 0 else nc.sync
                ch_pm.dma(eng.dma_start(out=hp[:, g, :], in_=h1[g*4:(g+1)*4, :, :]))
            ch_pm.wait(nc.tensor)
            evb = [0, 0]
            for g in range(32):
                sl = g % 2
                psA = ps_b if sl == 0 else ps_a
                nc.tensor.wait_ge(ch_ev.sem, evb[sl])
                ch_mm.bump(nc.tensor.matmul(psA[:], Lmat[:, 3, g, :], hp[:, g, :], start=True, stop=True))
                nc.vector.wait_ge(ch_mm.sem, ch_mm.n)
                ch_ev.bump(nc.vector.tensor_copy(h1[:, g, :], psA[:]))
                evb[sl] = ch_ev.n
            ch_ev.wait(nc.sync)
            ch_hf.dma(nc.sync.dma_start(out=hfin[:, :, t*512:(t+1)*512], in_=h1[:]))
            ch_hf.wait(nc.sync)
        nc.all_engine_barrier()
    with contextlib.ExitStack() as s2:
        wcons = s2.enter_context(nc.sbuf_tensor("wcons", [128, 32, 1024], BF16))
        bcons = s2.enter_context(nc.sbuf_tensor("bcons", [1, 1024], BF16))
        dbs   = s2.enter_context(nc.sbuf_tensor("dbs", [1, 1024], BF16))
        x4t   = s2.enter_context(nc.sbuf_tensor("x4t", [128, 1024], F32))
        for g in range(32):
            ch_w.dma(nc.gpsimd.dma_start(out=wcons[:, g, :], in_=wcon[g*128:(g+1)*128, :]))
        ch_w.dma(nc.gpsimd.dma_start(out=bcons[:], in_=bcon[None, :]))
        ch_w.dma(nc.gpsimd.dma_start(out=dbs[:], in_=dbias[None, :]))
        ch_w.wait(nc.tensor); ch_w.wait(nc.vector); ch_w.wait(nc.sync)
        nc.all_engine_barrier()
        NT = NTOK // 512
        for t in range(NT):
            ch_mm.wait(nc.sync)
            ch_x.dma(nc.sync.dma_start(out=h1[:], in_=hfin[:, :, t*512:(t+1)*512]))
            for st4 in range(4):
                gtok = t*512 + st4*128
                b, po = gtok // NPOS, gtok % NPOS
                ch_od.wait(nc.sync)
                ch_x.dma(nc.sync.dma_start(out=xt[:], in_=x3[b, po:po+128, :]))
                ch_x.wait(nc.tensor)
                ch_od.wait(nc.tensor)
                for nn, ps in ((0, ps_a), (1, ps_b)):
                    for g in range(32):
                        nc.tensor.matmul(ps[:], h1[:, g, st4*128:(st4+1)*128], wcons[:, g, nn*512:(nn+1)*512],
                                         start=(g == 0), stop=False)
                    nc.tensor.matmul(ps[:], ones5[:, 0:128], bcons[:, nn*512:(nn+1)*512], start=False, stop=False)
                    ch_mm.bump(nc.tensor.matmul(ps[:], ones5[:, 0:128], dbs[:, nn*512:(nn+1)*512], start=False, stop=True))
                ch_mm.wait(nc.vector)
                nc.vector.tensor_add(x4t[:, 0:512], ps_a[:], xt[:, 0:512])
                ch_od.bump(nc.vector.tensor_add(x4t[:, 512:1024], ps_b[:], xt[:, 512:1024]))
                ch_od.wait(nc.sync)
                ch_od.dma(nc.sync.dma_start(out=x4[b, po:po+128, :], in_=x4t[:]))
        ch_od.wait(nc.sync)
        nc.all_engine_barrier()
    return nc



def _run(build, in_maps):
    nc = bass.Bass("TRN2", target_bir_lowering=False)
    with contextlib.ExitStack() as stack:
        build(nc, stack)
    return run_bass_kernel_spmd(nc, in_maps, core_ids=list(range(8))).results

def kernel(**inp):
    f32 = np.float32
    asf = lambda a: np.ascontiguousarray(np.asarray(a, dtype=f32))
    x0 = asf(inp["L0_attn_x"])
    idn, masks = host_constants()
    in_maps = [{"x": np.ascontiguousarray(x0[:, c*256:(c+1)*256, :]),
                "wqkv": asf(inp["L0_attn_Wqkv"]), "bqkv": asf(inp["L0_attn_bqkv"]),
                "idn": idn, "mask": masks} for c in range(8)]
    r1 = _run(build_l0attn, in_maps)
    Y = [r["y"] for r in r1]; DEN = [r["den"] for r in r1]
    sel = host_sel()
    in_maps = []
    for c in range(8):
        yg = np.ascontiguousarray(np.stack(
            [Y[src][hl, :, :, c*256:(c+1)*256] for src in range(8) for hl in range(2)]))
        dg = np.ascontiguousarray(np.stack(
            [DEN[src][hl, :, c*256:(c+1)*256] for src in range(8) for hl in range(2)]))
        in_maps.append({"yg": yg, "den": dg, "x": np.ascontiguousarray(x0[:, c*256:(c+1)*256, :]),
                        "wp": asf(inp["L0_attn_Wproj"]), "bp": asf(inp["L0_attn_bproj"]), "sel": sel})
    r2 = _run(build_l2, in_maps)
    X1 = [r["x1"] for r in r2]
    w3 = np.ascontiguousarray(np.concatenate(
        [asf(inp["L0_ffn_wf"]).T, asf(inp["L0_ffn_wi"]).T, asf(inp["L0_ffn_wc"]).T], axis=1))
    u3 = np.ascontiguousarray(np.concatenate(
        [asf(inp["L0_ffn_uf"]).T, asf(inp["L0_ffn_ui"]).T, asf(inp["L0_ffn_uc"]).T], axis=1))
    b3 = np.concatenate([asf(inp["L0_ffn_bf"]), asf(inp["L0_ffn_bi"]), asf(inp["L0_ffn_bc"])])
    in_maps = [{"x1": X1[c], "w3": w3, "u3": u3, "b3": b3,
                "wo": np.ascontiguousarray(asf(inp["L0_ffn_wo"]).T),
                "uo": np.ascontiguousarray(asf(inp["L0_ffn_uo"]).T),
                "bo": asf(inp["L0_ffn_bo"]), "inc": asf(inp["L0_ffn_in_c"]), "idn": idn}
               for c in range(8)]
    r3 = _run(build_l3, in_maps)
    X2 = [r["x2"] for r in r3]
    wqr, bqr = host_l4_weights(asf(inp["L1_attn_W_qkv"]), asf(inp["L1_attn_b_qkv"]))
    in_maps = [{"x2": X2[c], "wq": wqr, "bq": bqr,
                "wo": asf(inp["L1_attn_W_out"]), "bo": asf(inp["L1_attn_b_out"]), "idn": idn}
               for c in range(8)]
    r4 = _run(build_l4, in_maps)
    X3 = [r["x3"] for r in r4]
    perm = l5_perm_idx()
    wband_all = np.zeros((2, 12, 3, 128, 32), np.float32)
    for bf, wts in ((0, asf(inp["L1_ffn_up_weights"])), (1, asf(inp["L1_ffn_down_weights"]))):
        wb = l5_band_vectors(wts)
        wband_all[bf] = wb.reshape(12, 3, 32, 128).transpose(0, 1, 3, 2)
    upb_in = asf(inp["L1_ffn_up_bias"])[perm].reshape(32, 128).T.copy()
    wcon_perm = np.ascontiguousarray(asf(inp["L1_ffn_W_con"])[perm, :])
    in_maps = [{"x3": X3[c], "wexp": asf(inp["L1_ffn_W_exp"]), "bexp": asf(inp["L1_ffn_b_exp"]),
                "wband": wband_all, "upb": upb_in, "wcon": wcon_perm,
                "bcon": asf(inp["L1_ffn_b_con"]), "dbias": asf(inp["L1_ffn_down_bias"]), "idn": idn}
               for c in range(8)]
    r5 = _run(build_l5, in_maps)
    out = np.concatenate([r5[c]["x4"] for c in range(8)], axis=1)
    return out.astype(np.float32)


# version 05
# speedup vs baseline: 1.1066x; 1.0232x over previous
import contextlib
import numpy as np
import ml_dtypes
import concourse.bass as bass
import concourse.mybir as mybir
from concourse.bass_utils import run_bass_kernel_spmd

F32 = mybir.dt.float32; BF16 = mybir.dt.bfloat16
AF = mybir.ActivationFunctionType; ALU = mybir.AluOpType; AX = mybir.AxisListType
NB, NPOS, D = 32, 256, 1024
NTOK = NB * NPOS
GB = 4
NG = NB // GB
BSH = 20.0
LN_EPS = 1e-5

class Chain:
    def __init__(self, nc, name, stack):
        self.sem = stack.enter_context(nc.semaphore(name)); self.n = 0
    def bump(self, inst, k=1):
        inst.then_inc(self.sem, k); self.n += k; return inst
    def dma(self, inst):
        inst.then_inc(self.sem, 16); self.n += 16; return inst
    def wait(self, eng):
        eng.wait_ge(self.sem, self.n)


def build_l0attn(nc, stack):
    x = nc.dram_tensor("x", [NB, NPOS, D], F32, kind="ExternalInput").ap()
    wqkv = nc.dram_tensor("wqkv", [D, 3*D], F32, kind="ExternalInput").ap()
    bqkv = nc.dram_tensor("bqkv", [3*D], F32, kind="ExternalInput").ap()
    idn_in = nc.dram_tensor("idn", [128, 128], F32, kind="ExternalInput").ap()
    mask_in = nc.dram_tensor("mask", [4, 128, 512], F32, kind="ExternalInput").ap()
    y_out = nc.dram_tensor("y", [2, NB, 64, 2048], BF16, kind="ExternalOutput").ap()
    d_out = nc.dram_tensor("den", [2, NB, 2048], F32, kind="ExternalOutput").ap()
    E = stack.enter_context
    wq   = E(nc.sbuf_tensor("wq", [128, 8, 3072], BF16))
    bq   = E(nc.sbuf_tensor("bq", [1, 3072], BF16))
    idn  = E(nc.sbuf_tensor("idn_s", [128, 128], BF16))
    mask = E(nc.sbuf_tensor("mask_s", [128, 4, 512], BF16))
    onesr= E(nc.sbuf_tensor("onesr", [1, 512], BF16))
    xt   = E(nc.sbuf_tensor("xt", [128, GB, 1024], F32))
    stat = E(nc.sbuf_tensor("stat", [128, 20], F32))
    xbf  = E(nc.sbuf_tensor("xbf", [128, GB, 1024], BF16))
    xfm  = E(nc.sbuf_tensor("xfm", [128, 8, GB*128], BF16))
    qs_t = E(nc.sbuf_tensor("qs", [64, GB, 2048], BF16))
    ks_t = E(nc.sbuf_tensor("ks", [64, GB, 2048], BF16))
    vtm  = E(nc.sbuf_tensor("vtm", [128, GB, 1024], BF16))
    vsp  = E(nc.sbuf_tensor("vsp", [128, GB, 16, 65], BF16))
    atile= E(nc.sbuf_tensor("atile", [128, 16, 512], BF16))
    ybuf = E(nc.sbuf_tensor("ybuf", [64, 2048], BF16))
    dbuf = E(nc.sbuf_tensor("dbuf", [1, 2048], F32))
    cst  = E(nc.sbuf_tensor("cst", [128, 2], F32))
    ps_tp0 = E(nc.psum_tensor("ps_tp0", [128, 128], BF16))
    ps_tp1 = E(nc.psum_tensor("ps_tp1", [128, 128], BF16))
    ps_tp = [ps_tp0, ps_tp1]
    ps_mm = E(nc.psum_tensor("ps_mm", [128, 2, 512], F32))
    ps_s  = E(nc.psum_tensor("ps_s", [128, 2, 512], F32))
    ps_av = E(nc.psum_tensor("ps_av", [65, 512], F32))

    ch_w   = Chain(nc, "ch_w", stack)
    ch_x   = Chain(nc, "ch_x", stack)
    ch_v1  = Chain(nc, "ch_v1", stack)
    ch_s1  = Chain(nc, "ch_s1", stack)
    ch_v2  = Chain(nc, "ch_v2", stack)
    ch_s2  = Chain(nc, "ch_s2", stack)
    ch_v3  = Chain(nc, "ch_v3", stack)
    tp_p = [Chain(nc, f"tp_p{i}", stack) for i in range(2)]
    tp_c = [Chain(nc, f"tp_c{i}", stack) for i in range(2)]
    mm_p = [Chain(nc, f"mm_p{i}", stack) for i in range(2)]
    mm_c = [Chain(nc, f"mm_c{i}", stack) for i in range(2)]
    s_p  = [Chain(nc, f"s_p{i}", stack) for i in range(2)]
    s_c  = [Chain(nc, f"s_c{i}", stack) for i in range(2)]
    ch_fold = Chain(nc, "ch_fold", stack)
    ch_ae  = Chain(nc, "ch_ae", stack)
    ch_am  = Chain(nc, "ch_am", stack)
    ch_avd = Chain(nc, "ch_avd", stack)
    ch_ye  = Chain(nc, "ch_ye", stack)
    ch_yd  = Chain(nc, "ch_yd", stack)
    slot_need = [0, 0]

    for kc in range(8):
        ch_w.dma(nc.gpsimd.dma_start(out=wq[:, kc, :], in_=wqkv[kc*128:(kc+1)*128, :]))
    ch_w.dma(nc.gpsimd.dma_start(out=bq[:], in_=bqkv[None, :]))
    ch_w.dma(nc.gpsimd.dma_start(out=idn[:], in_=idn_in[:]))
    ch_w.dma(nc.gpsimd.dma_start(out=mask[:], in_=mask_in.transpose([1, 0, 2])))
    nc.vector.memset(onesr[:], 1.0)
    nc.vector.memset(cst[:, 0:1], 1e-5)
    nc.vector.memset(cst[:, 1:2], -20.0)
    ch_w.bump(nc.vector.memset(vsp[:, :, :, 64:65], 1.0))
    nc.all_engine_barrier()

    for hloc in range(2):
      for g in range(NG):
        for b in range(GB):
            ch_x.dma(nc.sync.dma_start(out=xt[:, b, :], in_=x[g*GB+b, hloc*128:(hloc+1)*128, :]))
        ch_x.wait(nc.vector)
        for b in range(GB):
            inst = nc.vector.reduce_sum(stat[:, 0+b:1+b], xt[:, b, :], axis=AX.X)
        ch_v1.bump(inst)
        ch_v1.wait(nc.vector)
        ch_v1.bump(nc.vector.tensor_scalar_mul(stat[:, 8:12], stat[:, 0:4], 1.0/1024.0))
        ch_x.wait(nc.scalar)
        for b in range(GB):
            inst = nc.scalar.activation(xbf[:, b, :], xt[:, b, :], AF.Square, accum_out=stat[:, 4+b:5+b])
        ch_s1.bump(inst)
        ch_v1.wait(nc.vector); ch_s1.wait(nc.vector)
        nc.vector.tensor_tensor(stat[:, 16:20], stat[:, 8:12], stat[:, 8:12], op=ALU.mult)
        ch_v2.bump(nc.vector.engine_nop())
        ch_v2.wait(nc.vector)
        ch_v2.bump(nc.vector.tensor_scalar(stat[:, 12:16], stat[:, 4:8], 1.0/1024.0, None, op0=ALU.mult))
        ch_v2.wait(nc.vector)
        ch_v2.bump(nc.vector.tensor_tensor(stat[:, 12:16], stat[:, 12:16], stat[:, 16:20], op=ALU.subtract))
        ch_v2.wait(nc.scalar)
        ch_s2.bump(nc.scalar.activation(stat[:, 16:20], stat[:, 12:16], AF.Ln, bias=cst[:, 0:1]))
        ch_s2.wait(nc.scalar)
        ch_s2.bump(nc.scalar.activation(stat[:, 16:20], stat[:, 16:20], AF.Exp, scale=-0.5))
        ch_s2.wait(nc.vector)
        for b in range(GB):
            inst = nc.vector.tensor_scalar(xbf[:, b, :], xt[:, b, :], stat[:, 8+b:9+b], stat[:, 16+b:17+b],
                                           op0=ALU.subtract, op1=ALU.mult)
        ch_v3.bump(inst)
        ch_v3.wait(nc.tensor)
        ti = 0
        for kc in range(8):
            for b in range(GB):
                sl = ti % 2
                tp_c[sl].wait(nc.tensor)
                tp_p[sl].bump(nc.tensor.transpose(ps_tp[sl][:], xbf[:, b, kc*128:(kc+1)*128], idn[:]))
                tp_p[sl].wait(nc.vector)
                tp_c[sl].bump(nc.vector.tensor_copy(xfm[:, kc, b*128:(b+1)*128], ps_tp[sl][:]))
                ti += 1
        ch_w.wait(nc.tensor)
        for mc in range(16):
            sl = mc % 2
            mm_c[sl].wait(nc.tensor)
            for kc in range(8):
                nc.tensor.matmul(ps_mm[:, sl, :], wq[:, kc, mc*128:(mc+1)*128], xfm[:, kc, :],
                                 start=(kc == 0), stop=False)
            mm_p[sl].bump(nc.tensor.matmul(ps_mm[:, sl, :], bq[:, mc*128:(mc+1)*128], onesr[:],
                                           start=False, stop=True))
            mm_p[sl].wait(nc.vector)
            third, mloc = (0, mc) if mc < 8 else (1, mc-8)
            dst = qs_t if third == 0 else ks_t
            scale = 0.125 if third == 0 else 1.0
            for jsub in range(2):
                j = 2*mloc + jsub
                for b in range(GB):
                    inst = nc.vector.tensor_scalar(dst[:, b, j:2048:16], ps_mm[jsub*64:(jsub+1)*64, sl, b*128:(b+1)*128],
                                                   scale, None, op0=ALU.mult)
            mm_c[sl].bump(inst)
        for b in range(GB):
            for nn in range(2):
                sl = nn % 2
                mm_c[sl].wait(nc.tensor)
                for kc in range(8):
                    nc.tensor.matmul(ps_mm[:, sl, :], xfm[:, kc, b*128:(b+1)*128], wq[:, kc, 2048+nn*512:2048+(nn+1)*512],
                                     start=(kc == 0), stop=False)
                mm_p[sl].bump(nc.tensor.matmul(ps_mm[:, sl, :], onesr[:, 0:128], bq[:, 2048+nn*512:2048+(nn+1)*512],
                                               start=False, stop=True))
                mm_p[sl].wait(nc.vector)
                mm_c[sl].bump(nc.vector.tensor_copy(vtm[:, b, nn*512:(nn+1)*512], ps_mm[:, sl, :]))
        mm_c[0].wait(nc.gpsimd); mm_c[1].wait(nc.gpsimd)
        for b in range(GB):
            for kt in range(16):
                ch_fold.dma(nc.gpsimd.dma_start(out=vsp[:, b, kt, 0:64],
                            in_=vtm[kt*8:(kt+1)*8, b, :].rearrange("p (j dd) -> p j dd", dd=64)))
        for b in range(GB):
            ch_fold.wait(nc.tensor)
            for qsup in range(4):
                nkt = 4*qsup + 4
                for kt in range(nkt):
                    sl = kt % 2
                    nc.tensor.wait_ge(ch_ae.sem, slot_need[sl])
                    inst = nc.tensor.matmul(ps_s[:, sl, :], ks_t[:, b, kt*128:(kt+1)*128],
                                            qs_t[:, b, qsup*512:(qsup+1)*512], start=True, stop=True)
                    s_p[sl].bump(inst)
                    s_p[sl].wait(nc.scalar)
                    inst = nc.scalar.activation(atile[:, kt, :], ps_s[:, sl, :], AF.Exp, bias=cst[0:128, 1:2])
                    ch_ae.bump(inst)
                    slot_need[sl] = ch_ae.n
                ch_ae.wait(nc.vector)
                for mi in range(4):
                    kt = 4*qsup + mi
                    inst = nc.vector.tensor_tensor(atile[:, kt, :], atile[:, kt, :], mask[:, mi, :], op=ALU.mult)
                ch_am.bump(inst)
                ch_ae.wait(nc.tensor); ch_am.wait(nc.tensor); ch_ye.wait(nc.tensor)
                for kt in range(nkt):
                    inst = nc.tensor.matmul(ps_av[:, :], vsp[:, b, kt, :], atile[:, kt, :],
                                            start=(kt == 0), stop=(kt == nkt-1))
                ch_avd.bump(inst)
                ch_avd.wait(nc.vector)
                nc.vector.tensor_copy(ybuf[:, qsup*512:(qsup+1)*512], ps_av[0:64, :])
                inst = nc.vector.tensor_copy(dbuf[:, qsup*512:(qsup+1)*512], ps_av[64:65, :])
                ch_ye.bump(inst)
            ch_ye.wait(nc.sync)
            ch_yd.dma(nc.sync.dma_start(out=y_out[hloc, g*GB+b, :, :], in_=ybuf[:]))
            ch_yd.dma(nc.sync.dma_start(out=d_out[hloc, g*GB+b, :], in_=dbuf[:]))
            ch_yd.wait(nc.sync)
        nc.all_engine_barrier()
    ch_yd.wait(nc.sync)
    return nc


def host_constants():
    idn = np.eye(128, dtype=np.float32)
    masks = np.zeros((4, 128, 512), np.float32)
    for mi, o in enumerate((0, 128, 256, 384)):
        for k in range(128):
            masks[mi, k, :] = (np.arange(512) >= k + o)
    return idn, masks


def emit_ln(nc, xt, xbf, stat, cst_eps, ch):
    ch.bump(nc.vector.reduce_sum(stat[:, 0:1], xt[:], axis=AX.X))
    ch.bump(nc.scalar.activation(xbf[:], xt[:], AF.Square, accum_out=stat[:, 1:2]))
    ch.wait(nc.vector)
    ch.bump(nc.vector.tensor_scalar_mul(stat[:, 2:3], stat[:, 0:1], 1.0/1024.0))
    ch.wait(nc.vector)
    nc.vector.tensor_tensor(stat[:, 3:4], stat[:, 2:3], stat[:, 2:3], op=ALU.mult)
    ch.bump(nc.vector.tensor_scalar(stat[:, 4:5], stat[:, 1:2], 1.0/1024.0, None, op0=ALU.mult))
    ch.wait(nc.vector)
    ch.bump(nc.vector.tensor_tensor(stat[:, 4:5], stat[:, 4:5], stat[:, 3:4], op=ALU.subtract))
    ch.wait(nc.scalar)
    ch.bump(nc.scalar.activation(stat[:, 5:6], stat[:, 4:5], AF.Ln, bias=cst_eps))
    ch.wait(nc.scalar)
    ch.bump(nc.scalar.activation(stat[:, 5:6], stat[:, 5:6], AF.Exp, scale=-0.5))
    ch.wait(nc.vector)
    ch.bump(nc.vector.tensor_scalar(xbf[:], xt[:], stat[:, 2:3], stat[:, 5:6],
                                    op0=ALU.subtract, op1=ALU.mult))


def emit_transposes(nc, srcs, dsts, idn, ps_tp, tp_p, tp_c, state):
    for src, dst in zip(srcs, dsts):
        sl = state[0] % 2
        tp_c[sl].wait(nc.tensor)
        tp_p[sl].bump(nc.tensor.transpose(ps_tp[sl][:], src, idn[:]))
        tp_p[sl].wait(nc.vector)
        tp_c[sl].bump(nc.vector.tensor_copy(dst, ps_tp[sl][:]))
        state[0] += 1


def build_l2(nc, stack):
    yg  = nc.dram_tensor("yg", [16, NB, 64, NPOS], BF16, kind="ExternalInput").ap()
    den = nc.dram_tensor("den", [16, NB, NPOS], F32, kind="ExternalInput").ap()
    x   = nc.dram_tensor("x", [NB, NPOS, D], F32, kind="ExternalInput").ap()
    wp  = nc.dram_tensor("wp", [D, D], F32, kind="ExternalInput").ap()
    bp  = nc.dram_tensor("bp", [D], F32, kind="ExternalInput").ap()
    sel = nc.dram_tensor("sel", [8, 16, 128], F32, kind="ExternalInput").ap()
    x1  = nc.dram_tensor("x1", [NB, NPOS, D], F32, kind="ExternalOutput").ap()
    E = stack.enter_context
    wps  = E(nc.sbuf_tensor("wps", [128, 8, 1024], BF16))
    bps  = E(nc.sbuf_tensor("bps", [1, 1024], BF16))
    sels = E(nc.sbuf_tensor("sels", [16, 8, 128], F32))
    onesr= E(nc.sbuf_tensor("onesr", [1, 128], BF16))
    yfm  = E(nc.sbuf_tensor("yfm", [128, 8, NPOS], BF16))
    dent = E(nc.sbuf_tensor("dent", [16, NPOS], F32))
    recip= E(nc.sbuf_tensor("recip", [16, NPOS], F32))
    ynrm = E(nc.sbuf_tensor("ynrm", [128, 8, NPOS], BF16))
    xres = E(nc.sbuf_tensor("xres", [128, 2, 1024], F32))
    x1t  = E(nc.sbuf_tensor("x1t", [128, 2, 1024], F32))
    ps_bc = E(nc.psum_tensor("ps_bc", [128, 512], F32))
    ps_o0 = E(nc.psum_tensor("ps_o0", [128, 512], F32))
    ps_o1 = E(nc.psum_tensor("ps_o1", [128, 512], F32))
    ch_w = Chain(nc, "l2w", stack); ch_y = Chain(nc, "l2y", stack)
    ch_b = Chain(nc, "l2b", stack); ch_n = Chain(nc, "l2n", stack)
    ch_mm = Chain(nc, "l2mm", stack); ch_ev = Chain(nc, "l2ev", stack)
    ch_od = Chain(nc, "l2od", stack)
    for kc in range(8):
        ch_w.dma(nc.gpsimd.dma_start(out=wps[:, kc, :], in_=wp[kc*128:(kc+1)*128, :]))
    ch_w.dma(nc.gpsimd.dma_start(out=bps[:], in_=bp[None, :]))
    ch_w.dma(nc.sync.dma_start(out=sels[:], in_=sel.transpose([1, 0, 2])))
    ch_w.bump(nc.vector.memset(onesr[:], 1.0))
    ch_w.wait(nc.sync); ch_w.wait(nc.vector); ch_w.wait(nc.tensor); ch_w.wait(nc.scalar); ch_w.wait(nc.gpsimd)
    nc.all_engine_barrier()
    for b in range(NB):
        for j in range(8):
            ch_y.dma(nc.sync.dma_start(out=yfm[:, j, :], in_=yg[2*j:2*j+2, b, :, :]))
        ch_y.dma(nc.sync.dma_start(out=dent[:], in_=den[:, b, :]))
        for pt in range(2):
            ch_y.dma(nc.sync.dma_start(out=xres[:, pt, :], in_=x[b, pt*128:(pt+1)*128, :]))
        ch_y.wait(nc.vector)
        ch_n.bump(nc.vector.reciprocal(recip[:], dent[:]))
        ch_n.wait(nc.tensor)
        for j in range(8):
            ch_b.bump(nc.tensor.matmul(ps_bc[:, 0:NPOS], sels[:, j, :], recip[:], start=True, stop=True))
            ch_b.wait(nc.vector)
            ch_n.bump(nc.vector.tensor_tensor(ynrm[:, j, :], yfm[:, j, :], ps_bc[:, 0:NPOS], op=ALU.mult))
            ch_n.wait(nc.tensor)
        for pt in range(2):
            ch_ev.wait(nc.tensor)
            for nn, ps in ((0, ps_o0), (1, ps_o1)):
                for kc in range(8):
                    nc.tensor.matmul(ps[:], ynrm[:, kc, pt*128:(pt+1)*128], wps[:, kc, nn*512:(nn+1)*512],
                                     start=(kc == 0), stop=False)
                ch_mm.bump(nc.tensor.matmul(ps[:], onesr[:], bps[:, nn*512:(nn+1)*512], start=False, stop=True))
            ch_mm.wait(nc.vector)
            nc.vector.tensor_add(x1t[:, pt, 0:512], ps_o0[:], xres[:, pt, 0:512])
            ch_ev.bump(nc.vector.tensor_add(x1t[:, pt, 512:1024], ps_o1[:], xres[:, pt, 512:1024]))
            ch_ev.wait(nc.sync)
            ch_od.dma(nc.sync.dma_start(out=x1[b, pt*128:(pt+1)*128, :], in_=x1t[:, pt, :]))
        nc.all_engine_barrier()
        ch_od.wait(nc.sync)
    nc.all_engine_barrier()
    return nc


def build_l3(nc, stack):
    x1  = nc.dram_tensor("x1", [NB, NPOS, D], F32, kind="ExternalInput").ap()
    w3  = nc.dram_tensor("w3", [D, 3*D], F32, kind="ExternalInput").ap()
    u3  = nc.dram_tensor("u3", [D, 3*D], F32, kind="ExternalInput").ap()
    b3  = nc.dram_tensor("b3", [3*D], F32, kind="ExternalInput").ap()
    wo  = nc.dram_tensor("wo", [D, D], F32, kind="ExternalInput").ap()
    uo  = nc.dram_tensor("uo", [D, D], F32, kind="ExternalInput").ap()
    bo  = nc.dram_tensor("bo", [D], F32, kind="ExternalInput").ap()
    inc = nc.dram_tensor("inc", [D], F32, kind="ExternalInput").ap()
    idn_in = nc.dram_tensor("idn", [128, 128], F32, kind="ExternalInput").ap()
    x2  = nc.dram_tensor("x2", [NB, NPOS, D], F32, kind="ExternalOutput").ap()
    E = stack.enter_context
    w3s = E(nc.sbuf_tensor("w3s", [128, 8, 3072], BF16))
    u3s = E(nc.sbuf_tensor("u3s", [128, 8, 3072], BF16))
    b3s = E(nc.sbuf_tensor("b3s", [1, 3072], BF16))
    wos = E(nc.sbuf_tensor("wos", [128, 8, 1024], BF16))
    uos = E(nc.sbuf_tensor("uos", [128, 8, 1024], BF16))
    bos = E(nc.sbuf_tensor("bos", [1, 1024], BF16))
    incr= E(nc.sbuf_tensor("incr", [1, 1024], F32))
    ones32 = E(nc.sbuf_tensor("ones32", [1, 128], F32))
    idn = E(nc.sbuf_tensor("idn3", [128, 128], BF16))
    onesr=E(nc.sbuf_tensor("onesr3", [1, 128], BF16))
    cst = E(nc.sbuf_tensor("cst3", [128, 1], F32))
    xt  = E(nc.sbuf_tensor("xt3", [128, 2, 1024], F32))
    stat= E(nc.sbuf_tensor("stat3", [128, 12], F32))
    xbf = E(nc.sbuf_tensor("xbf3", [128, 2, 1024], BF16))
    xsf = E(nc.sbuf_tensor("xsf", [128, 8, 256], BF16))
    c_tm= E(nc.sbuf_tensor("c_tm", [128, 2, 1024], F32))
    cfm = E(nc.sbuf_tensor("cfm", [128, 8, 256], BF16))
    gat = E(nc.sbuf_tensor("gat", [128, 2, 3072], F32))
    cuo = E(nc.sbuf_tensor("cuo", [128, 2, 1024], F32))
    ht  = E(nc.sbuf_tensor("ht", [128, 1024], F32))
    ps_g  = E(nc.psum_tensor("ps_g", [128, 3072], F32))
    ps_t0 = E(nc.psum_tensor("ps_t0", [128, 128], BF16))
    ps_t1 = E(nc.psum_tensor("ps_t1", [128, 128], BF16))
    ps_tp = [ps_t0, ps_t1]
    ch_w = Chain(nc, "l3w", stack); ch_x = Chain(nc, "l3x", stack)
    ch_ln = Chain(nc, "l3ln", stack)
    tp_p = [Chain(nc, f"l3tpp{i}", stack) for i in range(2)]
    tp_c = [Chain(nc, f"l3tpc{i}", stack) for i in range(2)]
    ch_cc = Chain(nc, "l3cc", stack)
    ch_mm = Chain(nc, "l3mm", stack)
    ch_ge = Chain(nc, "l3ge", stack)
    ch_cu = Chain(nc, "l3cu", stack)
    ch_h  = Chain(nc, "l3h", stack)
    ch_ta = Chain(nc, "l3ta", stack)
    ch_od = Chain(nc, "l3od", stack)
    tstate = [0]

    for kc in range(8):
        ch_w.dma(nc.gpsimd.dma_start(out=w3s[:, kc, :], in_=w3[kc*128:(kc+1)*128, :]))
        ch_w.dma(nc.gpsimd.dma_start(out=u3s[:, kc, :], in_=u3[kc*128:(kc+1)*128, :]))
        ch_w.dma(nc.gpsimd.dma_start(out=wos[:, kc, :], in_=wo[kc*128:(kc+1)*128, :]))
        ch_w.dma(nc.gpsimd.dma_start(out=uos[:, kc, :], in_=uo[kc*128:(kc+1)*128, :]))
    ch_w.dma(nc.gpsimd.dma_start(out=b3s[:], in_=b3[None, :]))
    ch_w.dma(nc.gpsimd.dma_start(out=bos[:], in_=bo[None, :]))
    ch_w.dma(nc.gpsimd.dma_start(out=idn[:], in_=idn_in[:]))
    ch_w.dma(nc.sync.dma_start(out=incr[:], in_=inc[None, :]))
    nc.vector.memset(onesr[:], 1.0)
    nc.vector.memset(ones32[:], 1.0)
    ch_w.bump(nc.vector.memset(cst[:], LN_EPS))
    ch_w.wait(nc.sync); ch_w.wait(nc.vector); ch_w.wait(nc.tensor); ch_w.wait(nc.scalar); ch_w.wait(nc.gpsimd)
    nc.all_engine_barrier()

    for nn in range(2):
        ch_cc.bump(nc.tensor.matmul(ps_g[:, nn*512:(nn+1)*512], ones32[:], incr[:, nn*512:(nn+1)*512],
                                    start=True, stop=True))
    ch_cc.wait(nc.vector)
    nc.vector.tensor_copy(c_tm[:, 0, :], ps_g[:, 0:1024])
    nc.vector.tensor_copy(c_tm[:, 1, :], ps_g[:, 0:1024])
    nc.all_engine_barrier()

    def ln_tr(t, dest):
        for pt in range(2):
            ch_x.dma(nc.sync.dma_start(out=xt[:, pt, :], in_=x1[t, pt*128:(pt+1)*128, :]))
        ch_x.wait(nc.vector); ch_x.wait(nc.scalar)
        for pt in range(2):
            emit_ln(nc, xt[:, pt, :], xbf[:, pt, :], stat[:, 6*pt:6*pt+6], cst[:], ch_ln)
        ch_ln.wait(nc.tensor)
        emit_transposes(nc,
            [xbf[:, pt, kc*128:(kc+1)*128] for pt in range(2) for kc in range(8)],
            [dest[:, kc, pt*128:(pt+1)*128] for pt in range(2) for kc in range(8)],
            idn, ps_tp, tp_p, tp_c, tstate)

    for t in range(NB):
        ln_tr(t, xsf)
        ch_cc.bump(nc.vector.tensor_copy(xbf[:], c_tm[:]))
        ch_cc.wait(nc.tensor)
        emit_transposes(nc,
            [xbf[:, pt, kc*128:(kc+1)*128] for pt in range(2) for kc in range(8)],
            [cfm[:, kc, pt*128:(pt+1)*128] for pt in range(2) for kc in range(8)],
            idn, ps_tp, tp_p, tp_c, tstate)
        tp_c[0].wait(nc.tensor); tp_c[1].wait(nc.tensor)
        for pt in range(2):
            ch_ge.wait(nc.tensor)
            for n6 in range(6):
                for kc in range(8):
                    nc.tensor.matmul(ps_g[:, n6*512:(n6+1)*512], cfm[:, kc, pt*128:(pt+1)*128],
                                     u3s[:, kc, n6*512:(n6+1)*512], start=(kc == 0), stop=False)
                for kc in range(8):
                    nc.tensor.matmul(ps_g[:, n6*512:(n6+1)*512], xsf[:, kc, pt*128:(pt+1)*128],
                                     w3s[:, kc, n6*512:(n6+1)*512], start=False, stop=False)
                ch_mm.bump(nc.tensor.matmul(ps_g[:, n6*512:(n6+1)*512], onesr[:],
                                            b3s[:, n6*512:(n6+1)*512], start=False, stop=True))
            ch_mm.wait(nc.scalar)
            nc.scalar.activation(gat[:, pt, 0:2048], ps_g[:, 0:2048], AF.Sigmoid)
            ch_ge.bump(nc.scalar.activation(gat[:, pt, 2048:3072], ps_g[:, 2048:3072], AF.Tanh))
        ch_ge.wait(nc.vector)
        ch_cu.bump(nc.vector.tensor_tensor(gat[:, :, 1024:2048], gat[:, :, 1024:2048], gat[:, :, 2048:3072], op=ALU.mult))
        ch_cu.bump(nc.vector.tensor_tensor(c_tm[:], c_tm[:], gat[:, :, 0:1024], op=ALU.mult))
        ch_cu.wait(nc.vector)
        ch_cu.bump(nc.vector.tensor_tensor(c_tm[:], c_tm[:], gat[:, :, 1024:2048], op=ALU.add))
        nc.all_engine_barrier()
    ch_cc.bump(nc.vector.tensor_copy(xbf[:], c_tm[:]))
    ch_cc.wait(nc.tensor)
    emit_transposes(nc,
        [xbf[:, pt, kc*128:(kc+1)*128] for pt in range(2) for kc in range(8)],
        [cfm[:, kc, pt*128:(pt+1)*128] for pt in range(2) for kc in range(8)],
        idn, ps_tp, tp_p, tp_c, tstate)
    tp_c[0].wait(nc.tensor); tp_c[1].wait(nc.tensor)
    for pt in range(2):
        for nn in range(2):
            for kc in range(8):
                nc.tensor.matmul(ps_g[:, pt*1024+nn*512:pt*1024+(nn+1)*512], cfm[:, kc, pt*128:(pt+1)*128],
                                 uos[:, kc, nn*512:(nn+1)*512], start=(kc == 0), stop=False)
            ch_mm.bump(nc.tensor.matmul(ps_g[:, pt*1024+nn*512:pt*1024+(nn+1)*512], onesr[:],
                                        bos[:, nn*512:(nn+1)*512], start=False, stop=True))
    ch_mm.wait(nc.vector)
    nc.vector.tensor_copy(cuo[:, 0, :], ps_g[:, 0:1024])
    nc.vector.tensor_copy(cuo[:, 1, :], ps_g[:, 1024:2048])
    nc.all_engine_barrier()
    for t in range(NB):
        ln_tr(t, xsf)
        tp_c[0].wait(nc.tensor); tp_c[1].wait(nc.tensor)
        for pt in range(2):
            for nn in range(2):
                for kc in range(8):
                    inst = nc.tensor.matmul(ps_g[:, nn*512:(nn+1)*512], xsf[:, kc, pt*128:(pt+1)*128],
                                     wos[:, kc, nn*512:(nn+1)*512], start=(kc == 0), stop=(kc == 7))
                ch_mm.bump(inst)
            ch_mm.wait(nc.vector)
            ch_h.bump(nc.vector.tensor_add(ht[:], ps_g[:, 0:1024], cuo[:, pt, :]))
            ch_h.wait(nc.scalar)
            x2t = gat[:, 0, 0:1024]
            ch_ta.bump(nc.scalar.activation(x2t, ht[:], AF.Tanh))
            ch_ta.wait(nc.vector)
            ch_od.bump(nc.vector.tensor_add(x2t, x2t, xt[:, pt, :]))
            ch_od.wait(nc.sync)
            ch_od.dma(nc.sync.dma_start(out=x2[t, pt*128:(pt+1)*128, :], in_=x2t))
            nc.all_engine_barrier()
        ch_od.wait(nc.sync)
    nc.all_engine_barrier()
    return nc


def host_sel():
    sel = np.zeros((8, 16, 128), np.float32)
    for j in range(8):
        for h2 in range(2):
            sel[j, 2*j+h2, h2*64:(h2+1)*64] = 1.0
    return sel


def build_l4(nc, stack):
    x2  = nc.dram_tensor("x2", [NB, NPOS, D], F32, kind="ExternalInput").ap()
    wq  = nc.dram_tensor("wq", [D, 3*D], F32, kind="ExternalInput").ap()
    bq  = nc.dram_tensor("bq", [3*D], F32, kind="ExternalInput").ap()
    wo  = nc.dram_tensor("wo", [D, D], F32, kind="ExternalInput").ap()
    bo  = nc.dram_tensor("bo", [D], F32, kind="ExternalInput").ap()
    idn_in = nc.dram_tensor("idn", [128, 128], F32, kind="ExternalInput").ap()
    x3  = nc.dram_tensor("x3", [NB, NPOS, D], F32, kind="ExternalOutput").ap()
    E = stack.enter_context
    wqs = E(nc.sbuf_tensor("wqs4", [128, 8, 3072], BF16))
    bqs = E(nc.sbuf_tensor("bqs4", [1, 3072], BF16))
    wos = E(nc.sbuf_tensor("wos4", [128, 8, 1024], BF16))
    bos = E(nc.sbuf_tensor("bos4", [1, 1024], BF16))
    idn = E(nc.sbuf_tensor("idn4", [128, 128], BF16))
    onesr=E(nc.sbuf_tensor("onesr4", [1, 128], BF16))
    cst = E(nc.sbuf_tensor("cst4", [128, 1], F32))
    xt  = E(nc.sbuf_tensor("xt4", [128, 1024], F32))
    stat= E(nc.sbuf_tensor("stat4", [128, 6], F32))
    xbf = E(nc.sbuf_tensor("xbf4", [128, 1024], BF16))
    xfm = E(nc.sbuf_tensor("xfm4", [128, 8, 128], BF16))
    qkv = E(nc.sbuf_tensor("qkv4", [128, 3072], BF16))
    stm = E(nc.sbuf_tensor("stm4", [128, 256], F32))
    etm = E(nc.sbuf_tensor("etm4", [128, 256], F32))
    mx  = E(nc.sbuf_tensor("mx4", [128, 16], F32))
    sm  = E(nc.sbuf_tensor("sm4", [128, 16], F32))
    atm = E(nc.sbuf_tensor("atm4", [128, 256], BF16))
    prd = [E(nc.sbuf_tensor(f"prd4_{i}", [128, 1024], BF16)) for i in range(2)]
    ctxb= E(nc.sbuf_tensor("ctxb4", [128, 1024], BF16))
    ctxf= E(nc.sbuf_tensor("ctxf4", [128, 8, 128], BF16))
    x3t = E(nc.sbuf_tensor("x3t4", [128, 1024], F32))
    ps_q0 = E(nc.psum_tensor("ps_q0", [128, 512], F32))
    ps_q1 = E(nc.psum_tensor("ps_q1", [128, 512], F32))
    ps_c  = E(nc.psum_tensor("ps_c4", [128, 1024], F32))
    ps_t0 = E(nc.psum_tensor("ps_t04", [128, 128], BF16))
    ps_t1 = E(nc.psum_tensor("ps_t14", [128, 128], BF16))
    ps_tp = [ps_t0, ps_t1]
    ps_qs = [ps_q0, ps_q1]
    ch_w = Chain(nc, "l4w", stack); ch_x = Chain(nc, "l4x", stack); ch_ln = Chain(nc, "l4ln", stack)
    tp_p = [Chain(nc, f"l4tpp{i}", stack) for i in range(2)]
    tp_c = [Chain(nc, f"l4tpc{i}", stack) for i in range(2)]
    ch_mm = Chain(nc, "l4mm", stack); ch_qe = Chain(nc, "l4qe", stack)
    ch_p  = Chain(nc, "l4p", stack)
    ch_r  = Chain(nc, "l4r", stack)
    ch_sx = Chain(nc, "l4sx", stack)
    ch_cm = Chain(nc, "l4cm", stack)
    ch_ce = Chain(nc, "l4ce", stack)
    ch_od = Chain(nc, "l4od", stack)
    tstate = [0]
    for kc in range(8):
        ch_w.dma(nc.gpsimd.dma_start(out=wqs[:, kc, :], in_=wq[kc*128:(kc+1)*128, :]))
        ch_w.dma(nc.gpsimd.dma_start(out=wos[:, kc, :], in_=wo[kc*128:(kc+1)*128, :]))
    ch_w.dma(nc.gpsimd.dma_start(out=bqs[:], in_=bq[None, :]))
    ch_w.dma(nc.gpsimd.dma_start(out=bos[:], in_=bo[None, :]))
    ch_w.dma(nc.gpsimd.dma_start(out=idn[:], in_=idn_in[:]))
    nc.vector.memset(onesr[:], 1.0)
    ch_w.bump(nc.vector.memset(cst[:], LN_EPS))
    ch_w.wait(nc.sync); ch_w.wait(nc.vector); ch_w.wait(nc.tensor); ch_w.wait(nc.scalar); ch_w.wait(nc.gpsimd)
    nc.all_engine_barrier()

    def emit_tp(srcs, dsts):
        for src, dst in zip(srcs, dsts):
            sl = tstate[0] % 2
            tp_c[sl].wait(nc.tensor)
            tp_p[sl].bump(nc.tensor.transpose(ps_tp[sl][:], src, idn[:]))
            tp_p[sl].wait(nc.vector)
            tp_c[sl].bump(nc.vector.tensor_copy(dst, ps_tp[sl][:]))
            tstate[0] += 1

    for ti in range(NTOK // 128):
        b, pt = ti // 2, ti % 2
        ch_x.dma(nc.sync.dma_start(out=xt[:], in_=x2[b, pt*128:(pt+1)*128, :]))
        ch_x.wait(nc.vector); ch_x.wait(nc.scalar)
        emit_ln(nc, xt[:], xbf[:], stat[:], cst[:], ch_ln)
        ch_ln.wait(nc.tensor)
        emit_tp([xbf[:, kc*128:(kc+1)*128] for kc in range(8)],
                [xfm[:, kc, :] for kc in range(8)])
        tp_c[0].wait(nc.tensor); tp_c[1].wait(nc.tensor)
        for n6 in range(6):
            sl = n6 % 2
            ch_qe.wait(nc.tensor) if n6 >= 2 else None
            for kc in range(8):
                nc.tensor.matmul(ps_qs[sl][:], xfm[:, kc, :], wqs[:, kc, n6*512:(n6+1)*512],
                                 start=(kc == 0), stop=False)
            ch_mm.bump(nc.tensor.matmul(ps_qs[sl][:], onesr[:], bqs[:, n6*512:(n6+1)*512],
                                        start=False, stop=True))
            ch_mm.wait(nc.vector)
            if n6 < 2:
                ch_qe.bump(nc.vector.tensor_scalar(qkv[:, n6*512:(n6+1)*512], ps_qs[sl][:], 0.125, None, op0=ALU.mult))
            else:
                ch_qe.bump(nc.vector.tensor_copy(qkv[:, n6*512:(n6+1)*512], ps_qs[sl][:]))
        q = qkv[:, 0:1024]; k = qkv[:, 1024:2048]; v = qkv[:, 2048:3072]
        stm3 = stm.rearrange("p (h d) -> p h d", d=16)
        ch_qe.wait(nc.vector); ch_qe.wait(nc.gpsimd)
        rn = [0, 0]
        for dl in range(16):
            sl = dl % 2
            eng = nc.vector if dl % 2 == 0 else nc.gpsimd
            eng.wait_ge(ch_r.sem, rn[sl])
            w1 = (16 - dl) * 64
            eng.tensor_tensor(prd[sl][:, 0:w1], q[:, 0:w1], k[:, dl*64:1024], op=ALU.mult)
            if dl > 0:
                inst = eng.tensor_tensor(prd[sl][:, w1:1024], q[:, w1:1024], k[:, 0:dl*64], op=ALU.mult)
            else:
                inst = eng.tensor_tensor(prd[sl][:, 1020:1024], q[:, 1020:1024], k[:, 1020:1024], op=ALU.mult)
            ch_p.bump(inst)
            nc.vector.wait_ge(ch_p.sem, ch_p.n)
            ch_r.bump(nc.vector.tensor_reduce(stm3[:, :, dl:dl+1],
                        prd[sl][:].rearrange("p (h d) -> p h d", d=64), axis=AX.X, op=ALU.add))
            rn[sl] = ch_r.n
        ch_r.wait(nc.vector)
        ch_sx.bump(nc.vector.reduce_max(mx[:, :, None], stm3[:, :, :], axis=AX.X))
        ch_sx.wait(nc.vector)
        ch_sx.bump(nc.vector.tensor_tensor(etm.rearrange("p (h d) -> p h d", d=16), stm3[:, :, :],
                                           mx[:, :, None].broadcast_to([128, 16, 16]), op=ALU.subtract))
        ch_sx.wait(nc.scalar)
        ch_sx.bump(nc.scalar.activation(etm[:], etm[:], AF.Exp))
        ch_sx.wait(nc.vector)
        ch_sx.bump(nc.vector.reduce_sum(sm[:, :, None], etm.rearrange("p (h d) -> p h d", d=16), axis=AX.X))
        ch_sx.wait(nc.vector)
        ch_sx.bump(nc.vector.reciprocal(sm[:], sm[:]))
        ch_sx.wait(nc.vector)
        ch_sx.bump(nc.vector.tensor_tensor(atm.rearrange("p (h d) -> p h d", d=16),
                                           etm.rearrange("p (h d) -> p h d", d=16),
                                           sm[:, :, None].broadcast_to([128, 16, 16]), op=ALU.mult))
        atm3 = atm.rearrange("p (h d) -> p h d", d=16)
        ch_sx.wait(nc.vector); ch_sx.wait(nc.gpsimd)
        cmn = [0, 0]
        for dl in range(16):
            sl = dl % 2
            eng = nc.vector if dl % 2 == 0 else nc.gpsimd
            eng.wait_ge(ch_cm.sem, cmn[sl])
            w1 = (16 - dl) * 64
            p3a = prd[sl][:, 0:w1].rearrange("p (h d) -> p h d", d=64)
            inst = eng.tensor_tensor(p3a, v[:, dl*64:1024].rearrange("p (h d) -> p h d", d=64),
                                    atm3[:, 0:(16-dl), dl:dl+1].broadcast_to([128, 16-dl, 64]), op=ALU.mult)
            if dl > 0:
                inst = eng.tensor_tensor(prd[sl][:, w1:1024].rearrange("p (h d) -> p h d", d=64),
                                        v[:, 0:dl*64].rearrange("p (h d) -> p h d", d=64),
                                        atm3[:, (16-dl):16, dl:dl+1].broadcast_to([128, dl, 64]), op=ALU.mult)
            ch_p.bump(inst)
            nc.tensor.wait_ge(ch_p.sem, ch_p.n)
            for nn in range(2):
                inst = nc.tensor.matmul(ps_c[:, nn*512:(nn+1)*512], idn[:], prd[sl][:, nn*512:(nn+1)*512],
                                        start=(dl == 0), stop=(dl == 15))
            ch_cm.bump(inst)
            cmn[sl] = ch_cm.n
        ch_cm.wait(nc.vector)
        ch_ce.bump(nc.vector.tensor_copy(ctxb[:], ps_c[:]))
        ch_ce.wait(nc.tensor)
        emit_tp([ctxb[:, kc*128:(kc+1)*128] for kc in range(8)],
                [ctxf[:, kc, :] for kc in range(8)])
        tp_c[0].wait(nc.tensor); tp_c[1].wait(nc.tensor)
        for nn in range(2):
            sl = nn
            for kc in range(8):
                nc.tensor.matmul(ps_qs[sl][:], ctxf[:, kc, :], wos[:, kc, nn*512:(nn+1)*512],
                                 start=(kc == 0), stop=False)
            ch_mm.bump(nc.tensor.matmul(ps_qs[sl][:], onesr[:], bos[:, nn*512:(nn+1)*512],
                                        start=False, stop=True))
        ch_mm.wait(nc.vector)
        nc.vector.tensor_add(x3t[:, 0:512], ps_q0[:], xt[:, 0:512])
        ch_od.bump(nc.vector.tensor_add(x3t[:, 512:1024], ps_q1[:], xt[:, 512:1024]))
        ch_od.wait(nc.sync)
        ch_od.dma(nc.sync.dma_start(out=x3[b, pt*128:(pt+1)*128, :], in_=x3t[:]))
        nc.all_engine_barrier()
    ch_od.wait(nc.sync)
    nc.all_engine_barrier()
    return nc


def host_l4_weights(Wqkv, bqkv):
    idx = []
    for third in range(3):
        for h in range(16):
            idx.extend(range(h*192 + third*64, h*192 + third*64 + 64))
    idx = np.array(idx)
    return np.ascontiguousarray(Wqkv[:, idx]), np.ascontiguousarray(bqkv[idx])


def l5_band_vectors(weights):
    wband = np.zeros((12, 3, 4096), np.float32)
    for slot in range(12):
        s = slot
        is_g = slot in (6, 11)
        for fp in range(4096):
            if slot < 7:
                f = fp
            else:
                j = fp & 31; u = fp >> 5
                f = j*128 + u
            S_ref = 1 << s
            q = f >> (s+1); r = (f >> s) & 1; u_ref = f & (S_ref - 1)
            widx = q * S_ref + u_ref
            w = weights[s, widx]
            wband[slot, 0, fp] = w[0, 0] if r == 0 else w[1, 1]
            if is_g:
                wband[slot, 1, fp] = w[1, 0] if r == 0 else 0.0
                wband[slot, 2, fp] = w[0, 1] if r == 1 else 0.0
            else:
                wband[slot, 1, fp] = w[0, 1] if r == 0 else 0.0
                wband[slot, 2, fp] = w[1, 0] if r == 1 else 0.0
    return wband

def l5_band_stride(slot):
    return 1 << slot if slot < 7 else 1 << (slot - 7)

def l5_make_mats(wband, slot):
    S = l5_band_stride(slot)
    mats = np.zeros((32, 128, 128), np.float32)
    for c in range(32):
        for p in range(128):
            fp = c*128 + p
            mats[c, p, p] = wband[slot, 0, fp]
            if p + S < 128:
                mats[c, p, p+S] = wband[slot, 1, fp]
            if p - S >= 0:
                mats[c, p, p-S] = wband[slot, 2, fp]
    return mats

def l5_compose_host(wband, lo):
    slots = range(0, 7) if lo else range(7, 12)
    slots = list(slots)
    P = l5_make_mats(wband, slots[-1])
    for s in reversed(slots[:-1]):
        H = l5_make_mats(wband, s)
        P = np.einsum('cpm,cpn->cmn', H, P)
    return P

def l5_perm_idx():
    fp = np.arange(4096)
    j = fp & 31; u = fp >> 5
    return j*128 + u


def build_l5(nc, stack):
    x3  = nc.dram_tensor("x3", [NB, NPOS, D], F32, kind="ExternalInput").ap()
    wexp= nc.dram_tensor("wexp", [D, 4096], F32, kind="ExternalInput").ap()
    bexp= nc.dram_tensor("bexp", [4096], F32, kind="ExternalInput").ap()
    wband=nc.dram_tensor("wband", [2, 12, 3, 128, 32], F32, kind="ExternalInput").ap()
    upb = nc.dram_tensor("upb", [128, 32], F32, kind="ExternalInput").ap()
    wcon= nc.dram_tensor("wcon", [4096, D], F32, kind="ExternalInput").ap()
    bcon= nc.dram_tensor("bcon", [D], F32, kind="ExternalInput").ap()
    dbias=nc.dram_tensor("dbias", [D], F32, kind="ExternalInput").ap()
    idn_in = nc.dram_tensor("idn", [128, 128], F32, kind="ExternalInput").ap()
    x4  = nc.dram_tensor("x4", [NB, NPOS, D], F32, kind="ExternalOutput").ap()
    hfin = nc.dram_tensor("hfin", [128, 32, NTOK], BF16, kind="Internal").ap()
    E = stack.enter_context
    idn = E(nc.sbuf_tensor("idn5", [128, 128], BF16))
    ones5=E(nc.sbuf_tensor("ones5", [1, 512], BF16))
    cst = E(nc.sbuf_tensor("cst5", [128, 1], F32))
    upbs= E(nc.sbuf_tensor("upbs", [128, 32], F32))
    Lmat= E(nc.sbuf_tensor("Lmat", [128, 4, 32, 128], BF16))
    h1  = E(nc.sbuf_tensor("h1", [128, 32, 512], BF16))
    hp  = E(nc.sbuf_tensor("hp", [128, 32, 512], BF16))
    xt  = E(nc.sbuf_tensor("xt5", [128, 1024], F32))
    stat= E(nc.sbuf_tensor("stat5", [128, 6], F32))
    xbf = E(nc.sbuf_tensor("xbf5", [128, 1024], BF16))
    xfm = E(nc.sbuf_tensor("xfm5", [128, 8, 512], BF16))
    h0c = E(nc.sbuf_tensor("h0c", [128, 2, 512], BF16))
    wv  = E(nc.sbuf_tensor("wv5", [128, 3, 32], BF16))
    bexps=E(nc.sbuf_tensor("bexps", [1, 4096], BF16))
    ps_a = E(nc.psum_tensor("ps_a5", [128, 512], F32))
    ps_b = E(nc.psum_tensor("ps_b5", [128, 512], F32))
    ps_t0= E(nc.psum_tensor("ps_t05", [128, 128], BF16))
    ps_t1= E(nc.psum_tensor("ps_t15", [128, 128], BF16))
    ps_c0= E(nc.psum_tensor("ps_c05", [128, 128], F32))
    ps_c1= E(nc.psum_tensor("ps_c15", [128, 128], F32))
    ps_tp = [ps_t0, ps_t1]
    ps_cc = [ps_c0, ps_c1]
    ch_w = Chain(nc, "l5w", stack); ch_x = Chain(nc, "l5x", stack); ch_ln = Chain(nc, "l5ln", stack)
    tp_p = [Chain(nc, f"l5tpp{i}", stack) for i in range(2)]
    tp_c = [Chain(nc, f"l5tpc{i}", stack) for i in range(2)]
    ch_bv = Chain(nc, "l5bv", stack)
    ch_bs = Chain(nc, "l5bs", stack)
    ch_ba = Chain(nc, "l5ba", stack)
    cc_p = [Chain(nc, f"l5ccp{i}", stack) for i in range(2)]
    cc_c = [Chain(nc, f"l5ccc{i}", stack) for i in range(2)]
    ch_mm = Chain(nc, "l5mm", stack)
    ch_ev = Chain(nc, "l5ev", stack)
    ch_pm = Chain(nc, "l5pm", stack)
    ch_hf = Chain(nc, "l5hf", stack)
    ch_od = Chain(nc, "l5od", stack)
    tstate = [0]

    ch_w.dma(nc.gpsimd.dma_start(out=idn[:], in_=idn_in[:]))
    ch_w.dma(nc.sync.dma_start(out=upbs[:], in_=upb[:]))
    ch_w.dma(nc.gpsimd.dma_start(out=bexps[:], in_=bexp[None, :]))
    nc.vector.memset(ones5[:], 1.0)
    ch_w.bump(nc.vector.memset(cst[:], LN_EPS))
    ch_w.wait(nc.sync); ch_w.wait(nc.vector); ch_w.wait(nc.tensor); ch_w.wait(nc.scalar); ch_w.wait(nc.gpsimd)
    nc.all_engine_barrier()

    bufA = h1[:, :, 0:128]; bufB = h1[:, :, 128:256]
    Hbuf = h1[:, :, 256:384]; t0 = h1[:, :, 384:512]; t1 = hp[:, :, 0:128]
    def construct(bf, slot, dest):
        S = l5_band_stride(slot)
        for band in range(3):
            ch_bv.dma(nc.gpsimd.dma_start(out=wv[:, band, :], in_=wband[bf, slot, band, :, :]))
        ch_bv.wait(nc.gpsimd); ch_ba.wait(nc.gpsimd)
        for band, (dst, off) in enumerate(((t0, 0), (t1, S), (dest, -S))):
            ch_bs.bump(nc.gpsimd.affine_select(out=dst, in_=wv[:, band, :, None].broadcast_to([128, 32, 128]),
                        compare_op=ALU.is_equal, fill=0.0, base=-off,
                        pattern=[[0, 32], [1, 128]], channel_multiplier=-1))
        ch_bs.wait(nc.vector)
        nc.vector.tensor_tensor(dest, dest, t0, op=ALU.add)
        ch_ba.bump(nc.vector.tensor_tensor(dest, dest, t1, op=ALU.add))
    for bf in range(2):
        for lo in range(2):
            slots = list(range(0, 7)) if lo == 0 else list(range(7, 12))
            construct(bf, slots[-1], bufA)
            cur, nxt = bufA, bufB
            for s in reversed(slots[:-1]):
                construct(bf, s, Hbuf)
                ch_ba.wait(nc.tensor)
                for c in range(32):
                    sl = c % 2
                    cc_c[sl].wait(nc.tensor)
                    cc_p[sl].bump(nc.tensor.matmul(ps_cc[sl][:], Hbuf[:, c, :], cur[:, c, :],
                                                   start=True, stop=True))
                    cc_p[sl].wait(nc.vector)
                    cc_c[sl].bump(nc.vector.tensor_copy(nxt[:, c, :], ps_cc[sl][:]))
                cur, nxt = nxt, cur
                cc_c[0].wait(nc.gpsimd); cc_c[1].wait(nc.gpsimd)
            cc_c[0].wait(nc.vector); cc_c[1].wait(nc.vector)
            ch_ba.wait(nc.vector)
            seti = bf*2 + lo
            ch_ba.bump(nc.vector.tensor_copy(Lmat[:, seti, :, :], cur))
            ch_ba.wait(nc.tensor); ch_ba.wait(nc.gpsimd)
    nc.all_engine_barrier()

    with contextlib.ExitStack() as s1:
        wexps = s1.enter_context(nc.sbuf_tensor("wexps", [128, 8, 4096], BF16))
        for kc in range(8):
            ch_w.dma(nc.gpsimd.dma_start(out=wexps[:, kc, :], in_=wexp[kc*128:(kc+1)*128, :]))
        ch_w.wait(nc.tensor); ch_w.wait(nc.vector)
        nc.all_engine_barrier()
        def emit_tp(srcs, dsts):
            for src, dst in zip(srcs, dsts):
                sl = tstate[0] % 2
                tp_c[sl].wait(nc.tensor)
                tp_p[sl].bump(nc.tensor.transpose(ps_tp[sl][:], src, idn[:]))
                tp_p[sl].wait(nc.vector)
                tp_c[sl].bump(nc.vector.tensor_copy(dst, ps_tp[sl][:]))
                tstate[0] += 1
        NT = NTOK // 512
        for t in range(NT):
            for st4 in range(4):
                gtok = t*512 + st4*128
                b, po = gtok // NPOS, gtok % NPOS
                ch_ln.wait(nc.sync)
                ch_x.dma(nc.sync.dma_start(out=xt[:], in_=x3[b, po:po+128, :]))
                ch_x.wait(nc.vector); ch_x.wait(nc.scalar)
                emit_ln(nc, xt[:], xbf[:], stat[:], cst[:], ch_ln)
                ch_ln.wait(nc.tensor)
                emit_tp([xbf[:, kc*128:(kc+1)*128] for kc in range(8)],
                        [xfm[:, kc, st4*128:(st4+1)*128] for kc in range(8)])
            tp_c[0].wait(nc.tensor); tp_c[1].wait(nc.tensor)
            ch_pm.wait(nc.tensor)
            evn = {"a0": 0, "a1": 0, "b0": 0, "b1": 0}
            psl = [ps_a, ps_b, ps_c0, ps_c1]
            for c in range(32):
                sl = c % 2
                psA = ps_a if sl == 0 else ps_b
                nc.tensor.wait_ge(ch_ev.sem, evn["a%d" % sl])
                for kc in range(8):
                    nc.tensor.matmul(psA[:], wexps[:, kc, c*128:(c+1)*128], xfm[:, kc, :],
                                     start=(kc == 0), stop=False)
                ch_mm.bump(nc.tensor.matmul(psA[:], bexps[:, c*128:(c+1)*128], ones5[:],
                                            start=False, stop=True))
                nc.vector.wait_ge(ch_mm.sem, ch_mm.n)
                ch_ev.bump(nc.vector.tensor_copy(h0c[:, sl, :], psA[:]))
                evn["a%d" % sl] = ch_ev.n
                nc.tensor.wait_ge(ch_ev.sem, ch_ev.n)
                ch_mm.bump(nc.tensor.matmul(psA[:], Lmat[:, 0, c, :], h0c[:, sl, :],
                                            start=True, stop=True))
                nc.vector.wait_ge(ch_mm.sem, ch_mm.n)
                ch_ev.bump(nc.vector.tensor_copy(h1[:, c, :], psA[:]))
                evn["a%d" % sl] = ch_ev.n
            ch_ev.wait(nc.gpsimd); ch_ev.wait(nc.sync)
            for g in range(32):
                eng = nc.gpsimd if g % 2 == 0 else nc.sync
                ch_pm.dma(eng.dma_start(out=hp[:, g, :], in_=h1[g*4:(g+1)*4, :, :]))
            ch_pm.wait(nc.tensor)
            evb = [0, 0]
            for g in range(32):
                sl = g % 2
                psA = ps_b if sl == 0 else ps_a
                nc.tensor.wait_ge(ch_ev.sem, evb[sl])
                ch_mm.bump(nc.tensor.matmul(psA[:], Lmat[:, 1, g, :], hp[:, g, :], start=True, stop=True))
                nc.scalar.wait_ge(ch_mm.sem, ch_mm.n)
                ch_ev.bump(nc.scalar.activation(h1[:, g, :], psA[:], AF.Gelu, bias=upbs[:, g:g+1]))
                evb[sl] = ch_ev.n
            ch_ev.wait(nc.gpsimd); ch_ev.wait(nc.sync)
            for g in range(32):
                eng = nc.gpsimd if g % 2 == 0 else nc.sync
                ch_pm.dma(eng.dma_start(out=hp[g*4:(g+1)*4, :, :], in_=h1[:, g, :]))
            ch_pm.wait(nc.tensor)
            evb = [0, 0]
            for c in range(32):
                sl = c % 2
                psA = ps_b if sl == 0 else ps_a
                nc.tensor.wait_ge(ch_ev.sem, evb[sl])
                ch_mm.bump(nc.tensor.matmul(psA[:], Lmat[:, 2, c, :], hp[:, c, :], start=True, stop=True))
                nc.vector.wait_ge(ch_mm.sem, ch_mm.n)
                ch_ev.bump(nc.vector.tensor_copy(h1[:, c, :], psA[:]))
                evb[sl] = ch_ev.n
            ch_ev.wait(nc.gpsimd); ch_ev.wait(nc.sync)
            for g in range(32):
                eng = nc.gpsimd if g % 2 == 0 else nc.sync
                ch_pm.dma(eng.dma_start(out=hp[:, g, :], in_=h1[g*4:(g+1)*4, :, :]))
            ch_pm.wait(nc.tensor)
            evb = [0, 0]
            for g in range(32):
                sl = g % 2
                psA = ps_b if sl == 0 else ps_a
                nc.tensor.wait_ge(ch_ev.sem, evb[sl])
                ch_mm.bump(nc.tensor.matmul(psA[:], Lmat[:, 3, g, :], hp[:, g, :], start=True, stop=True))
                nc.vector.wait_ge(ch_mm.sem, ch_mm.n)
                ch_ev.bump(nc.vector.tensor_copy(h1[:, g, :], psA[:]))
                evb[sl] = ch_ev.n
            ch_ev.wait(nc.sync)
            ch_hf.dma(nc.sync.dma_start(out=hfin[:, :, t*512:(t+1)*512], in_=h1[:]))
            ch_hf.wait(nc.sync)
        nc.all_engine_barrier()
    with contextlib.ExitStack() as s2:
        wcons = s2.enter_context(nc.sbuf_tensor("wcons", [128, 32, 1024], BF16))
        bcons = s2.enter_context(nc.sbuf_tensor("bcons", [1, 1024], BF16))
        dbs   = s2.enter_context(nc.sbuf_tensor("dbs", [1, 1024], BF16))
        x4t   = s2.enter_context(nc.sbuf_tensor("x4t", [128, 1024], F32))
        for g in range(32):
            ch_w.dma(nc.gpsimd.dma_start(out=wcons[:, g, :], in_=wcon[g*128:(g+1)*128, :]))
        ch_w.dma(nc.gpsimd.dma_start(out=bcons[:], in_=bcon[None, :]))
        ch_w.dma(nc.gpsimd.dma_start(out=dbs[:], in_=dbias[None, :]))
        ch_w.wait(nc.tensor); ch_w.wait(nc.vector); ch_w.wait(nc.sync)
        nc.all_engine_barrier()
        NT = NTOK // 512
        for t in range(NT):
            ch_mm.wait(nc.sync)
            ch_x.dma(nc.sync.dma_start(out=h1[:], in_=hfin[:, :, t*512:(t+1)*512]))
            for st4 in range(4):
                gtok = t*512 + st4*128
                b, po = gtok // NPOS, gtok % NPOS
                ch_od.wait(nc.sync)
                ch_x.dma(nc.sync.dma_start(out=xt[:], in_=x3[b, po:po+128, :]))
                ch_x.wait(nc.tensor)
                ch_od.wait(nc.tensor)
                for nn, ps in ((0, ps_a), (1, ps_b)):
                    for g in range(32):
                        nc.tensor.matmul(ps[:], h1[:, g, st4*128:(st4+1)*128], wcons[:, g, nn*512:(nn+1)*512],
                                         start=(g == 0), stop=False)
                    nc.tensor.matmul(ps[:], ones5[:, 0:128], bcons[:, nn*512:(nn+1)*512], start=False, stop=False)
                    ch_mm.bump(nc.tensor.matmul(ps[:], ones5[:, 0:128], dbs[:, nn*512:(nn+1)*512], start=False, stop=True))
                ch_mm.wait(nc.vector)
                nc.vector.tensor_add(x4t[:, 0:512], ps_a[:], xt[:, 0:512])
                ch_od.bump(nc.vector.tensor_add(x4t[:, 512:1024], ps_b[:], xt[:, 512:1024]))
                ch_od.wait(nc.sync)
                ch_od.dma(nc.sync.dma_start(out=x4[b, po:po+128, :], in_=x4t[:]))
        ch_od.wait(nc.sync)
        nc.all_engine_barrier()
    return nc



def _run(build, in_maps):
    nc = bass.Bass("TRN2", target_bir_lowering=False)
    with contextlib.ExitStack() as stack:
        build(nc, stack)
    return run_bass_kernel_spmd(nc, in_maps, core_ids=list(range(8))).results

def kernel(**inp):
    f32 = np.float32
    asf = lambda a: np.ascontiguousarray(np.asarray(a, dtype=f32))
    x0 = asf(inp["L0_attn_x"])
    idn, masks = host_constants()
    in_maps = [{"x": np.ascontiguousarray(x0[:, c*256:(c+1)*256, :]),
                "wqkv": asf(inp["L0_attn_Wqkv"]), "bqkv": asf(inp["L0_attn_bqkv"]),
                "idn": idn, "mask": masks} for c in range(8)]
    r1 = _run(build_l0attn, in_maps)
    Y = [r["y"] for r in r1]; DEN = [r["den"] for r in r1]
    sel = host_sel()
    in_maps = []
    for c in range(8):
        yg = np.ascontiguousarray(np.stack(
            [Y[src][hl, :, :, c*256:(c+1)*256] for src in range(8) for hl in range(2)]))
        dg = np.ascontiguousarray(np.stack(
            [DEN[src][hl, :, c*256:(c+1)*256] for src in range(8) for hl in range(2)]))
        in_maps.append({"yg": yg, "den": dg, "x": np.ascontiguousarray(x0[:, c*256:(c+1)*256, :]),
                        "wp": asf(inp["L0_attn_Wproj"]), "bp": asf(inp["L0_attn_bproj"]), "sel": sel})
    r2 = _run(build_l2, in_maps)
    X1 = [r["x1"] for r in r2]
    w3 = np.ascontiguousarray(np.concatenate(
        [asf(inp["L0_ffn_wf"]).T, asf(inp["L0_ffn_wi"]).T, asf(inp["L0_ffn_wc"]).T], axis=1))
    u3 = np.ascontiguousarray(np.concatenate(
        [asf(inp["L0_ffn_uf"]).T, asf(inp["L0_ffn_ui"]).T, asf(inp["L0_ffn_uc"]).T], axis=1))
    b3 = np.concatenate([asf(inp["L0_ffn_bf"]), asf(inp["L0_ffn_bi"]), asf(inp["L0_ffn_bc"])])
    in_maps = [{"x1": X1[c], "w3": w3, "u3": u3, "b3": b3,
                "wo": np.ascontiguousarray(asf(inp["L0_ffn_wo"]).T),
                "uo": np.ascontiguousarray(asf(inp["L0_ffn_uo"]).T),
                "bo": asf(inp["L0_ffn_bo"]), "inc": asf(inp["L0_ffn_in_c"]), "idn": idn}
               for c in range(8)]
    r3 = _run(build_l3, in_maps)
    X2 = [r["x2"] for r in r3]
    wqr, bqr = host_l4_weights(asf(inp["L1_attn_W_qkv"]), asf(inp["L1_attn_b_qkv"]))
    in_maps = [{"x2": X2[c], "wq": wqr, "bq": bqr,
                "wo": asf(inp["L1_attn_W_out"]), "bo": asf(inp["L1_attn_b_out"]), "idn": idn}
               for c in range(8)]
    r4 = _run(build_l4, in_maps)
    X3 = [r["x3"] for r in r4]
    perm = l5_perm_idx()
    wband_all = np.zeros((2, 12, 3, 128, 32), np.float32)
    for bf, wts in ((0, asf(inp["L1_ffn_up_weights"])), (1, asf(inp["L1_ffn_down_weights"]))):
        wb = l5_band_vectors(wts)
        wband_all[bf] = wb.reshape(12, 3, 32, 128).transpose(0, 1, 3, 2)
    upb_in = asf(inp["L1_ffn_up_bias"])[perm].reshape(32, 128).T.copy()
    wcon_perm = np.ascontiguousarray(asf(inp["L1_ffn_W_con"])[perm, :])
    in_maps = [{"x3": X3[c], "wexp": asf(inp["L1_ffn_W_exp"]), "bexp": asf(inp["L1_ffn_b_exp"]),
                "wband": wband_all, "upb": upb_in, "wcon": wcon_perm,
                "bcon": asf(inp["L1_ffn_b_con"]), "dbias": asf(inp["L1_ffn_down_bias"]), "idn": idn}
               for c in range(8)]
    r5 = _run(build_l5, in_maps)
    out = np.concatenate([r5[c]["x4"] for c in range(8)], axis=1)
    return out.astype(np.float32)


# version 08 (final)
# speedup vs baseline: 1.1304x; 1.0215x over previous
import contextlib
import numpy as np
import ml_dtypes
import concourse.bass as bass
import concourse.mybir as mybir
from concourse.bass_utils import run_bass_kernel_spmd

F32 = mybir.dt.float32; BF16 = mybir.dt.bfloat16
AF = mybir.ActivationFunctionType; ALU = mybir.AluOpType; AX = mybir.AxisListType
NB, NPOS, D = 32, 256, 1024
NTOK = NB * NPOS
GB = 4
NG = NB // GB
BSH = 20.0
LN_EPS = 1e-5

class Chain:
    def __init__(self, nc, name, stack):
        self.sem = stack.enter_context(nc.semaphore(name)); self.n = 0
    def bump(self, inst, k=1):
        inst.then_inc(self.sem, k); self.n += k; return inst
    def dma(self, inst):
        inst.then_inc(self.sem, 16); self.n += 16; return inst
    def wait(self, eng):
        eng.wait_ge(self.sem, self.n)


def build_l0attn(nc, stack):
    x = nc.dram_tensor("x", [NB, NPOS, D], F32, kind="ExternalInput").ap()
    wqkv = nc.dram_tensor("wqkv", [D, 3*D], F32, kind="ExternalInput").ap()
    bqkv = nc.dram_tensor("bqkv", [3*D], F32, kind="ExternalInput").ap()
    idn_in = nc.dram_tensor("idn", [128, 128], F32, kind="ExternalInput").ap()
    mask_in = nc.dram_tensor("mask", [4, 128, 512], F32, kind="ExternalInput").ap()
    y_out = nc.dram_tensor("y", [2, NB, 64, 2048], BF16, kind="ExternalOutput").ap()
    d_out = nc.dram_tensor("den", [2, NB, 2048], F32, kind="ExternalOutput").ap()
    E = stack.enter_context
    wq   = E(nc.sbuf_tensor("wq", [128, 8, 3072], BF16))
    bq   = E(nc.sbuf_tensor("bq", [1, 3072], BF16))
    idn  = E(nc.sbuf_tensor("idn_s", [128, 128], BF16))
    mask = E(nc.sbuf_tensor("mask_s", [128, 4, 512], BF16))
    onesr= E(nc.sbuf_tensor("onesr", [1, 512], BF16))
    xt   = E(nc.sbuf_tensor("xt", [128, GB, 1024], F32))
    stat = E(nc.sbuf_tensor("stat", [128, 20], F32))
    xbf  = E(nc.sbuf_tensor("xbf", [128, GB, 1024], BF16))
    xfm  = E(nc.sbuf_tensor("xfm", [128, 8, GB*128], BF16))
    qs_t = E(nc.sbuf_tensor("qs", [64, GB, 2048], BF16))
    ks_t = E(nc.sbuf_tensor("ks", [64, GB, 2048], BF16))
    vtm  = E(nc.sbuf_tensor("vtm", [128, GB, 1024], BF16))
    vsp  = E(nc.sbuf_tensor("vsp", [128, GB, 16, 65], BF16))
    atile= E(nc.sbuf_tensor("atile", [128, 16, 512], BF16))
    ybuf = E(nc.sbuf_tensor("ybuf", [64, 2048], BF16))
    dbuf = E(nc.sbuf_tensor("dbuf", [1, 2048], F32))
    cst  = E(nc.sbuf_tensor("cst", [128, 2], F32))
    ps_tp0 = E(nc.psum_tensor("ps_tp0", [128, 128], BF16))
    ps_tp1 = E(nc.psum_tensor("ps_tp1", [128, 128], BF16))
    ps_tp = [ps_tp0, ps_tp1]
    ps_mm = E(nc.psum_tensor("ps_mm", [128, 2, 512], F32))
    ps_s  = E(nc.psum_tensor("ps_s", [128, 2, 512], F32))
    ps_av = E(nc.psum_tensor("ps_av", [65, 512], F32))

    ch_w   = Chain(nc, "ch_w", stack)
    ch_x   = Chain(nc, "ch_x", stack)
    ch_v1  = Chain(nc, "ch_v1", stack)
    ch_s1  = Chain(nc, "ch_s1", stack)
    ch_v2  = Chain(nc, "ch_v2", stack)
    ch_s2  = Chain(nc, "ch_s2", stack)
    ch_v3  = Chain(nc, "ch_v3", stack)
    tp_p = [Chain(nc, f"tp_p{i}", stack) for i in range(2)]
    tp_c = [Chain(nc, f"tp_c{i}", stack) for i in range(2)]
    mm_p = [Chain(nc, f"mm_p{i}", stack) for i in range(2)]
    mm_c = [Chain(nc, f"mm_c{i}", stack) for i in range(2)]
    s_p  = [Chain(nc, f"s_p{i}", stack) for i in range(2)]
    s_c  = [Chain(nc, f"s_c{i}", stack) for i in range(2)]
    ch_fold = Chain(nc, "ch_fold", stack)
    ch_ae  = Chain(nc, "ch_ae", stack)
    ch_am  = Chain(nc, "ch_am", stack)
    ch_avd = Chain(nc, "ch_avd", stack)
    ch_ye  = Chain(nc, "ch_ye", stack)
    ch_yd  = Chain(nc, "ch_yd", stack)
    slot_need = [0, 0]

    for kc in range(8):
        ch_w.dma(nc.gpsimd.dma_start(out=wq[:, kc, :], in_=wqkv[kc*128:(kc+1)*128, :]))
    ch_w.dma(nc.gpsimd.dma_start(out=bq[:], in_=bqkv[None, :]))
    ch_w.dma(nc.gpsimd.dma_start(out=idn[:], in_=idn_in[:]))
    ch_w.dma(nc.gpsimd.dma_start(out=mask[:], in_=mask_in.transpose([1, 0, 2])))
    nc.vector.memset(onesr[:], 1.0)
    nc.vector.memset(cst[:, 0:1], 1e-5)
    nc.vector.memset(cst[:, 1:2], -20.0)
    ch_w.bump(nc.vector.memset(vsp[:, :, :, 64:65], 1.0))
    nc.all_engine_barrier()

    for hloc in range(2):
      for g in range(NG):
        for b in range(GB):
            ch_x.dma(nc.sync.dma_start(out=xt[:, b, :], in_=x[g*GB+b, hloc*128:(hloc+1)*128, :]))
        ch_x.wait(nc.vector)
        for b in range(GB):
            inst = nc.vector.reduce_sum(stat[:, 0+b:1+b], xt[:, b, :], axis=AX.X)
        ch_v1.bump(inst)
        ch_v1.wait(nc.vector)
        ch_v1.bump(nc.vector.tensor_scalar_mul(stat[:, 8:12], stat[:, 0:4], 1.0/1024.0))
        ch_x.wait(nc.scalar)
        for b in range(GB):
            inst = nc.scalar.activation(xbf[:, b, :], xt[:, b, :], AF.Square, accum_out=stat[:, 4+b:5+b])
        ch_s1.bump(inst)
        ch_v1.wait(nc.vector); ch_s1.wait(nc.vector)
        nc.vector.tensor_tensor(stat[:, 16:20], stat[:, 8:12], stat[:, 8:12], op=ALU.mult)
        ch_v2.bump(nc.vector.engine_nop())
        ch_v2.wait(nc.vector)
        ch_v2.bump(nc.vector.tensor_scalar(stat[:, 12:16], stat[:, 4:8], 1.0/1024.0, None, op0=ALU.mult))
        ch_v2.wait(nc.vector)
        ch_v2.bump(nc.vector.tensor_tensor(stat[:, 12:16], stat[:, 12:16], stat[:, 16:20], op=ALU.subtract))
        ch_v2.wait(nc.scalar)
        ch_s2.bump(nc.scalar.activation(stat[:, 16:20], stat[:, 12:16], AF.Ln, bias=cst[:, 0:1]))
        ch_s2.wait(nc.scalar)
        ch_s2.bump(nc.scalar.activation(stat[:, 16:20], stat[:, 16:20], AF.Exp, scale=-0.5))
        ch_s2.wait(nc.vector)
        for b in range(GB):
            inst = nc.vector.tensor_scalar(xbf[:, b, :], xt[:, b, :], stat[:, 8+b:9+b], stat[:, 16+b:17+b],
                                           op0=ALU.subtract, op1=ALU.mult)
        ch_v3.bump(inst)
        ch_v3.wait(nc.tensor)
        ti = 0
        for kc in range(8):
            for b in range(GB):
                sl = ti % 2
                tp_c[sl].wait(nc.tensor)
                tp_p[sl].bump(nc.tensor.transpose(ps_tp[sl][:], xbf[:, b, kc*128:(kc+1)*128], idn[:]))
                tp_p[sl].wait(nc.vector)
                tp_c[sl].bump(nc.vector.tensor_copy(xfm[:, kc, b*128:(b+1)*128], ps_tp[sl][:]))
                ti += 1
        ch_w.wait(nc.tensor)
        for mc in range(16):
            sl = mc % 2
            mm_c[sl].wait(nc.tensor)
            for kc in range(8):
                nc.tensor.matmul(ps_mm[:, sl, :], wq[:, kc, mc*128:(mc+1)*128], xfm[:, kc, :],
                                 start=(kc == 0), stop=False)
            mm_p[sl].bump(nc.tensor.matmul(ps_mm[:, sl, :], bq[:, mc*128:(mc+1)*128], onesr[:],
                                           start=False, stop=True))
            mm_p[sl].wait(nc.vector)
            third, mloc = (0, mc) if mc < 8 else (1, mc-8)
            dst = qs_t if third == 0 else ks_t
            scale = 0.125 if third == 0 else 1.0
            for jsub in range(2):
                j = 2*mloc + jsub
                for b in range(GB):
                    inst = nc.vector.tensor_scalar(dst[:, b, j:2048:16], ps_mm[jsub*64:(jsub+1)*64, sl, b*128:(b+1)*128],
                                                   scale, None, op0=ALU.mult)
            mm_c[sl].bump(inst)
        for b in range(GB):
            for nn in range(2):
                sl = nn % 2
                mm_c[sl].wait(nc.tensor)
                for kc in range(8):
                    nc.tensor.matmul(ps_mm[:, sl, :], xfm[:, kc, b*128:(b+1)*128], wq[:, kc, 2048+nn*512:2048+(nn+1)*512],
                                     start=(kc == 0), stop=False)
                mm_p[sl].bump(nc.tensor.matmul(ps_mm[:, sl, :], onesr[:, 0:128], bq[:, 2048+nn*512:2048+(nn+1)*512],
                                               start=False, stop=True))
                mm_p[sl].wait(nc.vector)
                mm_c[sl].bump(nc.vector.tensor_copy(vtm[:, b, nn*512:(nn+1)*512], ps_mm[:, sl, :]))
        mm_c[0].wait(nc.gpsimd); mm_c[1].wait(nc.gpsimd)
        for b in range(GB):
            for kt in range(16):
                ch_fold.dma(nc.gpsimd.dma_start(out=vsp[:, b, kt, 0:64],
                            in_=vtm[kt*8:(kt+1)*8, b, :].rearrange("p (j dd) -> p j dd", dd=64)))
        for b in range(GB):
            ch_fold.wait(nc.tensor)
            for qsup in range(4):
                nkt = 4*qsup + 4
                for kt in range(nkt):
                    sl = kt % 2
                    nc.tensor.wait_ge(ch_ae.sem, slot_need[sl])
                    inst = nc.tensor.matmul(ps_s[:, sl, :], ks_t[:, b, kt*128:(kt+1)*128],
                                            qs_t[:, b, qsup*512:(qsup+1)*512], start=True, stop=True)
                    s_p[sl].bump(inst)
                    s_p[sl].wait(nc.scalar)
                    inst = nc.scalar.activation(atile[:, kt, :], ps_s[:, sl, :], AF.Exp, bias=cst[0:128, 1:2])
                    ch_ae.bump(inst)
                    slot_need[sl] = ch_ae.n
                ch_ae.wait(nc.vector)
                for mi in range(4):
                    kt = 4*qsup + mi
                    inst = nc.vector.tensor_tensor(atile[:, kt, :], atile[:, kt, :], mask[:, mi, :], op=ALU.mult)
                ch_am.bump(inst)
                ch_ae.wait(nc.tensor); ch_am.wait(nc.tensor); ch_ye.wait(nc.tensor)
                for kt in range(nkt):
                    inst = nc.tensor.matmul(ps_av[:, :], vsp[:, b, kt, :], atile[:, kt, :],
                                            start=(kt == 0), stop=(kt == nkt-1))
                ch_avd.bump(inst)
                ch_avd.wait(nc.vector)
                nc.vector.tensor_copy(ybuf[:, qsup*512:(qsup+1)*512], ps_av[0:64, :])
                inst = nc.vector.tensor_copy(dbuf[:, qsup*512:(qsup+1)*512], ps_av[64:65, :])
                ch_ye.bump(inst)
            ch_ye.wait(nc.sync)
            ch_yd.dma(nc.sync.dma_start(out=y_out[hloc, g*GB+b, :, :], in_=ybuf[:]))
            ch_yd.dma(nc.sync.dma_start(out=d_out[hloc, g*GB+b, :], in_=dbuf[:]))
            ch_yd.wait(nc.sync)
        nc.all_engine_barrier()
    ch_yd.wait(nc.sync)
    return nc


def host_constants():
    idn = np.eye(128, dtype=np.float32)
    masks = np.zeros((4, 128, 512), np.float32)
    for mi, o in enumerate((0, 128, 256, 384)):
        for k in range(128):
            masks[mi, k, :] = (np.arange(512) >= k + o)
    return idn, masks


def emit_ln(nc, xt, xbf, stat, cst_eps, ch):
    ch.bump(nc.vector.reduce_sum(stat[:, 0:1], xt[:], axis=AX.X))
    ch.bump(nc.scalar.activation(xbf[:], xt[:], AF.Square, accum_out=stat[:, 1:2]))
    ch.wait(nc.vector)
    ch.bump(nc.vector.tensor_scalar_mul(stat[:, 2:3], stat[:, 0:1], 1.0/1024.0))
    ch.wait(nc.vector)
    nc.vector.tensor_tensor(stat[:, 3:4], stat[:, 2:3], stat[:, 2:3], op=ALU.mult)
    ch.bump(nc.vector.tensor_scalar(stat[:, 4:5], stat[:, 1:2], 1.0/1024.0, None, op0=ALU.mult))
    ch.wait(nc.vector)
    ch.bump(nc.vector.tensor_tensor(stat[:, 4:5], stat[:, 4:5], stat[:, 3:4], op=ALU.subtract))
    ch.wait(nc.scalar)
    ch.bump(nc.scalar.activation(stat[:, 5:6], stat[:, 4:5], AF.Ln, bias=cst_eps))
    ch.wait(nc.scalar)
    ch.bump(nc.scalar.activation(stat[:, 5:6], stat[:, 5:6], AF.Exp, scale=-0.5))
    ch.wait(nc.vector)
    ch.bump(nc.vector.tensor_scalar(xbf[:], xt[:], stat[:, 2:3], stat[:, 5:6],
                                    op0=ALU.subtract, op1=ALU.mult))


def emit_transposes(nc, srcs, dsts, idn, ps_tp, tp_p, tp_c, state):
    for src, dst in zip(srcs, dsts):
        sl = state[0] % 2
        tp_c[sl].wait(nc.tensor)
        tp_p[sl].bump(nc.tensor.transpose(ps_tp[sl][:], src, idn[:]))
        tp_p[sl].wait(nc.vector)
        tp_c[sl].bump(nc.vector.tensor_copy(dst, ps_tp[sl][:]))
        state[0] += 1


def build_l2(nc, stack):
    yg  = nc.dram_tensor("yg", [16, NB, 64, NPOS], BF16, kind="ExternalInput").ap()
    den = nc.dram_tensor("den", [16, NB, NPOS], F32, kind="ExternalInput").ap()
    x   = nc.dram_tensor("x", [NB, NPOS, D], F32, kind="ExternalInput").ap()
    wp  = nc.dram_tensor("wp", [D, D], F32, kind="ExternalInput").ap()
    bp  = nc.dram_tensor("bp", [D], F32, kind="ExternalInput").ap()
    sel = nc.dram_tensor("sel", [8, 16, 128], F32, kind="ExternalInput").ap()
    x1  = nc.dram_tensor("x1", [NB, NPOS, D], F32, kind="ExternalOutput").ap()
    E = stack.enter_context
    wps  = E(nc.sbuf_tensor("wps", [128, 8, 1024], BF16))
    bps  = E(nc.sbuf_tensor("bps", [1, 1024], BF16))
    sels = E(nc.sbuf_tensor("sels", [16, 8, 128], F32))
    onesr= E(nc.sbuf_tensor("onesr", [1, 128], BF16))
    yfm  = E(nc.sbuf_tensor("yfm", [128, 8, NPOS], BF16))
    dent = E(nc.sbuf_tensor("dent", [16, NPOS], F32))
    recip= E(nc.sbuf_tensor("recip", [16, NPOS], F32))
    ynrm = E(nc.sbuf_tensor("ynrm", [128, 8, NPOS], BF16))
    xres = E(nc.sbuf_tensor("xres", [128, 2, 1024], F32))
    x1t  = E(nc.sbuf_tensor("x1t", [128, 2, 1024], F32))
    ps_bc = E(nc.psum_tensor("ps_bc", [128, 512], F32))
    ps_o0 = E(nc.psum_tensor("ps_o0", [128, 512], F32))
    ps_o1 = E(nc.psum_tensor("ps_o1", [128, 512], F32))
    ch_w = Chain(nc, "l2w", stack); ch_y = Chain(nc, "l2y", stack)
    ch_b = Chain(nc, "l2b", stack); ch_n = Chain(nc, "l2n", stack)
    ch_mm = Chain(nc, "l2mm", stack); ch_ev = Chain(nc, "l2ev", stack)
    ch_od = Chain(nc, "l2od", stack)
    for kc in range(8):
        ch_w.dma(nc.gpsimd.dma_start(out=wps[:, kc, :], in_=wp[kc*128:(kc+1)*128, :]))
    ch_w.dma(nc.gpsimd.dma_start(out=bps[:], in_=bp[None, :]))
    ch_w.dma(nc.sync.dma_start(out=sels[:], in_=sel.transpose([1, 0, 2])))
    ch_w.bump(nc.vector.memset(onesr[:], 1.0))
    ch_w.wait(nc.sync); ch_w.wait(nc.vector); ch_w.wait(nc.tensor); ch_w.wait(nc.scalar); ch_w.wait(nc.gpsimd)
    nc.all_engine_barrier()
    for b in range(NB):
        for j in range(8):
            ch_y.dma(nc.sync.dma_start(out=yfm[:, j, :], in_=yg[2*j:2*j+2, b, :, :]))
        ch_y.dma(nc.sync.dma_start(out=dent[:], in_=den[:, b, :]))
        for pt in range(2):
            ch_y.dma(nc.sync.dma_start(out=xres[:, pt, :], in_=x[b, pt*128:(pt+1)*128, :]))
        ch_y.wait(nc.vector)
        ch_n.bump(nc.vector.reciprocal(recip[:], dent[:]))
        ch_n.wait(nc.tensor)
        for j in range(8):
            ch_b.bump(nc.tensor.matmul(ps_bc[:, 0:NPOS], sels[:, j, :], recip[:], start=True, stop=True))
            ch_b.wait(nc.vector)
            ch_n.bump(nc.vector.tensor_tensor(ynrm[:, j, :], yfm[:, j, :], ps_bc[:, 0:NPOS], op=ALU.mult))
            ch_n.wait(nc.tensor)
        for pt in range(2):
            ch_ev.wait(nc.tensor)
            for nn, ps in ((0, ps_o0), (1, ps_o1)):
                for kc in range(8):
                    nc.tensor.matmul(ps[:], ynrm[:, kc, pt*128:(pt+1)*128], wps[:, kc, nn*512:(nn+1)*512],
                                     start=(kc == 0), stop=False)
                ch_mm.bump(nc.tensor.matmul(ps[:], onesr[:], bps[:, nn*512:(nn+1)*512], start=False, stop=True))
            ch_mm.wait(nc.vector)
            nc.vector.tensor_add(x1t[:, pt, 0:512], ps_o0[:], xres[:, pt, 0:512])
            ch_ev.bump(nc.vector.tensor_add(x1t[:, pt, 512:1024], ps_o1[:], xres[:, pt, 512:1024]))
            ch_ev.wait(nc.sync)
            ch_od.dma(nc.sync.dma_start(out=x1[b, pt*128:(pt+1)*128, :], in_=x1t[:, pt, :]))
        nc.all_engine_barrier()
        ch_od.wait(nc.sync)
    nc.all_engine_barrier()
    return nc


def build_l3(nc, stack):
    x1  = nc.dram_tensor("x1", [NB, NPOS, D], F32, kind="ExternalInput").ap()
    w3  = nc.dram_tensor("w3", [D, 3*D], F32, kind="ExternalInput").ap()
    u3  = nc.dram_tensor("u3", [D, 3*D], F32, kind="ExternalInput").ap()
    b3  = nc.dram_tensor("b3", [3*D], F32, kind="ExternalInput").ap()
    wo  = nc.dram_tensor("wo", [D, D], F32, kind="ExternalInput").ap()
    uo  = nc.dram_tensor("uo", [D, D], F32, kind="ExternalInput").ap()
    bo  = nc.dram_tensor("bo", [D], F32, kind="ExternalInput").ap()
    inc = nc.dram_tensor("inc", [D], F32, kind="ExternalInput").ap()
    idn_in = nc.dram_tensor("idn", [128, 128], F32, kind="ExternalInput").ap()
    x2  = nc.dram_tensor("x2", [NB, NPOS, D], F32, kind="ExternalOutput").ap()
    E = stack.enter_context
    w3s = E(nc.sbuf_tensor("w3s", [128, 8, 3072], BF16))
    u3s = E(nc.sbuf_tensor("u3s", [128, 8, 3072], BF16))
    b3s = E(nc.sbuf_tensor("b3s", [1, 3072], BF16))
    wos = E(nc.sbuf_tensor("wos", [128, 8, 1024], BF16))
    uos = E(nc.sbuf_tensor("uos", [128, 8, 1024], BF16))
    bos = E(nc.sbuf_tensor("bos", [1, 1024], BF16))
    incr= E(nc.sbuf_tensor("incr", [1, 1024], F32))
    ones32 = E(nc.sbuf_tensor("ones32", [1, 128], F32))
    idn = E(nc.sbuf_tensor("idn3", [128, 128], BF16))
    onesr=E(nc.sbuf_tensor("onesr3", [1, 128], BF16))
    cst = E(nc.sbuf_tensor("cst3", [128, 1], F32))
    xt  = E(nc.sbuf_tensor("xt3", [128, 2, 1024], F32))
    stat= E(nc.sbuf_tensor("stat3", [128, 12], F32))
    xbf = E(nc.sbuf_tensor("xbf3", [128, 2, 1024], BF16))
    xsf = E(nc.sbuf_tensor("xsf", [128, 8, 256], BF16))
    c_tm= E(nc.sbuf_tensor("c_tm", [128, 2, 1024], F32))
    cfm = E(nc.sbuf_tensor("cfm", [128, 8, 256], BF16))
    gat = E(nc.sbuf_tensor("gat", [128, 2, 3072], F32))
    cuo = E(nc.sbuf_tensor("cuo", [128, 2, 1024], F32))
    ht  = E(nc.sbuf_tensor("ht", [128, 1024], F32))
    ps_g  = E(nc.psum_tensor("ps_g", [128, 3072], F32))
    ps_t0 = E(nc.psum_tensor("ps_t0", [128, 128], BF16))
    ps_t1 = E(nc.psum_tensor("ps_t1", [128, 128], BF16))
    ps_tp = [ps_t0, ps_t1]
    ch_w = Chain(nc, "l3w", stack); ch_x = Chain(nc, "l3x", stack)
    ch_ln = Chain(nc, "l3ln", stack)
    tp_p = [Chain(nc, f"l3tpp{i}", stack) for i in range(2)]
    tp_c = [Chain(nc, f"l3tpc{i}", stack) for i in range(2)]
    ch_cc = Chain(nc, "l3cc", stack)
    ch_mm = Chain(nc, "l3mm", stack)
    ch_ge = Chain(nc, "l3ge", stack)
    ch_cu = Chain(nc, "l3cu", stack)
    ch_h  = Chain(nc, "l3h", stack)
    ch_ta = Chain(nc, "l3ta", stack)
    ch_od = Chain(nc, "l3od", stack)
    tstate = [0]

    for kc in range(8):
        ch_w.dma(nc.gpsimd.dma_start(out=w3s[:, kc, :], in_=w3[kc*128:(kc+1)*128, :]))
        ch_w.dma(nc.gpsimd.dma_start(out=u3s[:, kc, :], in_=u3[kc*128:(kc+1)*128, :]))
        ch_w.dma(nc.gpsimd.dma_start(out=wos[:, kc, :], in_=wo[kc*128:(kc+1)*128, :]))
        ch_w.dma(nc.gpsimd.dma_start(out=uos[:, kc, :], in_=uo[kc*128:(kc+1)*128, :]))
    ch_w.dma(nc.gpsimd.dma_start(out=b3s[:], in_=b3[None, :]))
    ch_w.dma(nc.gpsimd.dma_start(out=bos[:], in_=bo[None, :]))
    ch_w.dma(nc.gpsimd.dma_start(out=idn[:], in_=idn_in[:]))
    ch_w.dma(nc.sync.dma_start(out=incr[:], in_=inc[None, :]))
    nc.vector.memset(onesr[:], 1.0)
    nc.vector.memset(ones32[:], 1.0)
    ch_w.bump(nc.vector.memset(cst[:], LN_EPS))
    ch_w.wait(nc.sync); ch_w.wait(nc.vector); ch_w.wait(nc.tensor); ch_w.wait(nc.scalar); ch_w.wait(nc.gpsimd)
    nc.all_engine_barrier()

    for nn in range(2):
        ch_cc.bump(nc.tensor.matmul(ps_g[:, nn*512:(nn+1)*512], ones32[:], incr[:, nn*512:(nn+1)*512],
                                    start=True, stop=True))
    ch_cc.wait(nc.vector)
    nc.vector.tensor_copy(c_tm[:, 0, :], ps_g[:, 0:1024])
    nc.vector.tensor_copy(c_tm[:, 1, :], ps_g[:, 0:1024])
    nc.all_engine_barrier()

    def ln_tr(t, dest):
        for pt in range(2):
            ch_x.dma(nc.sync.dma_start(out=xt[:, pt, :], in_=x1[t, pt*128:(pt+1)*128, :]))
        ch_x.wait(nc.vector); ch_x.wait(nc.scalar)
        for pt in range(2):
            emit_ln(nc, xt[:, pt, :], xbf[:, pt, :], stat[:, 6*pt:6*pt+6], cst[:], ch_ln)
        ch_ln.wait(nc.tensor)
        emit_transposes(nc,
            [xbf[:, pt, kc*128:(kc+1)*128] for pt in range(2) for kc in range(8)],
            [dest[:, kc, pt*128:(pt+1)*128] for pt in range(2) for kc in range(8)],
            idn, ps_tp, tp_p, tp_c, tstate)

    for t in range(NB):
        ln_tr(t, xsf)
        ch_cc.bump(nc.vector.tensor_copy(xbf[:], c_tm[:]))
        ch_cc.wait(nc.tensor)
        emit_transposes(nc,
            [xbf[:, pt, kc*128:(kc+1)*128] for pt in range(2) for kc in range(8)],
            [cfm[:, kc, pt*128:(pt+1)*128] for pt in range(2) for kc in range(8)],
            idn, ps_tp, tp_p, tp_c, tstate)
        tp_c[0].wait(nc.tensor); tp_c[1].wait(nc.tensor)
        for pt in range(2):
            ch_ge.wait(nc.tensor)
            for n6 in range(6):
                for kc in range(8):
                    nc.tensor.matmul(ps_g[:, n6*512:(n6+1)*512], cfm[:, kc, pt*128:(pt+1)*128],
                                     u3s[:, kc, n6*512:(n6+1)*512], start=(kc == 0), stop=False)
                for kc in range(8):
                    nc.tensor.matmul(ps_g[:, n6*512:(n6+1)*512], xsf[:, kc, pt*128:(pt+1)*128],
                                     w3s[:, kc, n6*512:(n6+1)*512], start=False, stop=False)
                ch_mm.bump(nc.tensor.matmul(ps_g[:, n6*512:(n6+1)*512], onesr[:],
                                            b3s[:, n6*512:(n6+1)*512], start=False, stop=True))
            ch_mm.wait(nc.scalar)
            nc.scalar.activation(gat[:, pt, 0:2048], ps_g[:, 0:2048], AF.Sigmoid)
            ch_ge.bump(nc.scalar.activation(gat[:, pt, 2048:3072], ps_g[:, 2048:3072], AF.Tanh))
        ch_ge.wait(nc.vector)
        ch_cu.bump(nc.vector.tensor_tensor(gat[:, :, 1024:2048], gat[:, :, 1024:2048], gat[:, :, 2048:3072], op=ALU.mult))
        ch_cu.bump(nc.vector.tensor_tensor(c_tm[:], c_tm[:], gat[:, :, 0:1024], op=ALU.mult))
        ch_cu.wait(nc.vector)
        ch_cu.bump(nc.vector.tensor_tensor(c_tm[:], c_tm[:], gat[:, :, 1024:2048], op=ALU.add))
        nc.all_engine_barrier()
    ch_cc.bump(nc.vector.tensor_copy(xbf[:], c_tm[:]))
    ch_cc.wait(nc.tensor)
    emit_transposes(nc,
        [xbf[:, pt, kc*128:(kc+1)*128] for pt in range(2) for kc in range(8)],
        [cfm[:, kc, pt*128:(pt+1)*128] for pt in range(2) for kc in range(8)],
        idn, ps_tp, tp_p, tp_c, tstate)
    tp_c[0].wait(nc.tensor); tp_c[1].wait(nc.tensor)
    for pt in range(2):
        for nn in range(2):
            for kc in range(8):
                nc.tensor.matmul(ps_g[:, pt*1024+nn*512:pt*1024+(nn+1)*512], cfm[:, kc, pt*128:(pt+1)*128],
                                 uos[:, kc, nn*512:(nn+1)*512], start=(kc == 0), stop=False)
            ch_mm.bump(nc.tensor.matmul(ps_g[:, pt*1024+nn*512:pt*1024+(nn+1)*512], onesr[:],
                                        bos[:, nn*512:(nn+1)*512], start=False, stop=True))
    ch_mm.wait(nc.vector)
    nc.vector.tensor_copy(cuo[:, 0, :], ps_g[:, 0:1024])
    nc.vector.tensor_copy(cuo[:, 1, :], ps_g[:, 1024:2048])
    nc.all_engine_barrier()
    for t in range(NB):
        ln_tr(t, xsf)
        tp_c[0].wait(nc.tensor); tp_c[1].wait(nc.tensor)
        for pt in range(2):
            for nn in range(2):
                for kc in range(8):
                    inst = nc.tensor.matmul(ps_g[:, nn*512:(nn+1)*512], xsf[:, kc, pt*128:(pt+1)*128],
                                     wos[:, kc, nn*512:(nn+1)*512], start=(kc == 0), stop=(kc == 7))
                ch_mm.bump(inst)
            ch_mm.wait(nc.vector)
            ch_h.bump(nc.vector.tensor_add(ht[:], ps_g[:, 0:1024], cuo[:, pt, :]))
            ch_h.wait(nc.scalar)
            x2t = gat[:, 0, 0:1024]
            ch_ta.bump(nc.scalar.activation(x2t, ht[:], AF.Tanh))
            ch_ta.wait(nc.vector)
            ch_od.bump(nc.vector.tensor_add(x2t, x2t, xt[:, pt, :]))
            ch_od.wait(nc.sync)
            ch_od.dma(nc.sync.dma_start(out=x2[t, pt*128:(pt+1)*128, :], in_=x2t))
            nc.all_engine_barrier()
        ch_od.wait(nc.sync)
    nc.all_engine_barrier()
    return nc


def host_sel():
    sel = np.zeros((8, 16, 128), np.float32)
    for j in range(8):
        for h2 in range(2):
            sel[j, 2*j+h2, h2*64:(h2+1)*64] = 1.0
    return sel


def build_l4(nc, stack):
    x2  = nc.dram_tensor("x2", [NB, NPOS, D], F32, kind="ExternalInput").ap()
    wq  = nc.dram_tensor("wq", [D, 3*D], F32, kind="ExternalInput").ap()
    bq  = nc.dram_tensor("bq", [3*D], F32, kind="ExternalInput").ap()
    wo  = nc.dram_tensor("wo", [D, D], F32, kind="ExternalInput").ap()
    bo  = nc.dram_tensor("bo", [D], F32, kind="ExternalInput").ap()
    idn_in = nc.dram_tensor("idn", [128, 128], F32, kind="ExternalInput").ap()
    x3  = nc.dram_tensor("x3", [NB, NPOS, D], F32, kind="ExternalOutput").ap()
    E = stack.enter_context
    wqs = E(nc.sbuf_tensor("wqs4", [128, 8, 3072], BF16))
    bqs = E(nc.sbuf_tensor("bqs4", [1, 3072], BF16))
    wos = E(nc.sbuf_tensor("wos4", [128, 8, 1024], BF16))
    bos = E(nc.sbuf_tensor("bos4", [1, 1024], BF16))
    idn = E(nc.sbuf_tensor("idn4", [128, 128], BF16))
    onesr=E(nc.sbuf_tensor("onesr4", [1, 128], BF16))
    cst = E(nc.sbuf_tensor("cst4", [128, 1], F32))
    xt  = E(nc.sbuf_tensor("xt4", [128, 1024], F32))
    stat= E(nc.sbuf_tensor("stat4", [128, 6], F32))
    xbf = E(nc.sbuf_tensor("xbf4", [128, 1024], BF16))
    xfm = E(nc.sbuf_tensor("xfm4", [128, 8, 128], BF16))
    qkv = E(nc.sbuf_tensor("qkv4", [128, 3072], BF16))
    stm = E(nc.sbuf_tensor("stm4", [128, 256], F32))
    etm = E(nc.sbuf_tensor("etm4", [128, 256], F32))
    mx  = E(nc.sbuf_tensor("mx4", [128, 16], F32))
    sm  = E(nc.sbuf_tensor("sm4", [128, 16], F32))
    atm = E(nc.sbuf_tensor("atm4", [128, 256], BF16))
    prd = [E(nc.sbuf_tensor(f"prd4_{i}", [128, 1024], BF16)) for i in range(2)]
    ctxb= E(nc.sbuf_tensor("ctxb4", [128, 1024], BF16))
    ctxf= E(nc.sbuf_tensor("ctxf4", [128, 8, 128], BF16))
    x3t = E(nc.sbuf_tensor("x3t4", [128, 1024], F32))
    ps_q0 = E(nc.psum_tensor("ps_q0", [128, 512], F32))
    ps_q1 = E(nc.psum_tensor("ps_q1", [128, 512], F32))
    ps_c  = E(nc.psum_tensor("ps_c4", [128, 1024], F32))
    ps_t0 = E(nc.psum_tensor("ps_t04", [128, 128], BF16))
    ps_t1 = E(nc.psum_tensor("ps_t14", [128, 128], BF16))
    ps_tp = [ps_t0, ps_t1]
    ps_qs = [ps_q0, ps_q1]
    ch_w = Chain(nc, "l4w", stack); ch_x = Chain(nc, "l4x", stack); ch_ln = Chain(nc, "l4ln", stack)
    tp_p = [Chain(nc, f"l4tpp{i}", stack) for i in range(2)]
    tp_c = [Chain(nc, f"l4tpc{i}", stack) for i in range(2)]
    ch_mm = Chain(nc, "l4mm", stack); ch_qe = Chain(nc, "l4qe", stack)
    ch_p  = Chain(nc, "l4p", stack)
    ch_r  = Chain(nc, "l4r", stack)
    ch_sx = Chain(nc, "l4sx", stack)
    ch_cm = Chain(nc, "l4cm", stack)
    ch_ce = Chain(nc, "l4ce", stack)
    ch_od = Chain(nc, "l4od", stack)
    tstate = [0]
    for kc in range(8):
        ch_w.dma(nc.gpsimd.dma_start(out=wqs[:, kc, :], in_=wq[kc*128:(kc+1)*128, :]))
        ch_w.dma(nc.gpsimd.dma_start(out=wos[:, kc, :], in_=wo[kc*128:(kc+1)*128, :]))
    ch_w.dma(nc.gpsimd.dma_start(out=bqs[:], in_=bq[None, :]))
    ch_w.dma(nc.gpsimd.dma_start(out=bos[:], in_=bo[None, :]))
    ch_w.dma(nc.gpsimd.dma_start(out=idn[:], in_=idn_in[:]))
    nc.vector.memset(onesr[:], 1.0)
    ch_w.bump(nc.vector.memset(cst[:], LN_EPS))
    ch_w.wait(nc.sync); ch_w.wait(nc.vector); ch_w.wait(nc.tensor); ch_w.wait(nc.scalar); ch_w.wait(nc.gpsimd)
    nc.all_engine_barrier()

    def emit_tp(srcs, dsts):
        for src, dst in zip(srcs, dsts):
            sl = tstate[0] % 2
            tp_c[sl].wait(nc.tensor)
            tp_p[sl].bump(nc.tensor.transpose(ps_tp[sl][:], src, idn[:]))
            tp_p[sl].wait(nc.vector)
            tp_c[sl].bump(nc.vector.tensor_copy(dst, ps_tp[sl][:]))
            tstate[0] += 1

    for ti in range(NTOK // 128):
        b, pt = ti // 2, ti % 2
        ch_x.dma(nc.sync.dma_start(out=xt[:], in_=x2[b, pt*128:(pt+1)*128, :]))
        ch_x.wait(nc.vector); ch_x.wait(nc.scalar)
        emit_ln(nc, xt[:], xbf[:], stat[:], cst[:], ch_ln)
        ch_ln.wait(nc.tensor)
        emit_tp([xbf[:, kc*128:(kc+1)*128] for kc in range(8)],
                [xfm[:, kc, :] for kc in range(8)])
        tp_c[0].wait(nc.tensor); tp_c[1].wait(nc.tensor)
        for n6 in range(6):
            sl = n6 % 2
            ch_qe.wait(nc.tensor) if n6 >= 2 else None
            for kc in range(8):
                nc.tensor.matmul(ps_qs[sl][:], xfm[:, kc, :], wqs[:, kc, n6*512:(n6+1)*512],
                                 start=(kc == 0), stop=False)
            ch_mm.bump(nc.tensor.matmul(ps_qs[sl][:], onesr[:], bqs[:, n6*512:(n6+1)*512],
                                        start=False, stop=True))
            ch_mm.wait(nc.vector)
            if n6 < 2:
                ch_qe.bump(nc.vector.tensor_scalar(qkv[:, n6*512:(n6+1)*512], ps_qs[sl][:], 0.125, None, op0=ALU.mult))
            else:
                ch_qe.bump(nc.vector.tensor_copy(qkv[:, n6*512:(n6+1)*512], ps_qs[sl][:]))
        q = qkv[:, 0:1024]; k = qkv[:, 1024:2048]; v = qkv[:, 2048:3072]
        stm3 = stm.rearrange("p (h d) -> p h d", d=16)
        ch_qe.wait(nc.vector); ch_qe.wait(nc.gpsimd)
        rn = [0, 0]
        for dl in range(16):
            sl = dl % 2
            eng = nc.vector if dl % 2 == 0 else nc.gpsimd
            eng.wait_ge(ch_r.sem, rn[sl])
            w1 = (16 - dl) * 64
            eng.tensor_tensor(prd[sl][:, 0:w1], q[:, 0:w1], k[:, dl*64:1024], op=ALU.mult)
            if dl > 0:
                inst = eng.tensor_tensor(prd[sl][:, w1:1024], q[:, w1:1024], k[:, 0:dl*64], op=ALU.mult)
            else:
                inst = eng.tensor_tensor(prd[sl][:, 1020:1024], q[:, 1020:1024], k[:, 1020:1024], op=ALU.mult)
            ch_p.bump(inst)
            nc.vector.wait_ge(ch_p.sem, ch_p.n)
            ch_r.bump(nc.vector.tensor_reduce(stm3[:, :, dl:dl+1],
                        prd[sl][:].rearrange("p (h d) -> p h d", d=64), axis=AX.X, op=ALU.add))
            rn[sl] = ch_r.n
        ch_r.wait(nc.vector)
        ch_sx.bump(nc.vector.reduce_max(mx[:, :, None], stm3[:, :, :], axis=AX.X))
        ch_sx.wait(nc.vector)
        ch_sx.bump(nc.vector.tensor_tensor(etm.rearrange("p (h d) -> p h d", d=16), stm3[:, :, :],
                                           mx[:, :, None].broadcast_to([128, 16, 16]), op=ALU.subtract))
        ch_sx.wait(nc.scalar)
        ch_sx.bump(nc.scalar.activation(etm[:], etm[:], AF.Exp))
        ch_sx.wait(nc.vector)
        ch_sx.bump(nc.vector.reduce_sum(sm[:, :, None], etm.rearrange("p (h d) -> p h d", d=16), axis=AX.X))
        ch_sx.wait(nc.vector)
        ch_sx.bump(nc.vector.reciprocal(sm[:], sm[:]))
        ch_sx.wait(nc.vector)
        ch_sx.bump(nc.vector.tensor_tensor(atm.rearrange("p (h d) -> p h d", d=16),
                                           etm.rearrange("p (h d) -> p h d", d=16),
                                           sm[:, :, None].broadcast_to([128, 16, 16]), op=ALU.mult))
        atm3 = atm.rearrange("p (h d) -> p h d", d=16)
        ch_sx.wait(nc.vector); ch_sx.wait(nc.gpsimd)
        cmn = [0, 0]
        for dl in range(16):
            sl = dl % 2
            eng = nc.vector if dl % 2 == 0 else nc.gpsimd
            eng.wait_ge(ch_cm.sem, cmn[sl])
            w1 = (16 - dl) * 64
            p3a = prd[sl][:, 0:w1].rearrange("p (h d) -> p h d", d=64)
            inst = eng.tensor_tensor(p3a, v[:, dl*64:1024].rearrange("p (h d) -> p h d", d=64),
                                    atm3[:, 0:(16-dl), dl:dl+1].broadcast_to([128, 16-dl, 64]), op=ALU.mult)
            if dl > 0:
                inst = eng.tensor_tensor(prd[sl][:, w1:1024].rearrange("p (h d) -> p h d", d=64),
                                        v[:, 0:dl*64].rearrange("p (h d) -> p h d", d=64),
                                        atm3[:, (16-dl):16, dl:dl+1].broadcast_to([128, dl, 64]), op=ALU.mult)
            ch_p.bump(inst)
            nc.tensor.wait_ge(ch_p.sem, ch_p.n)
            for nn in range(2):
                inst = nc.tensor.matmul(ps_c[:, nn*512:(nn+1)*512], idn[:], prd[sl][:, nn*512:(nn+1)*512],
                                        start=(dl == 0), stop=(dl == 15))
            ch_cm.bump(inst)
            cmn[sl] = ch_cm.n
        ch_cm.wait(nc.vector)
        ch_ce.bump(nc.vector.tensor_copy(ctxb[:], ps_c[:]))
        ch_ce.wait(nc.tensor)
        emit_tp([ctxb[:, kc*128:(kc+1)*128] for kc in range(8)],
                [ctxf[:, kc, :] for kc in range(8)])
        tp_c[0].wait(nc.tensor); tp_c[1].wait(nc.tensor)
        for nn in range(2):
            sl = nn
            for kc in range(8):
                nc.tensor.matmul(ps_qs[sl][:], ctxf[:, kc, :], wos[:, kc, nn*512:(nn+1)*512],
                                 start=(kc == 0), stop=False)
            ch_mm.bump(nc.tensor.matmul(ps_qs[sl][:], onesr[:], bos[:, nn*512:(nn+1)*512],
                                        start=False, stop=True))
        ch_mm.wait(nc.vector)
        nc.vector.tensor_add(x3t[:, 0:512], ps_q0[:], xt[:, 0:512])
        ch_od.bump(nc.vector.tensor_add(x3t[:, 512:1024], ps_q1[:], xt[:, 512:1024]))
        ch_od.wait(nc.sync)
        ch_od.dma(nc.sync.dma_start(out=x3[b, pt*128:(pt+1)*128, :], in_=x3t[:]))
        nc.all_engine_barrier()
    ch_od.wait(nc.sync)
    nc.all_engine_barrier()
    return nc


def host_l4_weights(Wqkv, bqkv):
    idx = []
    for third in range(3):
        for h in range(16):
            idx.extend(range(h*192 + third*64, h*192 + third*64 + 64))
    idx = np.array(idx)
    return np.ascontiguousarray(Wqkv[:, idx]), np.ascontiguousarray(bqkv[idx])


def l5_band_vectors(weights):
    wband = np.zeros((12, 3, 4096), np.float32)
    for slot in range(12):
        s = slot
        is_g = slot in (6, 11)
        for fp in range(4096):
            if slot < 7:
                f = fp
            else:
                j = fp & 31; u = fp >> 5
                f = j*128 + u
            S_ref = 1 << s
            q = f >> (s+1); r = (f >> s) & 1; u_ref = f & (S_ref - 1)
            widx = q * S_ref + u_ref
            w = weights[s, widx]
            wband[slot, 0, fp] = w[0, 0] if r == 0 else w[1, 1]
            if is_g:
                wband[slot, 1, fp] = w[1, 0] if r == 0 else 0.0
                wband[slot, 2, fp] = w[0, 1] if r == 1 else 0.0
            else:
                wband[slot, 1, fp] = w[0, 1] if r == 0 else 0.0
                wband[slot, 2, fp] = w[1, 0] if r == 1 else 0.0
    return wband

def l5_band_stride(slot):
    return 1 << slot if slot < 7 else 1 << (slot - 7)

def l5_make_mats(wband, slot):
    S = l5_band_stride(slot)
    mats = np.zeros((32, 128, 128), np.float32)
    for c in range(32):
        for p in range(128):
            fp = c*128 + p
            mats[c, p, p] = wband[slot, 0, fp]
            if p + S < 128:
                mats[c, p, p+S] = wband[slot, 1, fp]
            if p - S >= 0:
                mats[c, p, p-S] = wband[slot, 2, fp]
    return mats

def l5_compose_host(wband, lo):
    slots = range(0, 7) if lo else range(7, 12)
    slots = list(slots)
    P = l5_make_mats(wband, slots[-1])
    for s in reversed(slots[:-1]):
        H = l5_make_mats(wband, s)
        P = np.einsum('cpm,cpn->cmn', H, P)
    return P

def l5_perm_idx():
    fp = np.arange(4096)
    j = fp & 31; u = fp >> 5
    return j*128 + u


def build_l5(nc, stack):
    x3  = nc.dram_tensor("x3", [NB, NPOS, D], F32, kind="ExternalInput").ap()
    wexp= nc.dram_tensor("wexp", [D, 4096], F32, kind="ExternalInput").ap()
    bexp= nc.dram_tensor("bexp", [4096], F32, kind="ExternalInput").ap()
    wband=nc.dram_tensor("wband", [2, 12, 3, 128, 32], F32, kind="ExternalInput").ap()
    upb = nc.dram_tensor("upb", [128, 32], F32, kind="ExternalInput").ap()
    wcon= nc.dram_tensor("wcon", [4096, D], F32, kind="ExternalInput").ap()
    bcon= nc.dram_tensor("bcon", [D], F32, kind="ExternalInput").ap()
    dbias=nc.dram_tensor("dbias", [D], F32, kind="ExternalInput").ap()
    idn_in = nc.dram_tensor("idn", [128, 128], F32, kind="ExternalInput").ap()
    x4  = nc.dram_tensor("x4", [NB, NPOS, D], F32, kind="ExternalOutput").ap()
    hfin = nc.dram_tensor("hfin", [128, 32, NTOK], BF16, kind="Internal").ap()
    E = stack.enter_context
    idn = E(nc.sbuf_tensor("idn5", [128, 128], BF16))
    ones5=E(nc.sbuf_tensor("ones5", [1, 512], BF16))
    cst = E(nc.sbuf_tensor("cst5", [128, 1], F32))
    upbs= E(nc.sbuf_tensor("upbs", [128, 32], F32))
    Lmat= E(nc.sbuf_tensor("Lmat", [128, 4, 32, 128], BF16))
    h1  = E(nc.sbuf_tensor("h1", [128, 32, 512], BF16))
    hp  = E(nc.sbuf_tensor("hp", [128, 32, 512], BF16))
    xt  = E(nc.sbuf_tensor("xt5", [128, 1024], F32))
    stat= E(nc.sbuf_tensor("stat5", [128, 6], F32))
    xbf = E(nc.sbuf_tensor("xbf5", [128, 1024], BF16))
    xfm = E(nc.sbuf_tensor("xfm5", [128, 8, 512], BF16))
    h0c = E(nc.sbuf_tensor("h0c", [128, 2, 512], BF16))
    wv  = E(nc.sbuf_tensor("wv5", [128, 3, 32], BF16))
    bexps=E(nc.sbuf_tensor("bexps", [1, 4096], BF16))
    ps_a = E(nc.psum_tensor("ps_a5", [128, 512], F32))
    ps_b = E(nc.psum_tensor("ps_b5", [128, 512], F32))
    ps_t0= E(nc.psum_tensor("ps_t05", [128, 128], BF16))
    ps_t1= E(nc.psum_tensor("ps_t15", [128, 128], BF16))
    ps_c0= E(nc.psum_tensor("ps_c05", [128, 128], F32))
    ps_c1= E(nc.psum_tensor("ps_c15", [128, 128], F32))
    ps_tp = [ps_t0, ps_t1]
    ps_cc = [ps_c0, ps_c1]
    ch_w = Chain(nc, "l5w", stack); ch_x = Chain(nc, "l5x", stack); ch_ln = Chain(nc, "l5ln", stack)
    tp_p = [Chain(nc, f"l5tpp{i}", stack) for i in range(2)]
    tp_c = [Chain(nc, f"l5tpc{i}", stack) for i in range(2)]
    ch_bv = Chain(nc, "l5bv", stack)
    ch_bs = Chain(nc, "l5bs", stack)
    ch_ba = Chain(nc, "l5ba", stack)
    cc_p = [Chain(nc, f"l5ccp{i}", stack) for i in range(2)]
    cc_c = [Chain(nc, f"l5ccc{i}", stack) for i in range(2)]
    ch_mm = Chain(nc, "l5mm", stack)
    ch_ev = Chain(nc, "l5ev", stack)
    ch_pm = Chain(nc, "l5pm", stack)
    ch_hf = Chain(nc, "l5hf", stack)
    ch_od = Chain(nc, "l5od", stack)
    tstate = [0]

    ch_w.dma(nc.gpsimd.dma_start(out=idn[:], in_=idn_in[:]))
    ch_w.dma(nc.sync.dma_start(out=upbs[:], in_=upb[:]))
    ch_w.dma(nc.gpsimd.dma_start(out=bexps[:], in_=bexp[None, :]))
    nc.vector.memset(ones5[:], 1.0)
    ch_w.bump(nc.vector.memset(cst[:], LN_EPS))
    ch_w.wait(nc.sync); ch_w.wait(nc.vector); ch_w.wait(nc.tensor); ch_w.wait(nc.scalar); ch_w.wait(nc.gpsimd)
    nc.all_engine_barrier()

    bufA = h1[:, :, 0:128]; bufB = h1[:, :, 128:256]
    Hbuf = h1[:, :, 256:384]; t0 = h1[:, :, 384:512]; t1 = hp[:, :, 0:128]
    def construct(bf, slot, dest):
        S = l5_band_stride(slot)
        for band in range(3):
            ch_bv.dma(nc.gpsimd.dma_start(out=wv[:, band, :], in_=wband[bf, slot, band, :, :]))
        ch_bv.wait(nc.gpsimd); ch_ba.wait(nc.gpsimd)
        for band, (dst, off) in enumerate(((t0, 0), (t1, S), (dest, -S))):
            ch_bs.bump(nc.gpsimd.affine_select(out=dst, in_=wv[:, band, :, None].broadcast_to([128, 32, 128]),
                        compare_op=ALU.is_equal, fill=0.0, base=-off,
                        pattern=[[0, 32], [1, 128]], channel_multiplier=-1))
        ch_bs.wait(nc.vector)
        nc.vector.tensor_tensor(dest, dest, t0, op=ALU.add)
        ch_ba.bump(nc.vector.tensor_tensor(dest, dest, t1, op=ALU.add))
    for bf in range(2):
        for lo in range(2):
            slots = list(range(0, 7)) if lo == 0 else list(range(7, 12))
            construct(bf, slots[-1], bufA)
            cur, nxt = bufA, bufB
            for s in reversed(slots[:-1]):
                construct(bf, s, Hbuf)
                ch_ba.wait(nc.tensor)
                for c in range(32):
                    sl = c % 2
                    cc_c[sl].wait(nc.tensor)
                    cc_p[sl].bump(nc.tensor.matmul(ps_cc[sl][:], Hbuf[:, c, :], cur[:, c, :],
                                                   start=True, stop=True))
                    cc_p[sl].wait(nc.vector)
                    cc_c[sl].bump(nc.vector.tensor_copy(nxt[:, c, :], ps_cc[sl][:]))
                cur, nxt = nxt, cur
                cc_c[0].wait(nc.gpsimd); cc_c[1].wait(nc.gpsimd)
            cc_c[0].wait(nc.vector); cc_c[1].wait(nc.vector)
            ch_ba.wait(nc.vector)
            seti = bf*2 + lo
            ch_ba.bump(nc.vector.tensor_copy(Lmat[:, seti, :, :], cur))
            ch_ba.wait(nc.tensor); ch_ba.wait(nc.gpsimd)
    nc.all_engine_barrier()

    with contextlib.ExitStack() as s1:
        wexps = s1.enter_context(nc.sbuf_tensor("wexps", [128, 8, 4096], BF16))
        wexpT = s1.enter_context(nc.sbuf_tensor("wexpT", [128, 8, 128], BF16))
        bexpc = s1.enter_context(nc.sbuf_tensor("bexpc", [128, 32], BF16))
        bprim = s1.enter_context(nc.sbuf_tensor("bprim", [128, 32], F32))
        for kc in range(8):
            ch_w.dma(nc.gpsimd.dma_start(out=wexps[:, kc, :], in_=wexp[kc*128:(kc+1)*128, :]))
        ch_w.dma(nc.gpsimd.dma_start(out=bexpc[:], in_=bexp.rearrange("(c p) -> p c", p=128),
                                     allow_slow_non_contiguous=True))
        ch_w.wait(nc.tensor); ch_w.wait(nc.vector)
        nc.all_engine_barrier()
        fold_ev = [0, 0]
        for c in range(32):
            for kc in range(8):
                sl = (c*8 + kc) % 2
                psT = ps_t0 if sl == 0 else ps_t1
                psC = ps_c0 if sl == 0 else ps_c1
                nc.tensor.wait_ge(ch_ev.sem, fold_ev[sl])
                ch_mm.bump(nc.tensor.transpose(psT[:], wexps[:, kc, c*128:(c+1)*128], idn[:]))
                nc.vector.wait_ge(ch_mm.sem, ch_mm.n)
                ch_ev.bump(nc.vector.tensor_copy(wexpT[:, kc, :], psT[:]))
                nc.tensor.wait_ge(ch_ev.sem, ch_ev.n)
                ch_mm.bump(nc.tensor.matmul(psC[:], wexpT[:, kc, :], Lmat[:, 0, c, :], start=True, stop=True))
                nc.vector.wait_ge(ch_mm.sem, ch_mm.n)
                ch_ev.bump(nc.vector.tensor_copy(wexps[:, kc, c*128:(c+1)*128], psC[:]))
                fold_ev[sl] = ch_ev.n
            nc.tensor.wait_ge(ch_ev.sem, ch_ev.n)
            ch_mm.bump(nc.tensor.matmul(ps_c0[:, 0:1], Lmat[:, 0, c, :], bexpc[:, c:c+1], start=True, stop=True))
            nc.vector.wait_ge(ch_mm.sem, ch_mm.n)
            ch_ev.bump(nc.vector.tensor_copy(bprim[:, c:c+1], ps_c0[:, 0:1]))
            nc.tensor.wait_ge(ch_ev.sem, ch_ev.n)
        nc.all_engine_barrier()
        def emit_tp(srcs, dsts):
            for src, dst in zip(srcs, dsts):
                sl = tstate[0] % 2
                tp_c[sl].wait(nc.tensor)
                tp_p[sl].bump(nc.tensor.transpose(ps_tp[sl][:], src, idn[:]))
                tp_p[sl].wait(nc.vector)
                tp_c[sl].bump(nc.vector.tensor_copy(dst, ps_tp[sl][:]))
                tstate[0] += 1
        NT = NTOK // 512
        for t in range(NT):
            for st4 in range(4):
                gtok = t*512 + st4*128
                b, po = gtok // NPOS, gtok % NPOS
                ch_ln.wait(nc.sync)
                ch_x.dma(nc.sync.dma_start(out=xt[:], in_=x3[b, po:po+128, :]))
                ch_x.wait(nc.vector); ch_x.wait(nc.scalar)
                emit_ln(nc, xt[:], xbf[:], stat[:], cst[:], ch_ln)
                ch_ln.wait(nc.tensor)
                emit_tp([xbf[:, kc*128:(kc+1)*128] for kc in range(8)],
                        [xfm[:, kc, st4*128:(st4+1)*128] for kc in range(8)])
            tp_c[0].wait(nc.tensor); tp_c[1].wait(nc.tensor)
            ch_pm.wait(nc.tensor)
            evn = [0, 0]
            for c in range(32):
                sl = c % 2
                psA = ps_a if sl == 0 else ps_b
                nc.tensor.wait_ge(ch_ev.sem, evn[sl])
                for kc in range(8):
                    inst = nc.tensor.matmul(psA[:], wexps[:, kc, c*128:(c+1)*128], xfm[:, kc, :],
                                     start=(kc == 0), stop=(kc == 7))
                ch_mm.bump(inst)
                nc.vector.wait_ge(ch_mm.sem, ch_mm.n)
                ch_ev.bump(nc.vector.tensor_scalar(h1[:, c, :], psA[:], bprim[:, c:c+1], None, op0=ALU.add))
                evn[sl] = ch_ev.n
            ch_ev.wait(nc.gpsimd); ch_ev.wait(nc.sync)
            for g in range(32):
                eng = nc.gpsimd if g % 2 == 0 else nc.sync
                ch_pm.dma(eng.dma_start(out=hp[:, g, :], in_=h1[g*4:(g+1)*4, :, :]))
            ch_pm.wait(nc.tensor)
            evb = [0, 0]
            for g in range(32):
                sl = g % 2
                psA = ps_b if sl == 0 else ps_a
                nc.tensor.wait_ge(ch_ev.sem, evb[sl])
                ch_mm.bump(nc.tensor.matmul(psA[:], Lmat[:, 1, g, :], hp[:, g, :], start=True, stop=True))
                nc.scalar.wait_ge(ch_mm.sem, ch_mm.n)
                ch_ev.bump(nc.scalar.activation(h1[:, g, :], psA[:], AF.Gelu, bias=upbs[:, g:g+1]))
                evb[sl] = ch_ev.n
            ch_ev.wait(nc.gpsimd); ch_ev.wait(nc.sync)
            for g in range(32):
                eng = nc.gpsimd if g % 2 == 0 else nc.sync
                ch_pm.dma(eng.dma_start(out=hp[g*4:(g+1)*4, :, :], in_=h1[:, g, :]))
            ch_pm.wait(nc.tensor)
            evb = [0, 0]
            for c in range(32):
                sl = c % 2
                psA = ps_b if sl == 0 else ps_a
                nc.tensor.wait_ge(ch_ev.sem, evb[sl])
                ch_mm.bump(nc.tensor.matmul(psA[:], Lmat[:, 2, c, :], hp[:, c, :], start=True, stop=True))
                nc.vector.wait_ge(ch_mm.sem, ch_mm.n)
                ch_ev.bump(nc.vector.tensor_copy(h1[:, c, :], psA[:]))
                evb[sl] = ch_ev.n
            ch_ev.wait(nc.gpsimd); ch_ev.wait(nc.sync)
            for g in range(32):
                eng = nc.gpsimd if g % 2 == 0 else nc.sync
                ch_pm.dma(eng.dma_start(out=hp[:, g, :], in_=h1[g*4:(g+1)*4, :, :]))
            ch_pm.wait(nc.tensor)
            evb = [0, 0]
            for g in range(32):
                sl = g % 2
                psA = ps_b if sl == 0 else ps_a
                nc.tensor.wait_ge(ch_ev.sem, evb[sl])
                ch_mm.bump(nc.tensor.matmul(psA[:], Lmat[:, 3, g, :], hp[:, g, :], start=True, stop=True))
                nc.vector.wait_ge(ch_mm.sem, ch_mm.n)
                ch_ev.bump(nc.vector.tensor_copy(h1[:, g, :], psA[:]))
                evb[sl] = ch_ev.n
            ch_ev.wait(nc.sync)
            ch_hf.dma(nc.sync.dma_start(out=hfin[:, :, t*512:(t+1)*512], in_=h1[:]))
            ch_hf.wait(nc.sync)
        nc.all_engine_barrier()
    with contextlib.ExitStack() as s2:
        wcons = s2.enter_context(nc.sbuf_tensor("wcons", [128, 32, 1024], BF16))
        bcons = s2.enter_context(nc.sbuf_tensor("bcons", [1, 1024], BF16))
        dbs   = s2.enter_context(nc.sbuf_tensor("dbs", [1, 1024], BF16))
        x4t   = s2.enter_context(nc.sbuf_tensor("x4t", [128, 1024], F32))
        for g in range(32):
            ch_w.dma(nc.gpsimd.dma_start(out=wcons[:, g, :], in_=wcon[g*128:(g+1)*128, :]))
        ch_w.dma(nc.gpsimd.dma_start(out=bcons[:], in_=bcon[None, :]))
        ch_w.dma(nc.gpsimd.dma_start(out=dbs[:], in_=dbias[None, :]))
        ch_w.wait(nc.tensor); ch_w.wait(nc.vector); ch_w.wait(nc.sync)
        nc.all_engine_barrier()
        NT = NTOK // 512
        for t in range(NT):
            ch_mm.wait(nc.sync)
            ch_x.dma(nc.sync.dma_start(out=h1[:], in_=hfin[:, :, t*512:(t+1)*512]))
            for st4 in range(4):
                gtok = t*512 + st4*128
                b, po = gtok // NPOS, gtok % NPOS
                ch_od.wait(nc.sync)
                ch_x.dma(nc.sync.dma_start(out=xt[:], in_=x3[b, po:po+128, :]))
                ch_x.wait(nc.tensor)
                ch_od.wait(nc.tensor)
                for nn, ps in ((0, ps_a), (1, ps_b)):
                    for g in range(32):
                        nc.tensor.matmul(ps[:], h1[:, g, st4*128:(st4+1)*128], wcons[:, g, nn*512:(nn+1)*512],
                                         start=(g == 0), stop=False)
                    nc.tensor.matmul(ps[:], ones5[:, 0:128], bcons[:, nn*512:(nn+1)*512], start=False, stop=False)
                    ch_mm.bump(nc.tensor.matmul(ps[:], ones5[:, 0:128], dbs[:, nn*512:(nn+1)*512], start=False, stop=True))
                ch_mm.wait(nc.vector)
                nc.vector.tensor_add(x4t[:, 0:512], ps_a[:], xt[:, 0:512])
                ch_od.bump(nc.vector.tensor_add(x4t[:, 512:1024], ps_b[:], xt[:, 512:1024]))
                ch_od.wait(nc.sync)
                ch_od.dma(nc.sync.dma_start(out=x4[b, po:po+128, :], in_=x4t[:]))
        ch_od.wait(nc.sync)
        nc.all_engine_barrier()
    return nc



def _run(build, in_maps):
    nc = bass.Bass("TRN2", target_bir_lowering=False)
    with contextlib.ExitStack() as stack:
        build(nc, stack)
    return run_bass_kernel_spmd(nc, in_maps, core_ids=list(range(8))).results

def kernel(**inp):
    f32 = np.float32
    asf = lambda a: np.ascontiguousarray(np.asarray(a, dtype=f32))
    x0 = asf(inp["L0_attn_x"])
    idn, masks = host_constants()
    in_maps = [{"x": np.ascontiguousarray(x0[:, c*256:(c+1)*256, :]),
                "wqkv": asf(inp["L0_attn_Wqkv"]), "bqkv": asf(inp["L0_attn_bqkv"]),
                "idn": idn, "mask": masks} for c in range(8)]
    r1 = _run(build_l0attn, in_maps)
    Y = [r["y"] for r in r1]; DEN = [r["den"] for r in r1]
    sel = host_sel()
    in_maps = []
    for c in range(8):
        yg = np.ascontiguousarray(np.stack(
            [Y[src][hl, :, :, c*256:(c+1)*256] for src in range(8) for hl in range(2)]))
        dg = np.ascontiguousarray(np.stack(
            [DEN[src][hl, :, c*256:(c+1)*256] for src in range(8) for hl in range(2)]))
        in_maps.append({"yg": yg, "den": dg, "x": np.ascontiguousarray(x0[:, c*256:(c+1)*256, :]),
                        "wp": asf(inp["L0_attn_Wproj"]), "bp": asf(inp["L0_attn_bproj"]), "sel": sel})
    r2 = _run(build_l2, in_maps)
    X1 = [r["x1"] for r in r2]
    w3 = np.ascontiguousarray(np.concatenate(
        [asf(inp["L0_ffn_wf"]).T, asf(inp["L0_ffn_wi"]).T, asf(inp["L0_ffn_wc"]).T], axis=1))
    u3 = np.ascontiguousarray(np.concatenate(
        [asf(inp["L0_ffn_uf"]).T, asf(inp["L0_ffn_ui"]).T, asf(inp["L0_ffn_uc"]).T], axis=1))
    b3 = np.concatenate([asf(inp["L0_ffn_bf"]), asf(inp["L0_ffn_bi"]), asf(inp["L0_ffn_bc"])])
    in_maps = [{"x1": X1[c], "w3": w3, "u3": u3, "b3": b3,
                "wo": np.ascontiguousarray(asf(inp["L0_ffn_wo"]).T),
                "uo": np.ascontiguousarray(asf(inp["L0_ffn_uo"]).T),
                "bo": asf(inp["L0_ffn_bo"]), "inc": asf(inp["L0_ffn_in_c"]), "idn": idn}
               for c in range(8)]
    r3 = _run(build_l3, in_maps)
    X2 = [r["x2"] for r in r3]
    wqr, bqr = host_l4_weights(asf(inp["L1_attn_W_qkv"]), asf(inp["L1_attn_b_qkv"]))
    in_maps = [{"x2": X2[c], "wq": wqr, "bq": bqr,
                "wo": asf(inp["L1_attn_W_out"]), "bo": asf(inp["L1_attn_b_out"]), "idn": idn}
               for c in range(8)]
    r4 = _run(build_l4, in_maps)
    X3 = [r["x3"] for r in r4]
    perm = l5_perm_idx()
    wband_all = np.zeros((2, 12, 3, 128, 32), np.float32)
    for bf, wts in ((0, asf(inp["L1_ffn_up_weights"])), (1, asf(inp["L1_ffn_down_weights"]))):
        wb = l5_band_vectors(wts)
        wband_all[bf] = wb.reshape(12, 3, 32, 128).transpose(0, 1, 3, 2)
    upb_in = asf(inp["L1_ffn_up_bias"])[perm].reshape(32, 128).T.copy()
    wcon_perm = np.ascontiguousarray(asf(inp["L1_ffn_W_con"])[perm, :])
    in_maps = [{"x3": X3[c], "wexp": asf(inp["L1_ffn_W_exp"]), "bexp": asf(inp["L1_ffn_b_exp"]),
                "wband": wband_all, "upb": upb_in, "wcon": wcon_perm,
                "bcon": asf(inp["L1_ffn_b_con"]), "dbias": asf(inp["L1_ffn_down_bias"]), "idn": idn}
               for c in range(8)]
    r5 = _run(build_l5, in_maps)
    out = np.concatenate([r5[c]["x4"] for c in range(8)], axis=1)
    return out.astype(np.float32)
